# version 02
# baseline (speedup 1.0000x reference)
import os
import numpy as np
import ml_dtypes
import concourse.bass as bass
import concourse.mybir as mybir
from concourse.bass_utils import run_bass_kernel_spmd
from concourse.bass_core_rust import IndirectOffsetOnAxis

F32 = mybir.dt.float32
BF16 = mybir.dt.bfloat16
I32 = mybir.dt.int32
AF = mybir.ActivationFunctionType
ALU = mybir.AluOpType
AX = mybir.AxisListType

NCORES = 8
T = 2048
NTT = 16
H = 4096
NCB = 67
NCH = 16
EPS = 1e-5
TP = T + 4

DEBUG_TAPS = bool(int(os.environ.get("MK_DEBUG_TAPS", "0")))
STAGES = int(os.environ.get("MK_STAGES", "99"))
SCAN_DEBUG = bool(int(os.environ.get("MK_SCAN_DEBUG", "0")))
NO_Y2 = os.environ.get("MK_NO_Y2") == "1"
NO_Y1 = os.environ.get("MK_NO_Y1") == "1"


class KB:
    ENGS = ("sync", "tensor", "vector", "scalar", "gpsimd")

    def __init__(self):
        self.nc = bass.Bass("TRN2", target_bir_lowering=False, debug=False)
        self.q = {e: [] for e in self.ENGS}
        self.semc = {}
        self.sem_order = []
        self._cursor = 16512
        self.SBLIM = 229344

    def sem(self, *names):
        for n in names:
            self.semc[n] = 0
            self.sem_order.append(n)

    def val(self, s):
        return self.semc[s]

    def emit(self, eng, fn, waits=(), inc=None, inc_n=1):
        post = None
        if inc is not None:
            self.semc[inc] += inc_n
            post = self.semc[inc]
        self.q[eng].append((tuple((s, v) for s, v in waits if v > 0), fn, inc, inc_n))
        return post

    def dma(self, eng, out, in_, waits=(), inc=None, **kw):
        return self.emit(eng, (lambda e, o=out, i=in_, kk=dict(kw): e.dma_start(out=o, in_=i, **kk)),
                         waits=waits, inc=inc, inc_n=16)

    def wait(self, eng, waits):
        self.emit(eng, None, waits=waits)

    def barrier(self):
        vals = [(s, self.semc[s]) for s in self.sem_order]
        for e in self.ENGS:
            self.wait(e, vals)

    def sb(self, name, shape, dtype, at=None):
        sz = int(np.prod(shape[1:])) * np.dtype(mybir.dt.np(dtype)).itemsize
        off = self._cursor if at is None else at
        off = (off + 63) & ~63
        end = off + sz
        assert end <= self.SBLIM, f"SBUF overflow at {name}: {off}+{sz}"
        if at is None:
            self._cursor = end
        return self.nc.alloc_sbuf_tensor_at(name, list(shape), dtype, offset=off)

    def mark(self):
        return self._cursor

    def reset(self, c):
        self._cursor = c

    def finalize(self):
        nc = self.nc
        from contextlib import ExitStack
        with ExitStack() as ctx:
            handles = {n: ctx.enter_context(nc.semaphore(n)) for n in self.sem_order}
            block = ctx.enter_context(nc.Block())
            for eng in self.ENGS:
                qeng = self.q[eng]

                def body(e, qeng=qeng):
                    for waits, fn, inc, inc_n in qeng:
                        for s, v in waits:
                            e.wait_ge(handles[s], v)
                        if fn is not None:
                            ins = fn(e)
                            if inc is not None:
                                ins.then_inc(handles[inc], inc_n)
                getattr(block, eng)(body)
        return nc


def build_nc():
    k = KB()
    nc = k.nc

    def din(name, shape, dt=F32):
        return nc.dram_tensor(name, list(shape), dt, kind="ExternalInput").ap()

    xin = din("xin", [T, H])
    xhalo = din("xhalo", [4, H])
    w_ip = din("w_ip", [NCB, 128, 32, 128])
    w_op = din("w_op", [8, 128, 32, 512])
    w_up = din("w_up", [8, 128, 32, 512])
    gate_t = din("gate_t", [128, 32, 4])
    loraA_t = din("loraA_t", [128, 32, 32])
    loraB_t = din("loraB_t", [32, 4096])
    conv_t = din("conv_t", [34, 128, 4])
    mup_t = din("mup_t", [128, 67])
    lnw_t = din("lnw_t", [128, 32])
    nrmw_t = din("nrmw_t", [128, 32])
    dtb_i = din("dtb", [64, 1])
    alog_i = din("alog", [64, 1])
    drep_i = din("drep", [1, 4096])
    ebias_i = din("ebias", [1, 4])
    identb_i = din("identb", [128, 128], BF16)
    identf_i = din("identf", [128, 128])
    maskbin_i = din("maskbin", [128, 128])
    hidx_i = din("hidx", [128, 1], I32)

    out_d = nc.dram_tensor("out", [T, H], F32, kind="ExternalOutput").ap()

    def dint(name, shape, dt=F32):
        return nc.dram_tensor(name, list(shape), dt, kind="Internal").ap()

    xs_dram = dint("xs_dram", [NTT, 128, H], BF16)
    sz_dram = dint("sz_dram", [NTT, 128, H], BF16)
    y_dram = dint("y_dram", [NTT, 128, H])
    at_dram = dint("at_dram", [8, NTT, 128, 512])
    xm_dram = dint("xm_dram", [NTT, 128, H])
    ff_dram = dint("ff_dram", [8, NTT, 128, 512])
    lam_dram = dint("lam_dram", [NCH, 64, 128])
    dbg_sink = dint("dbg_sink", [128, 512])
    cc_in = dint("cc_in", [128, H], BF16)
    cc_out = dint("cc_out", [128 * NCORES, H], BF16)

    taps = []
    scan_taps = {}
    if SCAN_DEBUG:
        for nm, shp, dt_ in [("snap_xs", [128, H], BF16), ("snap_y", [128, H], F32),
                             ("snap_P", [128, 4096], BF16), ("snap_Xdd", [128, 2048], BF16),
                             ("snap_ssc", [128, 2048], BF16), ("snap_W", [128, 4096], F32),
                             ("snap_lr0", [128, 4096], F32), ("snap_lr1", [128, 4096], F32),
                             ("snap_D", [128, 4096], F32), ("snap_lr1b", [128, 4096], F32),
                             ("snap_lsd", [128, 1024], F32), ("snap_edq", [128, 1024], F32),
                             ("snap_edt", [128, 1024], F32)]:
            scan_taps[nm] = nc.dram_tensor(nm, shp, dt_, kind="ExternalOutput").ap()
    if DEBUG_TAPS:
        for nm, src in [("tap_xs", xs_dram), ("tap_sz", sz_dram), ("tap_y", y_dram),
                        ("tap_at", at_dram), ("tap_xm", xm_dram), ("tap_ff", ff_dram),
                        ("tap_lam", lam_dram)]:
            tp = nc.dram_tensor(nm, list(src.tensor.shape), src.dtype, kind="ExternalOutput").ap()
            taps.append((tp, src))

    k.sem("d_xin", "d_w", "d_xs", "d_lrep", "d_y", "d_sz", "d_at", "d_xm", "d_ff",
          "d_cin", "d_scrw", "d_out", "d_cst", "d_cw", "ccs", "c_ydbg",
          "c_pe", "c_dve", "c_act", "c_pool")


    def finish_early():
        od = k.dma("gpsimd", out_d, xin, waits=[(s_, k.val(s_)) for s_ in k.sem_order
                                                if s_ not in ("d_out",)], inc="d_out")
        if DEBUG_TAPS:
            for tp, src in taps:
                k.dma("gpsimd", tp, src[:], inc="d_scrw")
        k.wait("gpsimd", [("d_out", od), ("d_scrw", k.val("d_scrw"))])
        k.wait("sync", [("d_out", od)])
        return k.finalize()
    ident_bf = k.sb("ident_bf", [128, 128], BF16)
    ident_f = k.sb("ident_f", [128, 128], F32)
    maskbin = k.sb("maskbin", [128, 128], F32)
    mup_s = k.sb("mup", [128, 68], F32)
    lnw_s = k.sb("lnw", [128, 32], F32)
    nrmw_s = k.sb("nrmw", [128, 32], F32)
    gate_bf = k.sb("gate_bf", [128, 32 * 4], BF16)
    loraA_bf = k.sb("loraA_bf", [128, 32 * 32], BF16)
    dtb_s = k.sb("dtb", [64, 1], F32)
    alog_s = k.sb("alog", [64, 1], F32)
    expA_s = k.sb("expA", [64, 1], F32)
    ebias_r = k.sb("ebias_r", [128, 4], F32)
    hidx_s = k.sb("hidx", [128, 1], I32)
    onesf = k.sb("onesf", [1, 128], F32)
    stat = k.sb("stat", [128, 16], F32)
    ones64 = k.sb("ones64", [64, 1], F32)
    zero64 = k.sb("zero64", [64, 1], F32)
    rowtmp = k.sb("rowtmp", [1, 64], F32)
    B_bf = k.sb("B_bf", [128, T], BF16)
    C_bf = k.sb("C_bf", [128, T], BF16)
    lsd_tmA = k.sb("lsd_tmA", [128, NCH * 64], F32)
    edec_tmA = k.sb("edec_tmA", [128, NCH * 64], F32)
    edecQA = k.sb("edecQA", [128, NCH * 64], F32)
    lam_tmA = k.sb("lam_tmA", [128, NCH * 64], F32)
    SBASE = k.mark()

    x2_fm = k.sb("x2_fm", [128, 32 * TP], BF16, at=SBASE)
    XEND = ((SBASE + 63) & ~63) + 32 * TP * 2

    psb = [nc.alloc_psum_tensor(f"psb{i}", [128, 512], F32) for i in range(8)]

    def pbf(i):
        return psb[i].ap().bitcast(BF16)

    for dst, src in [(ident_bf, identb_i), (ident_f, identf_i), (maskbin, maskbin_i),
                     (lnw_s, lnw_t), (nrmw_s, nrmw_t),
                     (dtb_s, dtb_i), (alog_s, alog_i), (hidx_s, hidx_i)]:
        k.dma("scalar", dst[:], src, inc="d_cst")
    k.dma("scalar", mup_s[:, 0:67], mup_t, inc="d_cst")
    k.dma("scalar", ebias_r[:], ebias_i.partition_broadcast(128), inc="d_cst")
    k.dma("gpsimd", gate_bf[:], gate_t.rearrange("p b e -> p (b e)"), inc="d_w")
    k.dma("gpsimd", loraA_bf[:], loraA_t.rearrange("p b e -> p (b e)"), inc="d_w")
    CST = k.val("d_cst")
    k.wait("vector", [("d_cst", CST)])
    k.emit("vector", lambda e: e.memset(onesf[:], 1.0), inc="c_dve")
    k.emit("vector", lambda e: e.memset(ones64[:], 1.0), inc="c_dve")
    k.emit("vector", lambda e: e.memset(zero64[:], 0.0), inc="c_dve")
    k.emit("scalar", lambda e: e.activation(out=expA_s[:], in_=alog_s[:], func=AF.Exp),
           waits=[("d_cst", CST)], inc="c_act")

    def ln_stats(xt, scrap, double, np_=128, extra_waits=()):
        ws = list(extra_waits)
        t_sum = k.emit("vector", lambda e: e.reduce_sum(out=stat[0:np_, 0:1], in_=xt, axis=AX.X),
                       waits=ws, inc="c_dve")
        t_ssq = k.emit("scalar", lambda e: e.activation(out=scrap, in_=xt, func=AF.Square,
                                                        accum_out=stat[0:np_, 1:2]),
                       waits=ws, inc="c_act")
        k.emit("vector", lambda e: e.tensor_scalar(out=stat[0:np_, 2:3], in0=stat[0:np_, 0:1],
                                                   scalar1=1.0 / H, scalar2=None, op0=ALU.mult),
               waits=[("c_dve", t_sum)], inc="c_dve")
        t1 = k.emit("vector", lambda e: e.tensor_scalar(out=stat[0:np_, 3:4], in0=stat[0:np_, 2:3],
                                                        scalar1=stat[0:np_, 2:3], scalar2=-1.0,
                                                        op0=ALU.mult, op1=ALU.mult), inc="c_dve")
        t2 = k.emit("vector", lambda e: e.scalar_tensor_tensor(out=stat[0:np_, 3:4], in0=stat[0:np_, 1:2],
                                                               scalar=1.0 / H, in1=stat[0:np_, 3:4],
                                                               op0=ALU.mult, op1=ALU.add),
                    waits=[("c_dve", t1), ("c_act", t_ssq)], inc="c_dve")
        a1 = k.emit("scalar", lambda e: e.activation(out=stat[0:np_, 4:5], in_=stat[0:np_, 3:4],
                                                     func=AF.Ln, bias=np.float32(EPS)),
                    waits=[("c_dve", t2)], inc="c_act")
        a2 = k.emit("scalar", lambda e: e.activation(out=stat[0:np_, 5:6], in_=stat[0:np_, 4:5],
                                                     func=AF.Exp, scale=np.float32(-0.5)),
                    waits=[("c_act", a1)], inc="c_act")
        if double:
            a3 = k.emit("scalar", lambda e: e.activation(out=stat[0:np_, 6:7], in_=stat[0:np_, 4:5],
                                                         func=AF.Exp, scale=np.float32(-1.0)),
                        waits=[("c_act", a2)], inc="c_act")
            v3 = k.emit("vector", lambda e: e.tensor_tensor(out=stat[0:np_, 7:8], in0=stat[0:np_, 3:4],
                                                            in1=stat[0:np_, 6:7], op=ALU.mult),
                        waits=[("c_act", a3)], inc="c_dve")
            a4 = k.emit("scalar", lambda e: e.activation(out=stat[0:np_, 8:9], in_=stat[0:np_, 7:8],
                                                         func=AF.Ln, bias=np.float32(EPS)),
                        waits=[("c_dve", v3)], inc="c_act")
            a5 = k.emit("scalar", lambda e: e.activation(out=stat[0:np_, 9:10], in_=stat[0:np_, 8:9],
                                                         func=AF.Exp, scale=np.float32(-0.5)),
                        waits=[("c_act", a4)], inc="c_act")
            v4 = k.emit("vector", lambda e: e.tensor_tensor(out=stat[0:np_, 10:11], in0=stat[0:np_, 5:6],
                                                            in1=stat[0:np_, 9:10], op=ALU.mult),
                        waits=[("c_act", a5)], inc="c_dve")
        else:
            v4 = k.emit("vector", lambda e: e.tensor_copy(out=stat[0:np_, 10:11], in_=stat[0:np_, 5:6]),
                        waits=[("c_act", a2)], inc="c_dve")
        v5 = k.emit("vector", lambda e: e.tensor_scalar(out=stat[0:np_, 11:12], in0=stat[0:np_, 2:3],
                                                        scalar1=stat[0:np_, 10:11], scalar2=-1.0,
                                                        op0=ALU.mult, op1=ALU.mult),
                    waits=[("c_dve", v4)], inc="c_dve")
        return v5

    k.reset(XEND)
    xin_t = [k.sb(f"xin{i}", [128, H], F32) for i in range(2)]
    x2tm = k.sb("x2tm", [128, H], BF16)
    assert k.mark() <= k.SBLIM

    apply_toks = []
    last_tr = 0
    for i in range(NTT + 1):
        halo = (i == NTT)
        np_ = 4 if halo else 128
        if halo:
            buf = xin_t[0][0:4, :]
            w = [("c_act", apply_toks[-2])]
            k.dma("sync", xin_t[0][0:4, :], xhalo, waits=w, inc="d_xin")
        else:
            buf = xin_t[i % 2][:]
            w = [("c_act", apply_toks[i - 2])] if i >= 2 else []
            k.dma("sync", xin_t[i % 2][:],
                  xin.rearrange("(n p) h -> n p h", p=128)[i], waits=w, inc="d_xin")
        ldv = k.val("d_xin")
        prev_apply = apply_toks[-1] if apply_toks else 0
        stok = ln_stats(buf, x2tm[0:np_, :], True, np_=np_,
                        extra_waits=[("d_xin", ldv), ("c_act", prev_apply),
                                     ("c_pe", last_tr), ("d_cst", CST)])
        ap_tok = k.emit("scalar", lambda e, b_=buf, np_=np_: e.activation(
            out=x2tm[0:np_, :], in_=b_, func=AF.Identity,
            scale=stat[0:np_, 10:11], bias=stat[0:np_, 11:12]),
            waits=[("c_dve", stok)], inc="c_act")
        apply_toks.append(ap_tok)
        for grp in range(4):
            bank = 4 + (grp % 2)
            trt = None
            for kk in range(8):
                hb = grp * 8 + kk
                if halo:
                    fn = (lambda e, hb=hb, bank=bank, kk=kk: e.transpose(
                        out=pbf(bank)[:, kk * 128:kk * 128 + 3],
                        in_=x2tm[0:3, hb * 128:(hb + 1) * 128], identity=ident_bf[0:3, 0:3]))
                else:
                    fn = (lambda e, hb=hb, bank=bank, kk=kk: e.transpose(
                        out=pbf(bank)[:, kk * 128:(kk + 1) * 128],
                        in_=x2tm[:, hb * 128:(hb + 1) * 128], identity=ident_bf[:]))
                trt = k.emit("tensor", fn,
                             waits=([("c_act", ap_tok), ("c_dve", k.val("c_dve")),
                                     ("d_cst", CST)] if kk == 0 else ()),
                             inc="c_pe" if kk == 7 else None)
            last_tr = trt
            for kk in range(8):
                hb = grp * 8 + kk
                if halo:
                    fn = (lambda e, hb=hb, bank=bank, kk=kk: e.tensor_scalar(
                        out=x2_fm[:, hb * TP + T: hb * TP + T + 3],
                        in0=pbf(bank)[:, kk * 128:kk * 128 + 3],
                        scalar1=lnw_s[:, hb:hb + 1], scalar2=None, op0=ALU.mult))
                else:
                    fn = (lambda e, hb=hb, bank=bank, kk=kk, i=i: e.tensor_scalar(
                        out=x2_fm[:, hb * TP + i * 128: hb * TP + (i + 1) * 128],
                        in0=pbf(bank)[:, kk * 128:(kk + 1) * 128],
                        scalar1=lnw_s[:, hb:hb + 1], scalar2=None, op0=ALU.mult))
                k.emit("vector", fn, waits=[("c_pe", trt), ("d_cst", CST)] if kk == 0 else (),
                       inc="c_dve" if kk == 7 else None)
    k.barrier()
    if STAGES <= 1:
        return finish_early()

    k.reset(XEND)
    wip_bf = [k.sb(f"wip{i}", [128, 4096], BF16) for i in range(2)]
    xbc_b = k.sb("xbc", [128, 2052], F32)
    convo = k.sb("convo", [128, T], F32)
    sil_bf = k.sb("sil_bf", [128, T], BF16)
    stg = k.sb("stg", [128, NTT * 128], BF16)
    cw_s = k.sb("cw", [128, 4], F32)
    dtmp = k.sb("dtmp", [64, T], F32)
    assert k.mark() <= k.SBLIM

    def emit_tm_scratch(sil_tok, dram, c0):
        ev_tok = None
        for g in range(2):
            bank = 6 + g
            trt = None
            for kk in range(8):
                tt = g * 8 + kk
                trt = k.emit("tensor", lambda e, tt=tt, bank=bank, kk=kk: e.transpose(
                    out=pbf(bank)[:, kk * 128:(kk + 1) * 128],
                    in_=sil_bf[:, tt * 128:(tt + 1) * 128], identity=ident_bf[:]),
                    waits=([("c_act", sil_tok), ("c_dve", k.val("c_dve"))] if kk == 0 else ()),
                    inc="c_pe" if kk == 7 else None)
            ev_tok = k.emit("vector", lambda e, g=g, bank=bank: e.tensor_copy(
                out=stg[:, g * 1024:(g + 1) * 1024], in_=pbf(bank)[:, 0:1024]),
                waits=[("c_pe", trt), ("d_scrw", k.val("d_scrw"))], inc="c_dve")
        return k.dma("scalar", dram[:, :, c0:c0 + 128].transpose([1, 0, 2]),
                     stg[:].rearrange("p (n c) -> p n c", n=NTT),
                     waits=[("c_dve", ev_tok)], inc="d_scrw")

    def dt_process(ev_tok):
        dr = convo[0:64, 0:T]
        a1 = k.emit("scalar", lambda e: e.activation(out=dtmp[:], in_=dr, func=AF.Abs, bias=dtb_s[:]),
                    waits=[("c_act", ev_tok)], inc="c_act")
        a2 = k.emit("scalar", lambda e: e.activation(out=dtmp[:], in_=dtmp[:], func=AF.Exp, scale=np.float32(-1.0)),
                    waits=[("c_act", a1)], inc="c_act")
        a3 = k.emit("scalar", lambda e: e.activation(out=dtmp[:], in_=dtmp[:], func=AF.Ln, bias=np.float32(1.0)),
                    waits=[("c_act", a2)], inc="c_act")
        a4 = k.emit("scalar", lambda e: e.activation(out=xbc_b[0:64, 0:T], in_=dr, func=AF.Relu,
                                                     bias=dtb_s[:]),
                    waits=[("c_act", a3)], inc="c_act")
        v1 = k.emit("vector", lambda e: e.tensor_tensor(out=dr, in0=dtmp[:], in1=xbc_b[0:64, 0:T],
                                                        op=ALU.add),
                    waits=[("c_act", a4)], inc="c_dve")
        a5 = k.emit("scalar", lambda e: e.activation(out=dtmp[:], in_=dr, func=AF.Ln),
                    waits=[("c_dve", v1)], inc="c_act")
        v2 = k.emit("vector", lambda e: e.tensor_scalar(out=xbc_b[0:64, 0:T], in0=dr,
                                                        scalar1=expA_s[:], scalar2=-1.0,
                                                        op0=ALU.mult, op1=ALU.mult),
                    waits=[("c_act", a5)], inc="c_dve")
        v3 = k.emit("vector", lambda e: e.tensor_tensor_scan(
            out=dr, data0=ones64[:].broadcast_to([64, T]), data1=xbc_b[0:64, 0:T],
            initial=0.0, op0=ALU.mult, op1=ALU.add),
            waits=[("c_dve", v2)], inc="c_dve")
        lamwr = k.dma("scalar", lam_dram.rearrange("c h s -> h c s"),
                      dr.rearrange("h (c s) -> h c s", c=NCH),
                      waits=[("c_dve", v3)], inc="d_scrw")
        v4 = k.emit("vector", lambda e: e.tensor_tensor(out=dtmp[:], in0=dr, in1=dtmp[:],
                                                        op=ALU.subtract),
                    waits=[("c_dve", v3)], inc="c_dve")
        a6 = k.emit("scalar", lambda e: e.activation(out=xbc_b[0:64, 0:T], in_=xbc_b[0:64, 0:T],
                                                     func=AF.Exp),
                    waits=[("c_dve", v4)], inc="c_act")
        for j in range(NCH):
            t1 = k.emit("tensor", lambda e, j=j: e.transpose(
                out=psb[7][:, 0:64], in_=dr[:, j * 128:(j + 1) * 128],
                identity=ident_f[0:64, 0:64]),
                waits=[("d_scrw", lamwr), ("c_dve", k.val("c_dve")), ("c_act", a6)], inc="c_pe")
            e1 = k.emit("vector", lambda e, j=j: e.tensor_copy(
                out=lam_tmA[:, j * 64:(j + 1) * 64], in_=psb[7][:, 0:64]),
                waits=[("c_pe", t1)], inc="c_dve")
            t2 = k.emit("tensor", lambda e, j=j: e.transpose(
                out=psb[7][:, 64:128], in_=dtmp[:, j * 128:(j + 1) * 128],
                identity=ident_f[0:64, 0:64]),
                waits=[("c_dve", e1)], inc="c_pe")
            e2 = k.emit("vector", lambda e, j=j: e.tensor_copy(
                out=lsd_tmA[:, j * 64:(j + 1) * 64], in_=psb[7][:, 64:128]),
                waits=[("c_pe", t2)], inc="c_dve")
            ed = xbc_b[0:64, 1924:2052]
            v5 = k.emit("vector", lambda e, j=j: e.tensor_tensor_scan(
                out=ed, data0=xbc_b[0:64, j * 128:(j + 1) * 128],
                data1=zero64[:].broadcast_to([64, 128]),
                initial=1.0, op0=ALU.mult, op1=ALU.add),
                waits=[("c_dve", e2)], inc="c_dve")
            t3 = k.emit("tensor", lambda e: e.transpose(
                out=psb[7][:, 128:192], in_=ed, identity=ident_f[0:64, 0:64]),
                waits=[("c_dve", v5)], inc="c_pe")
            e3 = k.emit("vector", lambda e, j=j: e.tensor_copy(
                out=edec_tmA[:, j * 64:(j + 1) * 64], in_=psb[7][:, 128:192]),
                waits=[("c_pe", t3)], inc="c_dve")
            t4 = k.emit("tensor", lambda e: e.transpose(
                out=psb[7][0:1, 192:256], in_=ed[:, 127:128], identity=ident_f[0:64, 0:64]),
                waits=[("c_dve", e3)], inc="c_pe")
            e4 = k.emit("vector", lambda e: e.tensor_copy(out=rowtmp[:], in_=psb[7][0:1, 192:256]),
                        waits=[("c_pe", t4)], inc="c_dve")
            t5 = k.emit("tensor", lambda e: e.matmul(out=psb[7][:, 256:320], lhsT=onesf[:],
                                                     rhs=rowtmp[:], start=True, stop=True),
                        waits=[("c_dve", e4)], inc="c_pe")
            k.emit("vector", lambda e, j=j: e.tensor_copy(
                out=edecQA[:, j * 64:(j + 1) * 64], in_=psb[7][:, 256:320]),
                waits=[("c_pe", t5)], inc="c_dve")
        return k.val("c_dve")

    cb_order = [66, 64, 65] + list(range(32, 64)) + list(range(0, 32))
    mm_done = {}
    bank_evac = {g: 0 for g in range(5)}
    conv_done = 0
    silu_done = 0

    for ki, cb in enumerate(cb_order):
        wb = wip_bf[ki % 2]
        wws = [("c_pe", mm_done[ki - 2])] if ki >= 2 else \
              [("c_act", k.val("c_act")), ("c_dve", k.val("c_dve")), ("c_pe", k.val("c_pe"))]
        wld = k.dma("gpsimd", wb[:], w_ip[cb].rearrange("p b c -> p (b c)"), waits=wws, inc="d_w")
        xbc = 32 <= cb <= 65
        if xbc:
            cwld = k.dma("sync", cw_s[:], conv_t[cb - 32], waits=[("c_dve", conv_done)], inc="d_cw")
        grp_tok = []
        for g in range(4):
            for hb in range(32):
                ws = []
                if g == 0 and hb == 0:
                    ws.append(("d_w", wld))
                if hb == 0:
                    ws.append(("c_act", bank_evac[g]))
                tok = k.emit("tensor", lambda e, g=g, hb=hb, wb=wb: e.matmul(
                    out=psb[g][:, 0:512],
                    lhsT=wb[:, hb * 128:(hb + 1) * 128],
                    rhs=x2_fm[:, hb * TP + g * 512: hb * TP + (g + 1) * 512],
                    start=(hb == 0), stop=(hb == 31)),
                    waits=ws, inc="c_pe" if hb == 31 else None)
            grp_tok.append(tok)
        halo_tok = None
        if xbc:
            for hb in range(32):
                ws = [("c_act", bank_evac[4])] if hb == 0 else ()
                halo_tok = k.emit("tensor", lambda e, hb=hb, wb=wb: e.matmul(
                    out=psb[4][:, 0:3],
                    lhsT=wb[:, hb * 128:(hb + 1) * 128],
                    rhs=x2_fm[:, hb * TP + T: hb * TP + T + 3],
                    start=(hb == 0), stop=(hb == 31)),
                    waits=ws, inc="c_pe" if hb == 31 else None)
        mm_done[ki] = halo_tok if halo_tok is not None else grp_tok[-1]

        if cb == 66:
            evs = []
            for g in range(4):
                evs.append(k.emit("scalar", lambda e, g=g: e.activation(
                    out=convo[0:64, g * 512:(g + 1) * 512], in_=psb[g][0:64, 0:512],
                    func=AF.Identity, scale=mup_s[0:64, 66:67]),
                    waits=[("c_pe", grp_tok[g])], inc="c_act"))
                bank_evac[g] = evs[-1]
            conv_done = dt_process(evs[-1])
        elif xbc:
            for g in range(4):
                ws = [("c_pe", grp_tok[g])]
                if g == 0:
                    ws += [("c_dve", conv_done)]
                bank_evac[g] = k.emit("scalar", lambda e, g=g, cb=cb: e.activation(
                    out=xbc_b[:, 3 + g * 512: 3 + (g + 1) * 512], in_=psb[g][:, 0:512],
                    func=AF.Identity, scale=mup_s[:, cb:cb + 1]),
                    waits=ws, inc="c_act")
            bank_evac[4] = k.emit("scalar", lambda e, cb=cb: e.activation(
                out=xbc_b[:, 0:3], in_=psb[4][:, 0:3],
                func=AF.Identity, scale=mup_s[:, cb:cb + 1]),
                waits=[("c_pe", halo_tok), ("c_dve", conv_done)], inc="c_act")
            cv = k.emit("vector", lambda e: e.tensor_scalar(
                out=convo[:, 0:T], in0=xbc_b[:, 0:T], scalar1=cw_s[:, 0:1],
                scalar2=None, op0=ALU.mult),
                waits=[("c_act", bank_evac[4]), ("c_act", bank_evac[3]), ("d_cw", cwld),
                       ("c_act", silu_done)], inc="c_dve")
            for kk in range(1, 4):
                cv = k.emit("vector", lambda e, kk=kk: e.scalar_tensor_tensor(
                    out=convo[:, 0:T], in0=xbc_b[:, kk:kk + T], scalar=cw_s[:, kk:kk + 1],
                    in1=convo[:, 0:T], op0=ALU.mult, op1=ALU.add),
                    waits=[("c_dve", cv)], inc="c_dve")
            conv_done = cv
            if cb == 64:
                silu_done = k.emit("scalar", lambda e: e.activation(
                    out=B_bf[:], in_=convo[:, 0:T], func=AF.Silu),
                    waits=[("c_dve", cv)], inc="c_act")
            elif cb == 65:
                silu_done = k.emit("scalar", lambda e: e.activation(
                    out=C_bf[:], in_=convo[:, 0:T], func=AF.Silu),
                    waits=[("c_dve", cv)], inc="c_act")
            else:
                silu_done = k.emit("scalar", lambda e: e.activation(
                    out=sil_bf[:], in_=convo[:, 0:T], func=AF.Silu),
                    waits=[("c_dve", cv), ("c_pe", k.val("c_pe"))], inc="c_act")
                emit_tm_scratch(silu_done, xs_dram, (cb - 32) * 128)
        else:
            for g in range(4):
                ws = [("c_pe", grp_tok[g])]
                if g == 0:
                    ws.append(("c_pe", k.val("c_pe")))
                bank_evac[g] = k.emit("scalar", lambda e, g=g, cb=cb: e.activation(
                    out=sil_bf[:, g * 512:(g + 1) * 512], in_=psb[g][:, 0:512],
                    func=AF.Silu, scale=mup_s[:, cb:cb + 1]),
                    waits=ws, inc="c_act")
            emit_tm_scratch(bank_evac[3], sz_dram, cb * 128)

    k.barrier()
    if STAGES <= 2:
        return finish_early()

    k.reset(SBASE)
    xs_bf = [k.sb(f"xsbf{i}", [128, H], BF16) for i in range(2)]
    state_bf = k.sb("state_bf", [128, H], BF16)
    lrep = [k.sb(f"lrep{i}", [128, 4096], F32) for i in range(2)]
    Dbuf = k.sb("Dbuf", [128, 4096], F32)
    Wbuf = k.sb("Wbuf", [128, 4096], F32)
    Pbuf = k.sb("Pbuf", [128, 4096], BF16)
    ybuf = k.sb("ybuf", [128, H], F32)
    yD = k.sb("yD", [128, H], F32)
    Dbcast = k.sb("Dbcast", [128, 4096], F32)
    Gtm = k.sb("Gtm", [128, 128], F32)
    B_tm = k.sb("B_tm", [128, 128], BF16)
    ssc = k.sb("ssc", [128, 2048], BF16)
    Xdd = k.sb("Xdd", [128, 2048], BF16)
    P2_END = k.mark()
    H_bf = k.sb("H_bf", [128, H], BF16, at=P2_END)
    assert P2_END + H * 2 <= k.SBLIM

    k.dma("scalar", Dbcast[:], drep_i.partition_broadcast(128), inc="d_cst")
    DB = k.val("d_cst")
    k.emit("vector", lambda e: e.memset(state_bf[:], 0.0), inc="c_dve")

    lam_consumed = {}
    xs_consumed = {}
    y_dma_tok = 0
    yasm_done = 0
    bank_free = {0: ("c_dve", 0), 2: ("c_dve", 0), 4: ("c_dve", 0)}
    for j in range(NCH):
        xsb = xs_bf[j % 2]
        if os.environ.get("MK_XSYNC") == "1":
            xw = [("c_pe", k.val("c_pe")), ("c_dve", k.val("c_dve")),
                  ("c_act", k.val("c_act")), ("c_pool", k.val("c_pool"))]
        else:
            xw = [("c_dve", k.val("c_dve")), ("c_pool", k.val("c_pool"))] if j < 2 else \
                 [("c_pe", xs_consumed[j - 2]), ("c_dve", lam_consumed[j - 2]),
                  ("c_pool", k.val("c_pool"))]
        xld = k.dma("sync", xsb[:], xs_dram[j], waits=xw, inc="d_xs")
        lld = []
        for hf in range(2):
            lws = [] if j == 0 else [("c_dve", lam_consumed.get(j - 1, 0))]
            lld.append(k.dma("sync", lrep[hf][:].rearrange("p (h s) -> p h s", h=32),
                             lam_dram[j, 32 * hf:32 * (hf + 1), :].unsqueeze(0).partition_broadcast(128),
                             waits=lws if hf == 0 else (), inc="d_lrep"))
        gt = k.emit("tensor", lambda e, j=j: e.matmul(
            out=psb[6][:, 0:128], lhsT=B_bf[:, j * 128:(j + 1) * 128],
            rhs=C_bf[:, j * 128:(j + 1) * 128], start=True, stop=True),
            waits=[("c_dve", k.val("c_dve"))], inc="c_pe")
        gtm = k.emit("vector", lambda e: e.tensor_tensor(out=Gtm[:], in0=psb[6][:, 0:128],
                                                         in1=maskbin[:], op=ALU.mult),
                     waits=[("c_pe", gt)], inc="c_dve")
        btr = k.emit("tensor", lambda e, j=j: e.transpose(
            out=pbf(6)[:, 512:640], in_=B_bf[:, j * 128:(j + 1) * 128], identity=ident_bf[:]),
            inc="c_pe")
        bev = k.emit("vector", lambda e: e.tensor_copy(out=B_tm[:], in_=pbf(6)[:, 512:640]),
                     waits=[("c_pe", btr)], inc="c_dve")
        if os.environ.get("MK_YD_DVE") == "1":
            ydt = k.emit("vector", lambda e, xsb=xsb: e.tensor_tensor(out=yD[:], in0=xsb[:],
                                                                       in1=Dbcast[:], op=ALU.mult),
                         waits=[("d_xs", xld), ("d_cst", DB)], inc="c_dve")
            yd_sem = "c_dve"
        else:
            ydt = k.emit("gpsimd", lambda e, xsb=xsb: e.tensor_tensor(out=yD[:], in0=xsb[:],
                                                                       in1=Dbcast[:], op=ALU.mult),
                         waits=[("d_xs", xld), ("d_cst", DB), ("c_dve", yasm_done)], inc="c_pool")
            yd_sem = "c_pool"

        for hf in range(2):
            hbase = 32 * hf
            dtok = None
            for hh in range(32):
                h = hbase + hh
                ws = []
                if hh == 0:
                    ws = [("d_lrep", lld[hf]), ("c_act", k.val("c_act"))]
                dtok = k.emit("vector", lambda e, hh=hh, h=h, hf=hf, j=j: e.tensor_scalar(
                    out=Dbuf[:, hh * 128:(hh + 1) * 128],
                    in0=lrep[hf][:, hh * 128:(hh + 1) * 128],
                    scalar1=lsd_tmA[:, j * 64 + h: j * 64 + h + 1], scalar2=88.0,
                    op0=ALU.subtract, op1=ALU.min),
                    waits=ws, inc="c_dve" if hh == 31 else None)
            if hf == 1:
                lam_consumed[j] = dtok
                if SCAN_DEBUG and j == 2:
                    k.dma("scalar", scan_taps["snap_D"], Dbuf[:], waits=[("c_dve", dtok)], inc="d_scrw")
                    k.dma("scalar", scan_taps["snap_lr1b"], lrep[1][:], waits=[("c_dve", dtok)], inc="d_scrw")
            wt = k.emit("scalar", lambda e: e.activation(out=Wbuf[:], in_=Dbuf[:], func=AF.Exp),
                        waits=[("c_dve", dtok), ("c_dve", yasm_done)], inc="c_act")
            pt = k.emit("vector", lambda e: e.tensor_tensor(
                out=Pbuf[:].rearrange("p (h t) -> p h t", h=32),
                in0=Wbuf[:].rearrange("p (h t) -> p h t", h=32),
                in1=Gtm[:].unsqueeze(1).broadcast_to([128, 32, 128]),
                op=ALU.mult),
                waits=[("c_act", wt), ("c_dve", gtm), ("c_pe", k.val("c_pe"))], inc="c_dve")
            xdt = None
            for hh in range(32):
                h = hbase + hh
                xdt = k.emit("vector", lambda e, hh=hh, h=h, xsb=xsb: e.tensor_scalar(
                    out=Xdd[:, hh * 64:(hh + 1) * 64], in0=xsb[:, h * 64:(h + 1) * 64],
                    scalar1=Wbuf[:, hh * 128 + 127: hh * 128 + 128], scalar2=None, op0=ALU.mult),
                    waits=[("c_act", wt), ("d_xs", xld), ("c_pe", k.val("c_pe"))] if hh == 0 else (),
                    inc="c_dve" if hh == 31 else None)
            sst = None
            for hh in range(32):
                h = hbase + hh
                sst = k.emit("vector", lambda e, hh=hh, h=h, j=j: e.tensor_scalar(
                    out=ssc[:, hh * 64:(hh + 1) * 64], in0=state_bf[:, h * 64:(h + 1) * 64],
                    scalar1=edecQA[:, j * 64 + h: j * 64 + h + 1], scalar2=None, op0=ALU.mult),
                    waits=[("c_pe", k.val("c_pe"))] if hh == 0 else (),
                    inc="c_dve" if hh == 31 else None)
            for sg in range(2):
                sgb = hbase + sg * 16
                y1t = y2t = sut = None
                for hh in range(16):
                    lhh = sg * 16 + hh
                    h = hbase + lhh
                    bank = hh // 8
                    slot = (hh % 8) * 64
                    ws = ([("c_dve", pt), bank_free[0], ("d_xs", xld)] if hh == 0 else ())
                    y1t = k.emit("tensor", lambda e, lhh=lhh, h=h, bank=bank, slot=slot, xsb=xsb: e.matmul(
                        out=psb[bank][:, slot:slot + 64],
                        lhsT=Pbuf[:, lhh * 128:(lhh + 1) * 128],
                        rhs=xsb[:, h * 64:(h + 1) * 64], start=True, stop=True),
                        waits=ws, inc="c_pe" if hh == 15 else None)
                for hh in range(16):
                    h = hbase + sg * 16 + hh
                    bank = 2 + hh // 8
                    slot = (hh % 8) * 64
                    ws = ([bank_free[2], ("c_dve", sst)] if hh == 0 else ())
                    y2t = k.emit("tensor", lambda e, h=h, bank=bank, slot=slot, j=j: e.matmul(
                        out=psb[bank][:, slot:slot + 64],
                        lhsT=C_bf[:, j * 128:(j + 1) * 128],
                        rhs=state_bf[:, h * 64:(h + 1) * 64], start=True, stop=True),
                        waits=ws, inc="c_pe" if hh == 15 else None)
                for hh in range(16):
                    lhh = sg * 16 + hh
                    h = hbase + lhh
                    bank = 4 + hh // 8
                    slot = (hh % 8) * 64
                    ws = []
                    if hh == 0:
                        ws = [("c_dve", xdt), ("c_dve", sst), ("c_dve", bev), bank_free[4]]
                    k.emit("tensor", lambda e, lhh=lhh, bank=bank, slot=slot: e.matmul(
                        out=psb[bank][:, slot:slot + 64], lhsT=B_tm[:],
                        rhs=Xdd[:, lhh * 64:(lhh + 1) * 64], start=True, stop=False),
                        waits=ws, inc=None)
                    sut = k.emit("tensor", lambda e, lhh=lhh, bank=bank, slot=slot: e.matmul(
                        out=psb[bank][:, slot:slot + 64], lhsT=ident_bf[:],
                        rhs=ssc[:, lhh * 64:(lhh + 1) * 64], start=False, stop=True),
                        inc="c_pe" if hh == 15 else None)
                pa = None
                for hh in range(16):
                    lhh = sg * 16 + hh
                    h = hbase + lhh
                    ws = [("c_pe", y2t), (yd_sem, ydt)] if hh == 0 else ()
                    if NO_Y2:
                        pa = k.emit("vector", lambda e, lhh=lhh, h=h: e.scalar_tensor_tensor(
                            out=Wbuf[:, lhh * 64:(lhh + 1) * 64],
                            in0=psb[2 + (lhh % 16) // 8][:, ((lhh % 16) % 8) * 64:((lhh % 16) % 8) * 64 + 64],
                            scalar=0.0,
                            in1=yD[:, h * 64:(h + 1) * 64], op0=ALU.mult, op1=ALU.add),
                            waits=ws, inc="c_dve" if hh == 15 else None)
                    else:
                        pa = k.emit("vector", lambda e, lhh=lhh, h=h, j=j: e.scalar_tensor_tensor(
                            out=Wbuf[:, lhh * 64:(lhh + 1) * 64],
                            in0=psb[2 + (lhh % 16) // 8][:, ((lhh % 16) % 8) * 64:((lhh % 16) % 8) * 64 + 64],
                            scalar=edec_tmA[:, j * 64 + h: j * 64 + h + 1],
                            in1=yD[:, h * 64:(h + 1) * 64], op0=ALU.mult, op1=ALU.add),
                            waits=ws, inc="c_dve" if hh == 15 else None)
                bank_free[2] = ("c_dve", pa)
                pb_ = None
                for hh in range(16):
                    lhh = sg * 16 + hh
                    h = hbase + lhh
                    ws = [("c_dve", pa), ("c_pe", y1t), ("d_scrw", y_dma_tok)] if hh == 0 else ()
                    if NO_Y1:
                        pb_ = k.emit("vector", lambda e, lhh=lhh, h=h: e.tensor_copy(
                            out=ybuf[:, h * 64:(h + 1) * 64],
                            in_=Wbuf[:, lhh * 64:(lhh + 1) * 64]),
                            waits=ws, inc="c_dve" if hh == 15 else None)
                    else:
                        pb_ = k.emit("vector", lambda e, lhh=lhh, h=h: e.tensor_tensor(
                            out=ybuf[:, h * 64:(h + 1) * 64],
                            in0=Wbuf[:, lhh * 64:(lhh + 1) * 64],
                            in1=psb[(lhh % 16) // 8][:, ((lhh % 16) % 8) * 64:((lhh % 16) % 8) * 64 + 64],
                            op=ALU.add),
                            waits=ws, inc="c_dve" if hh == 15 else None)
                bank_free[0] = ("c_dve", pb_)
                se = None
                for hh in range(16):
                    lhh = sg * 16 + hh
                    h = hbase + lhh
                    ws = [("c_pe", sut)] if hh == 0 else ()
                    se = k.emit("vector", lambda e, lhh=lhh, h=h: e.tensor_copy(
                        out=state_bf[:, h * 64:(h + 1) * 64],
                        in_=psb[4 + (lhh % 16) // 8][:, ((lhh % 16) % 8) * 64:((lhh % 16) % 8) * 64 + 64]),
                        waits=ws, inc="c_dve" if hh == 15 else None)
                bank_free[4] = ("c_dve", se)
                yasm_done = pb_
        xs_consumed[j] = k.val("c_pe")
        ydbg = k.emit("vector", lambda e: e.engine_nop(), waits=[("c_dve", yasm_done)], inc="c_ydbg")
        if SCAN_DEBUG and j == 2:
            k.dma("scalar", scan_taps["snap_xs"], xs_bf[0][:],
                  waits=[("c_dve", k.val("c_dve")), ("c_pe", k.val("c_pe"))], inc="d_scrw")
            k.dma("scalar", scan_taps["snap_y"], ybuf[:], waits=[("c_dve", yasm_done)], inc="d_scrw")
            k.dma("scalar", scan_taps["snap_P"], Pbuf[:], inc="d_scrw")
            k.dma("scalar", scan_taps["snap_Xdd"], Xdd[:], inc="d_scrw")
            k.dma("scalar", scan_taps["snap_ssc"], ssc[:], inc="d_scrw")
            k.dma("scalar", scan_taps["snap_W"], Wbuf[:], inc="d_scrw")
            k.dma("scalar", scan_taps["snap_lr0"], lrep[0][:], inc="d_scrw")
            k.dma("scalar", scan_taps["snap_lr1"], lrep[1][:], inc="d_scrw")
            k.dma("scalar", scan_taps["snap_lsd"], lsd_tmA[:], inc="d_scrw")
            k.dma("scalar", scan_taps["snap_edq"], edecQA[:], inc="d_scrw")
            k.dma("scalar", scan_taps["snap_edt"], edec_tmA[:], inc="d_scrw")
        y_dma_tok = k.dma("scalar", y_dram[j], ybuf[:],
                          waits=[("c_dve", yasm_done), ("c_ydbg", ydbg)], inc="d_scrw")

    if STAGES <= 3:
        return finish_early()
    cw = k.dma("sync", cc_in, state_bf[:], waits=[("c_dve", k.val("c_dve"))], inc="d_cin")
    k.emit("gpsimd", lambda e: e.memset(H_bf[:], 0.0), waits=[("c_pe", k.val("c_pe"))], inc="c_pool")
    k.wait("gpsimd", [("d_cin", cw)])
    k.emit("gpsimd", lambda e: e.collective_compute(
        "AllGather", ALU.bypass, replica_groups=[list(range(NCORES))],
        ins=[cc_in.opt()], outs=[cc_out.opt()]), inc="ccs")
    k.wait("gpsimd", [("ccs", 1)])
    gat = k.emit("gpsimd", lambda e: e.indirect_dma_start(
        out=H_bf[:], out_offset=None, in_=cc_out[0:128, :],
        in_offset=IndirectOffsetOnAxis(ap=hidx_s[:, 0:1], axis=0),
        bounds_check=128 * NCORES - 1, oob_is_err=False),
        waits=[("d_cst", CST)], inc="d_cin", inc_n=16)
    k.barrier()

    if STAGES <= 4:
        return finish_early()

    k.reset(SBASE)
    y3_fm = k.sb("y3_fm", [128, 32 * 1024], BF16)
    Y3END = k.mark()
    ytile = [k.sb(f"ytile{i}", [128, H], F32) for i in range(2)]
    sztile = [k.sb(f"sztile{i}", [128, H], BF16) for i in range(2)]
    ycbuf = k.sb("ycbuf", [128, H], F32)
    gbuf = k.sb("gbuf", [128, H], F32)
    y3tm = k.sb("y3tm", [128, H], BF16)
    expl = k.sb("expl", [128, 64], F32)
    P2B_END = k.mark()
    assert P2B_END <= P2_END

    SC2 = k.val("d_scrw")

    def p2b_half(hf, y_w_total):
        toks = {"y3": []}
        for jj in range(8):
            j = hf * 8 + jj
            yt = ytile[jj % 2]
            szt = sztile[jj % 2]
            ws = [("c_act", k.val("c_act")), ("c_dve", k.val("c_dve"))] if jj < 2 else \
                 [("c_act", toks[f"ssq{jj-2}"]), ("c_dve", toks[f"g{jj-2}"])]
            yld = k.dma("sync", yt[:], y_dram[j], waits=ws + [("d_scrw", y_w_total)], inc="d_y")
            szld = k.dma("gpsimd", szt[:], sz_dram[j],
                         waits=[("c_dve", toks[f"g{jj-2}"])] if jj >= 2 else [], inc="d_sz")
            ex = k.emit("scalar", lambda e, j=j: e.activation(
                out=expl[:], in_=lam_tmA[:, j * 64:(j + 1) * 64], func=AF.Exp),
                waits=[("c_dve", k.val("c_dve"))], inc="c_act")
            yct = None
            for half2 in range(2):
                mmt = None
                for hh in range(32):
                    h = half2 * 32 + hh
                    bank = hh // 8 % 4
                    slot = (hh % 8) * 64
                    ws2 = []
                    if hh == 0:
                        ws2 = [("d_cin", k.val("d_cin")), ("c_dve", yct or 0),
                               ("c_act", k.val("c_act"))]
                    mmt = k.emit("tensor", lambda e, h=h, bank=bank, slot=slot, j=j: e.matmul(
                        out=psb[bank][:, slot:slot + 64],
                        lhsT=C_bf[:, j * 128:(j + 1) * 128],
                        rhs=H_bf[:, h * 64:(h + 1) * 64], start=True, stop=True),
                        waits=ws2, inc="c_pe" if hh == 31 else None)
                for hh in range(32):
                    h = half2 * 32 + hh
                    bank = hh // 8 % 4
                    slot = (hh % 8) * 64
                    ws2 = [("c_pe", mmt), ("d_y", yld), ("c_act", ex)] if hh == 0 else ()
                    yct = k.emit("vector", lambda e, h=h, bank=bank, slot=slot, yt=yt: e.scalar_tensor_tensor(
                        out=ycbuf[:, h * 64:(h + 1) * 64], in0=psb[bank][:, slot:slot + 64],
                        scalar=expl[:, h:h + 1], in1=yt[:, h * 64:(h + 1) * 64],
                        op0=ALU.mult, op1=ALU.add),
                        waits=ws2, inc="c_dve" if hh == 31 else None)
            gt_ = k.emit("vector", lambda e, szt=szt: e.tensor_tensor(out=gbuf[:], in0=ycbuf[:],
                                                                       in1=szt[:], op=ALU.mult),
                         waits=[("c_dve", yct), ("d_sz", szld)], inc="c_dve")
            toks[f"g{jj}"] = gt_
            sq = k.emit("scalar", lambda e, yt=yt: e.activation(out=yt[:], in_=gbuf[:], func=AF.Square,
                                                                accum_out=stat[:, 1:2]),
                        waits=[("c_dve", gt_)], inc="c_act")
            toks[f"ssq{jj}"] = sq
            l1 = k.emit("scalar", lambda e: e.activation(out=stat[:, 8:9], in_=stat[:, 1:2],
                                                         func=AF.Ln, scale=np.float32(1.0 / H), bias=np.float32(EPS)),
                        waits=[("c_act", sq)], inc="c_act")
            r1 = k.emit("scalar", lambda e: e.activation(out=stat[:, 9:10], in_=stat[:, 8:9],
                                                         func=AF.Exp, scale=np.float32(-0.5)),
                        waits=[("c_act", l1)], inc="c_act")
            ap2 = k.emit("scalar", lambda e: e.activation(out=y3tm[:], in_=gbuf[:], func=AF.Identity,
                                                          scale=stat[:, 9:10]),
                         waits=[("c_act", r1)], inc="c_act")
            for grp in range(4):
                bank = 4 + grp % 2
                trt = None
                for kk in range(8):
                    dsb = grp * 8 + kk
                    trt = k.emit("tensor", lambda e, dsb=dsb, bank=bank, kk=kk: e.transpose(
                        out=pbf(bank)[:, kk * 128:(kk + 1) * 128],
                        in_=y3tm[:, dsb * 128:(dsb + 1) * 128], identity=ident_bf[:]),
                        waits=([("c_act", ap2), ("c_dve", k.val("c_dve"))] if kk == 0 else ()),
                        inc="c_pe" if kk == 7 else None)
                evt = None
                for kk in range(8):
                    dsb = grp * 8 + kk
                    evt = k.emit("vector", lambda e, dsb=dsb, bank=bank, kk=kk, jj=jj: e.tensor_scalar(
                        out=y3_fm[:, dsb * 1024 + jj * 128: dsb * 1024 + (jj + 1) * 128],
                        in0=pbf(bank)[:, kk * 128:(kk + 1) * 128],
                        scalar1=nrmw_s[:, dsb:dsb + 1], scalar2=None, op0=ALU.mult),
                        waits=[("c_pe", trt)] if kk == 0 else (),
                        inc="c_dve" if kk == 7 else None)
                toks["y3"].append(evt)
        return toks["y3"][-1]

    def p3_half(hf, wop_bf, atstg):
        ev_prev = {0: 0, 1: 0, 2: 0, 3: 0}
        wld_prev = None
        for oc in range(8):
            wb = wop_bf[oc % 2]
            ws = [("c_act", k.val("c_act"))] if oc < 2 else [("c_pe", wcons[oc - 2])]
            wld = k.dma("gpsimd", wb[:], w_op[oc].rearrange("p b c -> p (b c)"), waits=ws, inc="d_w")
            if oc == 0:
                wcons = {}
            for tt in range(8):
                bank = tt % 4
                mmt = None
                for dsb in range(32):
                    ws2 = []
                    if dsb == 0:
                        ws2 = [("c_act", ev_prev[bank])]
                        if tt == 0:
                            ws2.append(("d_w", wld))
                        if tt == 0 and oc == 0:
                            ws2.append(("c_dve", k.val("c_dve")))
                    mmt = k.emit("tensor", lambda e, dsb=dsb, tt=tt, wb=wb, bank=bank: e.matmul(
                        out=psb[bank][:, 0:512],
                        lhsT=y3_fm[:, dsb * 1024 + tt * 128:dsb * 1024 + (tt + 1) * 128],
                        rhs=wb[:, dsb * 512:(dsb + 1) * 512],
                        start=(dsb == 0), stop=(dsb == 31)),
                        waits=ws2, inc="c_pe" if dsb == 31 else None)
                if tt == 6:
                    wcons[oc] = mmt
                ev = k.emit("scalar", lambda e, bank=bank, tt=tt: e.activation(
                    out=atstg[tt % 2][:], in_=psb[bank][:, 0:512], func=AF.Identity),
                    waits=[("c_pe", mmt), ("d_scrw", at_w.get((tt - 2) % 8, 0))], inc="c_act")
                ev_prev[bank] = ev
                if os.environ.get("MK_P3_NOSTORE") == "1":
                    at_w[tt] = k.val("d_scrw")
                elif os.environ.get("MK_P3_DBGSTORE") == "1":
                    at_w[tt] = k.dma("scalar", dbg_sink, atstg[tt % 2][:],
                                     waits=[("c_act", ev)], inc="d_scrw")
                else:
                    at_w[tt] = k.dma("scalar", at_dram[oc, hf * 8 + tt],
                                     atstg[tt % 2][:], waits=[("c_act", ev)], inc="d_scrw")
            wcons[oc] = mmt
        return k.val("d_scrw")

    y_total_after_p2 = y_dma_tok
    at_w = {}
    p2b_done0 = p2b_half(0, y_total_after_p2)
    k.barrier()
    k.reset(Y3END)
    wop_bf = [k.sb(f"wop{i}", [128, 32 * 512], BF16) for i in range(2)]
    atstg = [k.sb(f"atstg{i}", [128, 512], F32, at=224256 + 2048 * i) for i in range(2)]
    assert k.mark() <= P2_END
    if os.environ.get("MK_ONLY_P2B") != "1":
        p3_half(0, wop_bf, atstg)
    k.barrier()
    p2b_done1 = p2b_half(1, y_total_after_p2)
    k.barrier()
    if os.environ.get("MK_ONLY_P2B") != "1":
        p3_half(1, wop_bf, atstg)
    k.barrier()

    if STAGES <= 5:
        return finish_early()

    k.reset(SBASE)
    x3_fm = k.sb("x3_fm", [128, 32 * 1024], BF16)
    zT_pad = k.sb("zT_pad", [128, 1024], BF16)
    loraB_pad = k.sb("loraB_pad", [128, 4096], BF16)
    X3END = k.mark()
    attile = [k.sb(f"attile{i}", [128, H], F32) for i in range(2)]
    xintile = [k.sb(f"xint{i}", [128, H], F32) for i in range(2)]
    abuf = k.sb("abuf", [128, H], F32)
    xmbuf = k.sb("xmbuf", [128, H], F32)
    x3tm = k.sb("x3tm", [128, H], BF16)
    logi = k.sb("logi", [128, 4], F32)
    mx = k.sb("mx", [128, 1], F32)
    oh = k.sb("oh", [128, 4], F32)
    zmk = k.sb("zmk", [128, 32], BF16)
    P4_END = k.mark()
    assert P4_END <= k.SBLIM

    lb1 = k.dma("gpsimd", loraB_pad[0:32, :], loraB_t, inc="d_w")
    k.wait("vector", [("c_pe", k.val("c_pe"))])
    k.emit("vector", lambda e: e.memset(loraB_pad[32:64, :], 0.0), inc="c_dve")
    k.emit("vector", lambda e: e.memset(loraB_pad[64:128, :], 0.0), inc="c_dve")
    k.emit("vector", lambda e: e.memset(zT_pad[32:64, :], 0.0), inc="c_dve")
    k.emit("vector", lambda e: e.memset(zT_pad[64:128, :], 0.0), inc="c_dve")

    def p4_tile(hf, jj):
        j = hf * 8 + jj
        att = attile[jj % 2]
        xit = xintile[jj % 2]
        ws = [("c_act", k.val("c_act")), ("c_dve", k.val("c_dve"))] if jj < 2 else \
             [("c_act", p4toks[f"x3{jj-2}"]), ("c_dve", p4toks[f"xm{jj-2}"])]
        for oc in range(8):
            ald = k.dma("sync", att[:, oc * 512:(oc + 1) * 512], at_dram[oc, j],
                        waits=(ws + [("d_scrw", k.val("d_scrw"))]) if oc == 0 else (),
                        inc="d_at")
        xld = k.dma("sync", xit[:], xin.rearrange("(n p) h -> n p h", p=128)[j],
                    waits=[("c_dve", p4toks[f"xm{jj-2}"])] if jj >= 2 else [], inc="d_xin")
        st1 = ln_stats(att[:], abuf[:], False,
                       extra_waits=[("d_at", ald), ("c_act", k.val("c_act"))])
        a1 = k.emit("scalar", lambda e, att=att: e.activation(
            out=abuf[:], in_=att[:], func=AF.Identity, scale=stat[:, 10:11], bias=stat[:, 11:12]),
            waits=[("c_dve", st1)], inc="c_act")
        xm1 = k.emit("vector", lambda e, xit=xit: e.tensor_tensor(out=xmbuf[:], in0=abuf[:],
                                                                  in1=xit[:], op=ALU.add),
                     waits=[("c_act", a1), ("d_xin", xld)], inc="c_dve")
        p4toks[f"xm{jj}"] = xm1
        if os.environ.get("MK_NO_XMW") == "1":
            xmw = k.val("d_scrw")
        else:
            xmw = k.dma("scalar", xm_dram[j], xmbuf[:], waits=[("c_dve", xm1)], inc="d_scrw")
        st2 = ln_stats(xmbuf[:], abuf[:], False, extra_waits=[("c_dve", xm1), ("c_act", a1)])
        a2 = k.emit("scalar", lambda e: e.activation(
            out=x3tm[:], in_=xmbuf[:], func=AF.Identity, scale=stat[:, 10:11], bias=stat[:, 11:12]),
            waits=[("c_dve", st2)], inc="c_act")
        p4toks[f"x3{jj}"] = a2
        for grp in range(4):
            bank = 4 + grp % 2
            trt = None
            for kk in range(8):
                hb = grp * 8 + kk
                trt = k.emit("tensor", lambda e, hb=hb, bank=bank, kk=kk: e.transpose(
                    out=pbf(bank)[:, kk * 128:(kk + 1) * 128],
                    in_=x3tm[:, hb * 128:(hb + 1) * 128], identity=ident_bf[:]),
                    waits=([("c_act", a2), ("c_dve", k.val("c_dve"))] if kk == 0 else ()),
                    inc="c_pe" if kk == 7 else None)
            for kk in range(8):
                hb = grp * 8 + kk
                k.emit("vector", lambda e, hb=hb, bank=bank, kk=kk, jj=jj: e.tensor_copy(
                    out=x3_fm[:, hb * 1024 + jj * 128: hb * 1024 + (jj + 1) * 128],
                    in_=pbf(bank)[:, kk * 128:(kk + 1) * 128]),
                    waits=[("c_pe", trt)] if kk == 0 else (),
                    inc="c_dve" if kk == 7 else None)
        x3done = k.val("c_dve")
        if os.environ.get("MK_P4_MIN") == "1":
            return k.val("c_dve")
        gmt = None
        for hb in range(32):
            ws2 = [("c_dve", x3done), ("c_act", k.val("c_act"))] if hb == 0 else ()
            gmt = k.emit("tensor", lambda e, hb=hb, jj=jj: e.matmul(
                out=psb[6][:, 0:4],
                lhsT=x3_fm[:, hb * 1024 + jj * 128: hb * 1024 + (jj + 1) * 128],
                rhs=gate_bf[:, hb * 4:(hb + 1) * 4],
                start=(hb == 0), stop=(hb == 31)),
                waits=ws2, inc="c_pe" if hb == 31 else None)
        lg = k.emit("vector", lambda e: e.tensor_tensor(out=logi[:], in0=psb[6][:, 0:4],
                                                        in1=ebias_r[:], op=ALU.add),
                    waits=[("c_pe", gmt)], inc="c_dve")
        mxt = k.emit("vector", lambda e: e.reduce_max(out=mx[:], in_=logi[:], axis=AX.X),
                     waits=[("c_dve", lg)], inc="c_dve")
        k.wait("vector", [("c_dve", mxt)])
        oht = k.emit("vector", lambda e: e.tensor_scalar(out=oh[:], in0=logi[:], scalar1=mx[:],
                                                         scalar2=None, op0=ALU.is_ge),
                     waits=[("c_dve", mxt)], inc="c_dve")
        zmt = None
        for hb in range(32):
            ws2 = [("c_dve", k.val("c_dve"))] if hb == 0 else ()
            zmt = k.emit("tensor", lambda e, hb=hb, jj=jj: e.matmul(
                out=psb[7][:, 0:32],
                lhsT=x3_fm[:, hb * 1024 + jj * 128: hb * 1024 + (jj + 1) * 128],
                rhs=loraA_bf[:, hb * 32:(hb + 1) * 32],
                start=(hb == 0), stop=(hb == 31)),
                waits=ws2, inc="c_pe" if hb == 31 else None)
        zmtk = None
        for ee in range(4):
            ws2 = [("c_pe", zmt), ("c_dve", oht)] if ee == 0 else ()
            zmtk = k.emit("vector", lambda e, ee=ee: e.tensor_scalar(
                out=zmk[:, ee * 8:(ee + 1) * 8], in0=psb[7][:, ee * 8:(ee + 1) * 8],
                scalar1=oh[:, ee:ee + 1], scalar2=None, op0=ALU.mult),
                waits=ws2, inc="c_dve" if ee == 3 else None)
        ztt = k.emit("tensor", lambda e: e.transpose(
            out=pbf(6)[0:32, 512:640], in_=zmk[:], identity=ident_bf[:]),
            waits=[("c_dve", zmtk)], inc="c_pe")
        zev = k.emit("vector", lambda e, jj=jj: e.tensor_copy(
            out=zT_pad[0:32, jj * 128:(jj + 1) * 128], in_=pbf(6)[0:32, 512:640]),
            waits=[("c_pe", ztt)], inc="c_dve")
        return zev

    def p5_half(hf, wup_bf, ffstg):
        ev_prev = {0: 0, 1: 0, 2: 0, 3: 0}
        for oc in range(8):
            wb = wup_bf[oc % 2]
            ws = [("c_act", k.val("c_act"))] if oc < 2 else [("c_pe", wcons5[oc - 2])]
            wld = k.dma("gpsimd", wb[:], w_up[oc].rearrange("p b c -> p (b c)"), waits=ws, inc="d_w")
            if oc == 0:
                wcons5 = {}
            for tt in range(8):
                bank = tt % 4
                mmt = None
                for dsb in range(32):
                    ws2 = []
                    if dsb == 0:
                        ws2 = [("c_act", ev_prev[bank])]
                        if tt == 0:
                            ws2.append(("d_w", wld))
                        if tt == 0 and oc == 0:
                            ws2.append(("c_dve", k.val("c_dve")))
                            ws2.append(("d_w", lb1))
                    mmt = k.emit("tensor", lambda e, dsb=dsb, tt=tt, wb=wb, bank=bank: e.matmul(
                        out=psb[bank][:, 0:512],
                        lhsT=x3_fm[:, dsb * 1024 + tt * 128:dsb * 1024 + (tt + 1) * 128],
                        rhs=wb[:, dsb * 512:(dsb + 1) * 512],
                        start=(dsb == 0), stop=False),
                        waits=ws2, inc=None)
                mmt = k.emit("tensor", lambda e, tt=tt, bank=bank, oc=oc: e.matmul(
                    out=psb[bank][:, 0:512],
                    lhsT=zT_pad[:, tt * 128:(tt + 1) * 128],
                    rhs=loraB_pad[:, oc * 512:(oc + 1) * 512],
                    start=False, stop=True),
                    inc="c_pe")
                ev = k.emit("scalar", lambda e, bank=bank, tt=tt: e.activation(
                    out=ffstg[tt % 2][:], in_=psb[bank][:, 0:512], func=AF.Identity),
                    waits=[("c_pe", mmt), ("d_scrw", ff_w.get((tt - 2) % 8, 0))], inc="c_act")
                ev_prev[bank] = ev
                if os.environ.get("MK_P5_NOSTORE") == "1":
                    ff_w[tt] = k.val("d_scrw")
                else:
                    ff_w[tt] = k.dma("scalar", ff_dram[oc, hf * 8 + tt],
                                     ffstg[tt % 2][:], waits=[("c_act", ev)], inc="d_scrw")
            wcons5[oc] = mmt

    def p6_tile(j, fftile, xmtile, obuf):
        fft = fftile[j % 2]
        xmt = xmtile[j % 2]
        ws = [("c_act", k.val("c_act")), ("d_out", k.val("d_out"))] if True else []
        for oc in range(8):
            fld = k.dma("sync", fft[:, oc * 512:(oc + 1) * 512], ff_dram[oc, j],
                        waits=(ws + [("d_scrw", k.val("d_scrw"))]) if oc == 0 else (),
                        inc="d_ff")
        xld = k.dma("sync", xmt[:], xm_dram[j], inc="d_xm")
        st1 = ln_stats(fft[:], obuf[:], False, extra_waits=[("d_ff", fld), ("c_act", k.val("c_act"))])
        a1 = k.emit("scalar", lambda e, fft=fft: e.activation(
            out=obuf[:], in_=fft[:], func=AF.Identity, scale=stat[:, 10:11], bias=stat[:, 11:12]),
            waits=[("c_dve", st1)], inc="c_act")
        o1 = k.emit("vector", lambda e, xmt=xmt, fft=fft: e.tensor_tensor(
            out=fft[:], in0=obuf[:], in1=xmt[:], op=ALU.add),
            waits=[("c_act", a1), ("d_xm", xld)], inc="c_dve")
        return k.dma("scalar", out_d.rearrange("(n p) h -> n p h", p=128)[j], fft[:],
                     waits=[("c_dve", o1)], inc="d_out")

    ff_w = {}
    k.reset(X3END)
    wup_bf = [k.sb(f"wup{i}", [128, 32 * 512], BF16) for i in range(2)]
    ffstg = [k.sb(f"ffstg{i}", [128, 512], F32, at=220000 + 2048 * i) for i in range(2)]
    k.reset(P4_END)
    for hf in range(2):
        p4toks = {}
        k.barrier()
        zl = None
        for jj in range(8):
            zl = p4_tile(hf, jj)
        k.barrier()
        if STAGES >= 7:
            p5_half(hf, wup_bf, ffstg)
            k.barrier()
    if STAGES <= 7:
        return finish_early()

    k.reset(X3END)
    fftile = [k.sb(f"fft{i}", [128, H], F32) for i in range(2)]
    xmtile = [k.sb(f"xmt{i}", [128, H], F32) for i in range(2)]
    obuf = k.sb("obuf", [128, H], F32)
    assert k.mark() <= k.SBLIM
    k.barrier()
    last_out = 0
    for j in range(NTT):
        last_out = p6_tile(j, fftile, xmtile, obuf)

    if DEBUG_TAPS:
        for tp, src in taps:
            k.dma("gpsimd", tp, src[:], waits=[("d_out", last_out)], inc="d_scrw")

    k.wait("gpsimd", [("d_out", last_out), ("d_scrw", k.val("d_scrw"))])
    k.wait("sync", [("d_out", last_out)])

    return k.finalize()


_NC_CACHE = None


def _get_nc():
    global _NC_CACHE
    if _NC_CACHE is None:
        _NC_CACHE = build_nc()
    return _NC_CACHE


def host_layout(inputs):
    f32 = np.float32
    hs = np.ascontiguousarray(inputs["L0_attn_hidden_states"], dtype=f32)
    Wip = np.ascontiguousarray(inputs["L0_attn_in_proj_w"], dtype=f32)
    Wop = np.ascontiguousarray(inputs["L0_attn_out_proj_w"], dtype=f32)
    Wup = np.ascontiguousarray(inputs["L0_ffn_W_up"], dtype=f32)
    gate = np.ascontiguousarray(inputs["L0_ffn_gate_w"], dtype=f32)
    loraA = np.ascontiguousarray(inputs["L0_ffn_lora_A"], dtype=f32)
    loraB = np.ascontiguousarray(inputs["L0_ffn_lora_B"], dtype=f32)
    convw = np.ascontiguousarray(inputs["L0_attn_conv_w"], dtype=f32)
    mup = np.ascontiguousarray(inputs["L0_attn_mup_vector"], dtype=f32)
    lnw = np.ascontiguousarray(inputs["L0_attn_ln_w"], dtype=f32)
    nrmw = np.ascontiguousarray(inputs["L0_attn_norm_w"], dtype=f32)
    dtb = np.ascontiguousarray(inputs["L0_attn_dt_bias"], dtype=f32)
    alog = np.ascontiguousarray(inputs["L0_attn_A_log"], dtype=f32)
    Dv = np.ascontiguousarray(inputs["L0_attn_D"], dtype=f32)
    ebias = np.ascontiguousarray(inputs["L0_ffn_expert_bias"], dtype=f32)

    wip_pad = np.zeros((4096, NCB * 128), f32)
    wip_pad[:, :8512] = Wip
    w_ip_t = np.ascontiguousarray(
        wip_pad.reshape(32, 128, NCB, 128).transpose(2, 1, 0, 3))
    w_op_t = np.ascontiguousarray(Wop.reshape(32, 128, 8, 512).transpose(2, 1, 0, 3))
    WupT = np.ascontiguousarray(Wup.T)
    w_up_t = np.ascontiguousarray(WupT.reshape(32, 128, 8, 512).transpose(2, 1, 0, 3))
    gate_tt = np.ascontiguousarray(gate.T.reshape(32, 128, 4).transpose(1, 0, 2))
    loraA_cat = np.ascontiguousarray(loraA.transpose(2, 0, 1).reshape(4096, 32))
    loraA_tt = np.ascontiguousarray(loraA_cat.reshape(32, 128, 32).transpose(1, 0, 2))
    loraB_tt = np.ascontiguousarray(loraB.transpose(0, 2, 1).reshape(32, 4096))
    conv_tt = np.ascontiguousarray(convw.reshape(34, 128, 4))
    mup_pad = np.zeros((NCB * 128,), f32)
    mup_pad[:8512] = mup
    mup_tt = np.ascontiguousarray(mup_pad.reshape(NCB, 128).T)
    lnw_tt = np.ascontiguousarray(lnw.reshape(32, 128).T)
    nrmw_tt = np.ascontiguousarray(nrmw.reshape(32, 128).T)
    drep = np.ascontiguousarray(np.repeat(Dv, 64).reshape(1, 4096))
    identb = np.eye(128, dtype=ml_dtypes.bfloat16)
    identf = np.eye(128, dtype=f32)
    maskb = np.tril(np.ones((128, 128), f32)).T
    maskb = np.ascontiguousarray(np.where(
        np.arange(128)[:, None] <= np.arange(128)[None, :], 1.0, 0.0).astype(f32))

    shared = {
        "w_ip": w_ip_t, "w_op": w_op_t, "w_up": w_up_t, "gate_t": gate_tt,
        "loraA_t": loraA_tt, "loraB_t": loraB_tt, "conv_t": conv_tt,
        "mup_t": mup_tt, "lnw_t": lnw_tt, "nrmw_t": nrmw_tt,
        "dtb": dtb.reshape(64, 1), "alog": alog.reshape(64, 1),
        "drep": drep, "ebias": ebias.reshape(1, 4),
        "identb": identb, "identf": identf, "maskbin": maskb,
    }

    in_maps = []
    for c in range(NCORES):
        bb, hf = c // 2, c % 2
        t0 = hf * T
        m = dict(shared)
        m["xin"] = np.ascontiguousarray(hs[bb, t0:t0 + T])
        xh = np.zeros((4, H), f32)
        if hf == 1:
            xh[0:3] = hs[bb, t0 - 3:t0]
        m["xhalo"] = xh
        if hf == 1:
            idx = ((c - 1) * 128 + np.arange(128, dtype=np.int32)).reshape(128, 1)
        else:
            idx = np.full((128, 1), 1 << 20, dtype=np.int32)
        m["hidx"] = idx
        in_maps.append(m)
    return in_maps


def kernel(**inputs):
    in_maps = host_layout(inputs)
    nc = _get_nc()
    res = run_bass_kernel_spmd(nc, in_maps, core_ids=list(range(NCORES)))
    out = np.zeros((4, 4096, 4096), np.float32)
    for c in range(NCORES):
        bb, hf = c // 2, c % 2
        out[bb, hf * T:(hf + 1) * T] = np.asarray(res.results[c]["out"])
    return out


if __name__ == "__main__":
    nc = build_nc()
    print("built ok")


# version 03
# speedup vs baseline: 1.0042x; 1.0042x over previous
import os
import numpy as np
import ml_dtypes
import concourse.bass as bass
import concourse.mybir as mybir
from concourse.bass_utils import run_bass_kernel_spmd
from concourse.bass_core_rust import IndirectOffsetOnAxis

F32 = mybir.dt.float32
BF16 = mybir.dt.bfloat16
I32 = mybir.dt.int32
AF = mybir.ActivationFunctionType
ALU = mybir.AluOpType
AX = mybir.AxisListType

NCORES = 8
T = 2048
NTT = 16
H = 4096
NCB = 67
NCH = 16
EPS = 1e-5
TP = T + 4

DEBUG_TAPS = bool(int(os.environ.get("MK_DEBUG_TAPS", "0")))
STAGES = int(os.environ.get("MK_STAGES", "99"))
SCAN_DEBUG = bool(int(os.environ.get("MK_SCAN_DEBUG", "0")))
NO_Y2 = os.environ.get("MK_NO_Y2") == "1"
NO_Y1 = os.environ.get("MK_NO_Y1") == "1"


class KB:
    ENGS = ("sync", "tensor", "vector", "scalar", "gpsimd")

    def __init__(self):
        self.nc = bass.Bass("TRN2", target_bir_lowering=False, debug=False)
        self.q = {e: [] for e in self.ENGS}
        self.semc = {}
        self.sem_order = []
        self._cursor = 16512
        self.SBLIM = 229344

    def sem(self, *names):
        for n in names:
            self.semc[n] = 0
            self.sem_order.append(n)

    def val(self, s):
        return self.semc[s]

    def emit(self, eng, fn, waits=(), inc=None, inc_n=1):
        post = None
        if inc is not None:
            self.semc[inc] += inc_n
            post = self.semc[inc]
        self.q[eng].append((tuple((s, v) for s, v in waits if v > 0), fn, inc, inc_n))
        return post

    def dma(self, eng, out, in_, waits=(), inc=None, **kw):
        return self.emit(eng, (lambda e, o=out, i=in_, kk=dict(kw): e.dma_start(out=o, in_=i, **kk)),
                         waits=waits, inc=inc, inc_n=16)

    def wait(self, eng, waits):
        self.emit(eng, None, waits=waits)

    def barrier(self):
        vals = [(s, self.semc[s]) for s in self.sem_order]
        for e in self.ENGS:
            self.wait(e, vals)

    def sb(self, name, shape, dtype, at=None):
        sz = int(np.prod(shape[1:])) * np.dtype(mybir.dt.np(dtype)).itemsize
        off = self._cursor if at is None else at
        off = (off + 63) & ~63
        end = off + sz
        assert end <= self.SBLIM, f"SBUF overflow at {name}: {off}+{sz}"
        if at is None:
            self._cursor = end
        return self.nc.alloc_sbuf_tensor_at(name, list(shape), dtype, offset=off)

    def mark(self):
        return self._cursor

    def reset(self, c):
        self._cursor = c

    def finalize(self):
        nc = self.nc
        from contextlib import ExitStack
        with ExitStack() as ctx:
            handles = {n: ctx.enter_context(nc.semaphore(n)) for n in self.sem_order}
            block = ctx.enter_context(nc.Block())
            for eng in self.ENGS:
                qeng = self.q[eng]

                def body(e, qeng=qeng):
                    for waits, fn, inc, inc_n in qeng:
                        for s, v in waits:
                            e.wait_ge(handles[s], v)
                        if fn is not None:
                            ins = fn(e)
                            if inc is not None:
                                ins.then_inc(handles[inc], inc_n)
                getattr(block, eng)(body)
        return nc


def build_nc():
    k = KB()
    nc = k.nc

    def din(name, shape, dt=F32):
        return nc.dram_tensor(name, list(shape), dt, kind="ExternalInput").ap()

    xin = din("xin", [T, H])
    xhalo = din("xhalo", [4, H])
    w_ip = din("w_ip", [NCB, 128, 32, 128])
    w_op = din("w_op", [8, 128, 32, 512])
    w_up = din("w_up", [8, 128, 32, 512])
    gate_t = din("gate_t", [128, 32, 4])
    loraA_t = din("loraA_t", [128, 32, 32])
    loraB_t = din("loraB_t", [32, 4096])
    conv_t = din("conv_t", [34, 128, 4])
    mup_t = din("mup_t", [128, 67])
    lnw_t = din("lnw_t", [128, 32])
    nrmw_t = din("nrmw_t", [128, 32])
    dtb_i = din("dtb", [64, 1])
    alog_i = din("alog", [64, 1])
    drep_i = din("drep", [1, 4096])
    ebias_i = din("ebias", [1, 4])
    identb_i = din("identb", [128, 128], BF16)
    identf_i = din("identf", [128, 128])
    maskbin_i = din("maskbin", [128, 128])
    hidx_i = din("hidx", [128, 1], I32)

    out_d = nc.dram_tensor("out", [T, H], F32, kind="ExternalOutput").ap()

    def dint(name, shape, dt=F32):
        return nc.dram_tensor(name, list(shape), dt, kind="Internal").ap()

    xs_dram = dint("xs_dram", [NTT, 128, H], BF16)
    sz_dram = dint("sz_dram", [NTT, 128, H], BF16)
    y_dram = dint("y_dram", [NTT, 128, H])
    at_dram = dint("at_dram", [8, NTT, 128, 512])
    xm_dram = dint("xm_dram", [NTT, 128, H])
    ff_dram = dint("ff_dram", [8, NTT, 128, 512])
    lam_dram = dint("lam_dram", [NCH, 64, 128])
    dbg_sink = dint("dbg_sink", [128, 512])
    cc_in = dint("cc_in", [128, H], BF16)
    cc_out = dint("cc_out", [128 * NCORES, H], BF16)

    taps = []
    scan_taps = {}
    if SCAN_DEBUG:
        for nm, shp, dt_ in [("snap_xs", [128, H], BF16), ("snap_y", [128, H], F32),
                             ("snap_P", [128, 4096], BF16), ("snap_Xdd", [128, 2048], BF16),
                             ("snap_ssc", [128, 2048], BF16), ("snap_W", [128, 4096], F32),
                             ("snap_lr0", [128, 4096], F32), ("snap_lr1", [128, 4096], F32),
                             ("snap_D", [128, 4096], F32), ("snap_lr1b", [128, 4096], F32),
                             ("snap_lsd", [128, 1024], F32), ("snap_edq", [128, 1024], F32),
                             ("snap_edt", [128, 1024], F32)]:
            scan_taps[nm] = nc.dram_tensor(nm, shp, dt_, kind="ExternalOutput").ap()
    if DEBUG_TAPS:
        for nm, src in [("tap_xs", xs_dram), ("tap_sz", sz_dram), ("tap_y", y_dram),
                        ("tap_at", at_dram), ("tap_xm", xm_dram), ("tap_ff", ff_dram),
                        ("tap_lam", lam_dram)]:
            tp = nc.dram_tensor(nm, list(src.tensor.shape), src.dtype, kind="ExternalOutput").ap()
            taps.append((tp, src))

    k.sem("d_xin", "d_w", "d_xs", "d_lrep", "d_y", "d_sz", "d_at", "d_xm", "d_ff",
          "d_cin", "d_scrw", "d_out", "d_cst", "d_cw", "ccs", "c_ydbg",
          "c_pe", "c_dve", "c_act", "c_pool")


    def finish_early():
        od = k.dma("gpsimd", out_d, xin, waits=[(s_, k.val(s_)) for s_ in k.sem_order
                                                if s_ not in ("d_out",)], inc="d_out")
        if DEBUG_TAPS:
            for tp, src in taps:
                k.dma("gpsimd", tp, src[:], inc="d_scrw")
        k.wait("gpsimd", [("d_out", od), ("d_scrw", k.val("d_scrw"))])
        k.wait("sync", [("d_out", od)])
        return k.finalize()
    ident_bf = k.sb("ident_bf", [128, 128], BF16)
    ident_f = k.sb("ident_f", [128, 128], F32)
    maskbin = k.sb("maskbin", [128, 128], F32)
    mup_s = k.sb("mup", [128, 68], F32)
    lnw_s = k.sb("lnw", [128, 32], F32)
    nrmw_s = k.sb("nrmw", [128, 32], F32)
    gate_bf = k.sb("gate_bf", [128, 32 * 4], BF16)
    loraA_bf = k.sb("loraA_bf", [128, 32 * 32], BF16)
    dtb_s = k.sb("dtb", [64, 1], F32)
    alog_s = k.sb("alog", [64, 1], F32)
    expA_s = k.sb("expA", [64, 1], F32)
    ebias_r = k.sb("ebias_r", [128, 4], F32)
    hidx_s = k.sb("hidx", [128, 1], I32)
    onesf = k.sb("onesf", [1, 128], F32)
    stat = k.sb("stat", [128, 16], F32)
    ones64 = k.sb("ones64", [64, 1], F32)
    zero64 = k.sb("zero64", [64, 1], F32)
    rowtmp = k.sb("rowtmp", [1, 64], F32)
    B_bf = k.sb("B_bf", [128, T], BF16)
    C_bf = k.sb("C_bf", [128, T], BF16)
    lsd_tmA = k.sb("lsd_tmA", [128, NCH * 64], F32)
    edec_tmA = k.sb("edec_tmA", [128, NCH * 64], F32)
    edecQA = k.sb("edecQA", [128, NCH * 64], F32)
    lam_tmA = k.sb("lam_tmA", [128, NCH * 64], F32)
    SBASE = k.mark()

    x2_fm = k.sb("x2_fm", [128, 32 * TP], BF16, at=SBASE)
    XEND = ((SBASE + 63) & ~63) + 32 * TP * 2

    psb = [nc.alloc_psum_tensor(f"psb{i}", [128, 512], F32) for i in range(8)]

    def pbf(i):
        return psb[i].ap().bitcast(BF16)

    for dst, src in [(ident_bf, identb_i), (ident_f, identf_i), (maskbin, maskbin_i),
                     (lnw_s, lnw_t), (nrmw_s, nrmw_t),
                     (dtb_s, dtb_i), (alog_s, alog_i), (hidx_s, hidx_i)]:
        k.dma("scalar", dst[:], src, inc="d_cst")
    k.dma("scalar", mup_s[:, 0:67], mup_t, inc="d_cst")
    k.dma("scalar", ebias_r[:], ebias_i.partition_broadcast(128), inc="d_cst")
    k.dma("gpsimd", gate_bf[:], gate_t.rearrange("p b e -> p (b e)"), inc="d_w")
    k.dma("gpsimd", loraA_bf[:], loraA_t.rearrange("p b e -> p (b e)"), inc="d_w")
    CST = k.val("d_cst")
    k.wait("vector", [("d_cst", CST)])
    k.emit("vector", lambda e: e.memset(onesf[:], 1.0), inc="c_dve")
    k.emit("vector", lambda e: e.memset(ones64[:], 1.0), inc="c_dve")
    k.emit("vector", lambda e: e.memset(zero64[:], 0.0), inc="c_dve")
    k.emit("scalar", lambda e: e.activation(out=expA_s[:], in_=alog_s[:], func=AF.Exp),
           waits=[("d_cst", CST)], inc="c_act")

    def ln_stats(xt, scrap, double, np_=128, extra_waits=()):
        ws = list(extra_waits)
        t_sum = k.emit("vector", lambda e: e.reduce_sum(out=stat[0:np_, 0:1], in_=xt, axis=AX.X),
                       waits=ws, inc="c_dve")
        t_ssq = k.emit("scalar", lambda e: e.activation(out=scrap, in_=xt, func=AF.Square,
                                                        accum_out=stat[0:np_, 1:2]),
                       waits=ws, inc="c_act")
        k.emit("vector", lambda e: e.tensor_scalar(out=stat[0:np_, 2:3], in0=stat[0:np_, 0:1],
                                                   scalar1=1.0 / H, scalar2=None, op0=ALU.mult),
               waits=[("c_dve", t_sum)], inc="c_dve")
        t1 = k.emit("vector", lambda e: e.tensor_scalar(out=stat[0:np_, 3:4], in0=stat[0:np_, 2:3],
                                                        scalar1=stat[0:np_, 2:3], scalar2=-1.0,
                                                        op0=ALU.mult, op1=ALU.mult), inc="c_dve")
        t2 = k.emit("vector", lambda e: e.scalar_tensor_tensor(out=stat[0:np_, 3:4], in0=stat[0:np_, 1:2],
                                                               scalar=1.0 / H, in1=stat[0:np_, 3:4],
                                                               op0=ALU.mult, op1=ALU.add),
                    waits=[("c_dve", t1), ("c_act", t_ssq)], inc="c_dve")
        a1 = k.emit("scalar", lambda e: e.activation(out=stat[0:np_, 4:5], in_=stat[0:np_, 3:4],
                                                     func=AF.Ln, bias=np.float32(EPS)),
                    waits=[("c_dve", t2)], inc="c_act")
        a2 = k.emit("scalar", lambda e: e.activation(out=stat[0:np_, 5:6], in_=stat[0:np_, 4:5],
                                                     func=AF.Exp, scale=np.float32(-0.5)),
                    waits=[("c_act", a1)], inc="c_act")
        if double:
            a3 = k.emit("scalar", lambda e: e.activation(out=stat[0:np_, 6:7], in_=stat[0:np_, 4:5],
                                                         func=AF.Exp, scale=np.float32(-1.0)),
                        waits=[("c_act", a2)], inc="c_act")
            v3 = k.emit("vector", lambda e: e.tensor_tensor(out=stat[0:np_, 7:8], in0=stat[0:np_, 3:4],
                                                            in1=stat[0:np_, 6:7], op=ALU.mult),
                        waits=[("c_act", a3)], inc="c_dve")
            a4 = k.emit("scalar", lambda e: e.activation(out=stat[0:np_, 8:9], in_=stat[0:np_, 7:8],
                                                         func=AF.Ln, bias=np.float32(EPS)),
                        waits=[("c_dve", v3)], inc="c_act")
            a5 = k.emit("scalar", lambda e: e.activation(out=stat[0:np_, 9:10], in_=stat[0:np_, 8:9],
                                                         func=AF.Exp, scale=np.float32(-0.5)),
                        waits=[("c_act", a4)], inc="c_act")
            v4 = k.emit("vector", lambda e: e.tensor_tensor(out=stat[0:np_, 10:11], in0=stat[0:np_, 5:6],
                                                            in1=stat[0:np_, 9:10], op=ALU.mult),
                        waits=[("c_act", a5)], inc="c_dve")
        else:
            v4 = k.emit("vector", lambda e: e.tensor_copy(out=stat[0:np_, 10:11], in_=stat[0:np_, 5:6]),
                        waits=[("c_act", a2)], inc="c_dve")
        v5 = k.emit("vector", lambda e: e.tensor_scalar(out=stat[0:np_, 11:12], in0=stat[0:np_, 2:3],
                                                        scalar1=stat[0:np_, 10:11], scalar2=-1.0,
                                                        op0=ALU.mult, op1=ALU.mult),
                    waits=[("c_dve", v4)], inc="c_dve")
        return v5

    k.reset(XEND)
    xin_t = [k.sb(f"xin{i}", [128, H], F32) for i in range(2)]
    x2tm = k.sb("x2tm", [128, H], BF16)
    assert k.mark() <= k.SBLIM

    apply_toks = []
    last_tr = 0
    for i in range(NTT + 1):
        halo = (i == NTT)
        np_ = 4 if halo else 128
        if halo:
            buf = xin_t[0][0:4, :]
            w = [("c_act", apply_toks[-2])]
            k.dma("sync", xin_t[0][0:4, :], xhalo, waits=w, inc="d_xin")
        else:
            buf = xin_t[i % 2][:]
            w = [("c_act", apply_toks[i - 2])] if i >= 2 else []
            k.dma("sync", xin_t[i % 2][:],
                  xin.rearrange("(n p) h -> n p h", p=128)[i], waits=w, inc="d_xin")
        ldv = k.val("d_xin")
        prev_apply = apply_toks[-1] if apply_toks else 0
        stok = ln_stats(buf, x2tm[0:np_, :], True, np_=np_,
                        extra_waits=[("d_xin", ldv), ("c_act", prev_apply),
                                     ("c_pe", last_tr), ("d_cst", CST)])
        ap_tok = k.emit("scalar", lambda e, b_=buf, np_=np_: e.activation(
            out=x2tm[0:np_, :], in_=b_, func=AF.Identity,
            scale=stat[0:np_, 10:11], bias=stat[0:np_, 11:12]),
            waits=[("c_dve", stok)], inc="c_act")
        apply_toks.append(ap_tok)
        for grp in range(4):
            bank = 4 + (grp % 2)
            trt = None
            for kk in range(8):
                hb = grp * 8 + kk
                if halo:
                    fn = (lambda e, hb=hb, bank=bank, kk=kk: e.transpose(
                        out=pbf(bank)[:, kk * 128:kk * 128 + 3],
                        in_=x2tm[0:3, hb * 128:(hb + 1) * 128], identity=ident_bf[0:3, 0:3]))
                else:
                    fn = (lambda e, hb=hb, bank=bank, kk=kk: e.transpose(
                        out=pbf(bank)[:, kk * 128:(kk + 1) * 128],
                        in_=x2tm[:, hb * 128:(hb + 1) * 128], identity=ident_bf[:]))
                trt = k.emit("tensor", fn,
                             waits=([("c_act", ap_tok), ("c_dve", k.val("c_dve")),
                                     ("d_cst", CST)] if kk == 0 else ()),
                             inc="c_pe" if kk == 7 else None)
            last_tr = trt
            for kk in range(8):
                hb = grp * 8 + kk
                if halo:
                    fn = (lambda e, hb=hb, bank=bank, kk=kk: e.tensor_scalar(
                        out=x2_fm[:, hb * TP + T: hb * TP + T + 3],
                        in0=pbf(bank)[:, kk * 128:kk * 128 + 3],
                        scalar1=lnw_s[:, hb:hb + 1], scalar2=None, op0=ALU.mult))
                else:
                    fn = (lambda e, hb=hb, bank=bank, kk=kk, i=i: e.tensor_scalar(
                        out=x2_fm[:, hb * TP + i * 128: hb * TP + (i + 1) * 128],
                        in0=pbf(bank)[:, kk * 128:(kk + 1) * 128],
                        scalar1=lnw_s[:, hb:hb + 1], scalar2=None, op0=ALU.mult))
                k.emit("vector", fn, waits=[("c_pe", trt), ("d_cst", CST)] if kk == 0 else (),
                       inc="c_dve" if kk == 7 else None)
    k.barrier()
    if STAGES <= 1:
        return finish_early()

    k.reset(XEND)
    wip_bf = [k.sb(f"wip{i}", [128, 4096], BF16) for i in range(2)]
    xbc_b = k.sb("xbc", [128, 2052], F32)
    convo = k.sb("convo", [128, T], F32)
    sil_bf = k.sb("sil_bf", [128, T], BF16)
    stg = k.sb("stg", [128, NTT * 128], BF16)
    cw_s = k.sb("cw", [128, 4], F32)
    dtmp = k.sb("dtmp", [64, T], F32)
    assert k.mark() <= k.SBLIM

    def emit_tm_scratch(sil_tok, dram, c0):
        ev_tok = None
        for g in range(2):
            bank = 6 + g
            trt = None
            for kk in range(8):
                tt = g * 8 + kk
                trt = k.emit("tensor", lambda e, tt=tt, bank=bank, kk=kk: e.transpose(
                    out=pbf(bank)[:, kk * 128:(kk + 1) * 128],
                    in_=sil_bf[:, tt * 128:(tt + 1) * 128], identity=ident_bf[:]),
                    waits=([("c_act", sil_tok), ("c_dve", k.val("c_dve"))] if kk == 0 else ()),
                    inc="c_pe" if kk == 7 else None)
            ev_tok = k.emit("vector", lambda e, g=g, bank=bank: e.tensor_copy(
                out=stg[:, g * 1024:(g + 1) * 1024], in_=pbf(bank)[:, 0:1024]),
                waits=[("c_pe", trt), ("d_scrw", k.val("d_scrw"))], inc="c_dve")
        return k.dma("scalar", dram[:, :, c0:c0 + 128].transpose([1, 0, 2]),
                     stg[:].rearrange("p (n c) -> p n c", n=NTT),
                     waits=[("c_dve", ev_tok)], inc="d_scrw")

    def dt_process(ev_tok):
        dr = convo[0:64, 0:T]
        a1 = k.emit("scalar", lambda e: e.activation(out=dtmp[:], in_=dr, func=AF.Abs, bias=dtb_s[:]),
                    waits=[("c_act", ev_tok)], inc="c_act")
        a2 = k.emit("scalar", lambda e: e.activation(out=dtmp[:], in_=dtmp[:], func=AF.Exp, scale=np.float32(-1.0)),
                    waits=[("c_act", a1)], inc="c_act")
        a3 = k.emit("scalar", lambda e: e.activation(out=dtmp[:], in_=dtmp[:], func=AF.Ln, bias=np.float32(1.0)),
                    waits=[("c_act", a2)], inc="c_act")
        a4 = k.emit("scalar", lambda e: e.activation(out=xbc_b[0:64, 0:T], in_=dr, func=AF.Relu,
                                                     bias=dtb_s[:]),
                    waits=[("c_act", a3)], inc="c_act")
        v1 = k.emit("vector", lambda e: e.tensor_tensor(out=dr, in0=dtmp[:], in1=xbc_b[0:64, 0:T],
                                                        op=ALU.add),
                    waits=[("c_act", a4)], inc="c_dve")
        a5 = k.emit("scalar", lambda e: e.activation(out=dtmp[:], in_=dr, func=AF.Ln),
                    waits=[("c_dve", v1)], inc="c_act")
        v2 = k.emit("vector", lambda e: e.tensor_scalar(out=xbc_b[0:64, 0:T], in0=dr,
                                                        scalar1=expA_s[:], scalar2=-1.0,
                                                        op0=ALU.mult, op1=ALU.mult),
                    waits=[("c_act", a5)], inc="c_dve")
        v3 = k.emit("vector", lambda e: e.tensor_tensor_scan(
            out=dr, data0=ones64[:].broadcast_to([64, T]), data1=xbc_b[0:64, 0:T],
            initial=0.0, op0=ALU.mult, op1=ALU.add),
            waits=[("c_dve", v2)], inc="c_dve")
        lamwr = k.dma("scalar", lam_dram.rearrange("c h s -> h c s"),
                      dr.rearrange("h (c s) -> h c s", c=NCH),
                      waits=[("c_dve", v3)], inc="d_scrw")
        v4 = k.emit("vector", lambda e: e.tensor_tensor(out=dtmp[:], in0=dr, in1=dtmp[:],
                                                        op=ALU.subtract),
                    waits=[("c_dve", v3)], inc="c_dve")
        a6 = k.emit("scalar", lambda e: e.activation(out=xbc_b[0:64, 0:T], in_=xbc_b[0:64, 0:T],
                                                     func=AF.Exp),
                    waits=[("c_dve", v4)], inc="c_act")
        for j in range(NCH):
            t1 = k.emit("tensor", lambda e, j=j: e.transpose(
                out=psb[7][:, 0:64], in_=dr[:, j * 128:(j + 1) * 128],
                identity=ident_f[0:64, 0:64]),
                waits=[("d_scrw", lamwr), ("c_dve", k.val("c_dve")), ("c_act", a6)], inc="c_pe")
            e1 = k.emit("vector", lambda e, j=j: e.tensor_copy(
                out=lam_tmA[:, j * 64:(j + 1) * 64], in_=psb[7][:, 0:64]),
                waits=[("c_pe", t1)], inc="c_dve")
            t2 = k.emit("tensor", lambda e, j=j: e.transpose(
                out=psb[7][:, 64:128], in_=dtmp[:, j * 128:(j + 1) * 128],
                identity=ident_f[0:64, 0:64]),
                waits=[("c_dve", e1)], inc="c_pe")
            e2 = k.emit("vector", lambda e, j=j: e.tensor_copy(
                out=lsd_tmA[:, j * 64:(j + 1) * 64], in_=psb[7][:, 64:128]),
                waits=[("c_pe", t2)], inc="c_dve")
            ed = xbc_b[0:64, 1924:2052]
            v5 = k.emit("vector", lambda e, j=j: e.tensor_tensor_scan(
                out=ed, data0=xbc_b[0:64, j * 128:(j + 1) * 128],
                data1=zero64[:].broadcast_to([64, 128]),
                initial=1.0, op0=ALU.mult, op1=ALU.add),
                waits=[("c_dve", e2)], inc="c_dve")
            t3 = k.emit("tensor", lambda e: e.transpose(
                out=psb[7][:, 128:192], in_=ed, identity=ident_f[0:64, 0:64]),
                waits=[("c_dve", v5)], inc="c_pe")
            e3 = k.emit("vector", lambda e, j=j: e.tensor_copy(
                out=edec_tmA[:, j * 64:(j + 1) * 64], in_=psb[7][:, 128:192]),
                waits=[("c_pe", t3)], inc="c_dve")
            t4 = k.emit("tensor", lambda e: e.transpose(
                out=psb[7][0:1, 192:256], in_=ed[:, 127:128], identity=ident_f[0:64, 0:64]),
                waits=[("c_dve", e3)], inc="c_pe")
            e4 = k.emit("vector", lambda e: e.tensor_copy(out=rowtmp[:], in_=psb[7][0:1, 192:256]),
                        waits=[("c_pe", t4)], inc="c_dve")
            t5 = k.emit("tensor", lambda e: e.matmul(out=psb[7][:, 256:320], lhsT=onesf[:],
                                                     rhs=rowtmp[:], start=True, stop=True),
                        waits=[("c_dve", e4)], inc="c_pe")
            k.emit("vector", lambda e, j=j: e.tensor_copy(
                out=edecQA[:, j * 64:(j + 1) * 64], in_=psb[7][:, 256:320]),
                waits=[("c_pe", t5)], inc="c_dve")
        return k.val("c_dve")

    cb_order = [66, 64, 65] + list(range(32, 64)) + list(range(0, 32))
    mm_done = {}
    bank_evac = {g: 0 for g in range(5)}
    conv_done = 0
    silu_done = 0

    for ki, cb in enumerate(cb_order):
        wb = wip_bf[ki % 2]
        wws = [("c_pe", mm_done[ki - 2])] if ki >= 2 else \
              [("c_act", k.val("c_act")), ("c_dve", k.val("c_dve")), ("c_pe", k.val("c_pe"))]
        wld = k.dma("gpsimd", wb[:], w_ip[cb].rearrange("p b c -> p (b c)"), waits=wws, inc="d_w")
        xbc = 32 <= cb <= 65
        if xbc:
            cwld = k.dma("sync", cw_s[:], conv_t[cb - 32], waits=[("c_dve", conv_done)], inc="d_cw")
        grp_tok = []
        for g in range(4):
            for hb in range(32):
                ws = []
                if g == 0 and hb == 0:
                    ws.append(("d_w", wld))
                if hb == 0:
                    ws.append(("c_act", bank_evac[g]))
                tok = k.emit("tensor", lambda e, g=g, hb=hb, wb=wb: e.matmul(
                    out=psb[g][:, 0:512],
                    lhsT=wb[:, hb * 128:(hb + 1) * 128],
                    rhs=x2_fm[:, hb * TP + g * 512: hb * TP + (g + 1) * 512],
                    start=(hb == 0), stop=(hb == 31)),
                    waits=ws, inc="c_pe" if hb == 31 else None)
            grp_tok.append(tok)
        halo_tok = None
        if xbc:
            for hb in range(32):
                ws = [("c_act", bank_evac[4])] if hb == 0 else ()
                halo_tok = k.emit("tensor", lambda e, hb=hb, wb=wb: e.matmul(
                    out=psb[4][:, 0:3],
                    lhsT=wb[:, hb * 128:(hb + 1) * 128],
                    rhs=x2_fm[:, hb * TP + T: hb * TP + T + 3],
                    start=(hb == 0), stop=(hb == 31)),
                    waits=ws, inc="c_pe" if hb == 31 else None)
        mm_done[ki] = halo_tok if halo_tok is not None else grp_tok[-1]

        if cb == 66:
            evs = []
            for g in range(4):
                evs.append(k.emit("scalar", lambda e, g=g: e.activation(
                    out=convo[0:64, g * 512:(g + 1) * 512], in_=psb[g][0:64, 0:512],
                    func=AF.Identity, scale=mup_s[0:64, 66:67]),
                    waits=[("c_pe", grp_tok[g])], inc="c_act"))
                bank_evac[g] = evs[-1]
            conv_done = dt_process(evs[-1])
        elif xbc:
            for g in range(4):
                ws = [("c_pe", grp_tok[g])]
                if g == 0:
                    ws += [("c_dve", conv_done)]
                bank_evac[g] = k.emit("scalar", lambda e, g=g, cb=cb: e.activation(
                    out=xbc_b[:, 3 + g * 512: 3 + (g + 1) * 512], in_=psb[g][:, 0:512],
                    func=AF.Identity, scale=mup_s[:, cb:cb + 1]),
                    waits=ws, inc="c_act")
            bank_evac[4] = k.emit("scalar", lambda e, cb=cb: e.activation(
                out=xbc_b[:, 0:3], in_=psb[4][:, 0:3],
                func=AF.Identity, scale=mup_s[:, cb:cb + 1]),
                waits=[("c_pe", halo_tok), ("c_dve", conv_done)], inc="c_act")
            cv = k.emit("vector", lambda e: e.tensor_scalar(
                out=convo[:, 0:T], in0=xbc_b[:, 0:T], scalar1=cw_s[:, 0:1],
                scalar2=None, op0=ALU.mult),
                waits=[("c_act", bank_evac[4]), ("c_act", bank_evac[3]), ("d_cw", cwld),
                       ("c_act", silu_done)], inc="c_dve")
            for kk in range(1, 4):
                cv = k.emit("vector", lambda e, kk=kk: e.scalar_tensor_tensor(
                    out=convo[:, 0:T], in0=xbc_b[:, kk:kk + T], scalar=cw_s[:, kk:kk + 1],
                    in1=convo[:, 0:T], op0=ALU.mult, op1=ALU.add),
                    waits=[("c_dve", cv)], inc="c_dve")
            conv_done = cv
            if cb == 64:
                silu_done = k.emit("scalar", lambda e: e.activation(
                    out=B_bf[:], in_=convo[:, 0:T], func=AF.Silu),
                    waits=[("c_dve", cv)], inc="c_act")
            elif cb == 65:
                silu_done = k.emit("scalar", lambda e: e.activation(
                    out=C_bf[:], in_=convo[:, 0:T], func=AF.Silu),
                    waits=[("c_dve", cv)], inc="c_act")
            else:
                silu_done = k.emit("scalar", lambda e: e.activation(
                    out=sil_bf[:], in_=convo[:, 0:T], func=AF.Silu),
                    waits=[("c_dve", cv), ("c_pe", k.val("c_pe"))], inc="c_act")
                emit_tm_scratch(silu_done, xs_dram, (cb - 32) * 128)
        else:
            for g in range(4):
                ws = [("c_pe", grp_tok[g])]
                if g == 0:
                    ws.append(("c_pe", k.val("c_pe")))
                bank_evac[g] = k.emit("scalar", lambda e, g=g, cb=cb: e.activation(
                    out=sil_bf[:, g * 512:(g + 1) * 512], in_=psb[g][:, 0:512],
                    func=AF.Silu, scale=mup_s[:, cb:cb + 1]),
                    waits=ws, inc="c_act")
            emit_tm_scratch(bank_evac[3], sz_dram, cb * 128)

    k.barrier()
    if STAGES <= 2:
        return finish_early()

    k.reset(SBASE)
    xs_bf = [k.sb(f"xsbf{i}", [128, H], BF16) for i in range(2)]
    state_bf = k.sb("state_bf", [128, H], BF16)
    lrep = [k.sb(f"lrep{i}", [128, 4096], F32) for i in range(2)]
    Dbuf = k.sb("Dbuf", [128, 4096], F32)
    Wbuf = k.sb("Wbuf", [128, 4096], F32)
    Pbuf = k.sb("Pbuf", [128, 4096], BF16)
    ybuf = k.sb("ybuf", [128, H], F32)
    yD = k.sb("yD", [128, H], F32)
    Dbcast = k.sb("Dbcast", [128, 4096], F32)
    Gtm = k.sb("Gtm", [128, 128], F32)
    B_tm = k.sb("B_tm", [128, 128], BF16)
    ssc = k.sb("ssc", [128, 2048], BF16)
    Xdd = k.sb("Xdd", [128, 2048], BF16)
    P2_END = k.mark()
    H_bf = k.sb("H_bf", [128, H], BF16, at=P2_END)
    assert P2_END + H * 2 <= k.SBLIM

    k.dma("scalar", Dbcast[:], drep_i.partition_broadcast(128), inc="d_cst")
    DB = k.val("d_cst")
    k.emit("vector", lambda e: e.memset(state_bf[:], 0.0), inc="c_dve")

    lam_consumed = {}
    xs_consumed = {}
    y_dma_tok = 0
    yasm_done = 0
    bank_free = {0: ("c_dve", 0), 2: ("c_dve", 0), 4: ("c_dve", 0)}
    for j in range(NCH):
        xsb = xs_bf[j % 2]
        if os.environ.get("MK_XSYNC") == "1":
            xw = [("c_pe", k.val("c_pe")), ("c_dve", k.val("c_dve")),
                  ("c_act", k.val("c_act")), ("c_pool", k.val("c_pool"))]
        else:
            xw = [("c_dve", k.val("c_dve")), ("c_pool", k.val("c_pool"))] if j < 2 else \
                 [("c_pe", xs_consumed[j - 2]), ("c_dve", lam_consumed[j - 2]),
                  ("c_pool", k.val("c_pool"))]
        xld = k.dma("sync", xsb[:], xs_dram[j], waits=xw, inc="d_xs")
        lld = []
        for hf in range(2):
            lws = [] if j == 0 else [("c_dve", lam_consumed.get(j - 1, 0))]
            lld.append(k.dma("sync", lrep[hf][:].rearrange("p (h s) -> p h s", h=32),
                             lam_dram[j, 32 * hf:32 * (hf + 1), :].unsqueeze(0).partition_broadcast(128),
                             waits=lws if hf == 0 else (), inc="d_lrep"))
        gt = k.emit("tensor", lambda e, j=j: e.matmul(
            out=psb[6][:, 0:128], lhsT=B_bf[:, j * 128:(j + 1) * 128],
            rhs=C_bf[:, j * 128:(j + 1) * 128], start=True, stop=True),
            waits=[("c_dve", k.val("c_dve"))], inc="c_pe")
        gtm = k.emit("vector", lambda e: e.tensor_tensor(out=Gtm[:], in0=psb[6][:, 0:128],
                                                         in1=maskbin[:], op=ALU.mult),
                     waits=[("c_pe", gt)], inc="c_dve")
        btr = k.emit("tensor", lambda e, j=j: e.transpose(
            out=pbf(6)[:, 512:640], in_=B_bf[:, j * 128:(j + 1) * 128], identity=ident_bf[:]),
            inc="c_pe")
        bev = k.emit("vector", lambda e: e.tensor_copy(out=B_tm[:], in_=pbf(6)[:, 512:640]),
                     waits=[("c_pe", btr)], inc="c_dve")
        if os.environ.get("MK_YD_DVE") == "1":
            ydt = k.emit("vector", lambda e, xsb=xsb: e.tensor_tensor(out=yD[:], in0=xsb[:],
                                                                       in1=Dbcast[:], op=ALU.mult),
                         waits=[("d_xs", xld), ("d_cst", DB)], inc="c_dve")
            yd_sem = "c_dve"
        else:
            ydt = k.emit("gpsimd", lambda e, xsb=xsb: e.tensor_tensor(out=yD[:], in0=xsb[:],
                                                                       in1=Dbcast[:], op=ALU.mult),
                         waits=[("d_xs", xld), ("d_cst", DB), ("c_dve", yasm_done)], inc="c_pool")
            yd_sem = "c_pool"

        for hf in range(2):
            hbase = 32 * hf
            dtok = None
            for hh in range(32):
                h = hbase + hh
                ws = []
                if hh == 0:
                    ws = [("d_lrep", lld[hf]), ("c_act", k.val("c_act"))]
                dtok = k.emit("vector", lambda e, hh=hh, h=h, hf=hf, j=j: e.tensor_scalar(
                    out=Dbuf[:, hh * 128:(hh + 1) * 128],
                    in0=lrep[hf][:, hh * 128:(hh + 1) * 128],
                    scalar1=lsd_tmA[:, j * 64 + h: j * 64 + h + 1], scalar2=88.0,
                    op0=ALU.subtract, op1=ALU.min),
                    waits=ws, inc="c_dve" if hh == 31 else None)
            if hf == 1:
                lam_consumed[j] = dtok
                if SCAN_DEBUG and j == 2:
                    k.dma("scalar", scan_taps["snap_D"], Dbuf[:], waits=[("c_dve", dtok)], inc="d_scrw")
                    k.dma("scalar", scan_taps["snap_lr1b"], lrep[1][:], waits=[("c_dve", dtok)], inc="d_scrw")
            wt = k.emit("scalar", lambda e: e.activation(out=Wbuf[:], in_=Dbuf[:], func=AF.Exp),
                        waits=[("c_dve", dtok), ("c_dve", yasm_done)], inc="c_act")
            pt = k.emit("vector", lambda e: e.tensor_tensor(
                out=Pbuf[:].rearrange("p (h t) -> p h t", h=32),
                in0=Wbuf[:].rearrange("p (h t) -> p h t", h=32),
                in1=Gtm[:].unsqueeze(1).broadcast_to([128, 32, 128]),
                op=ALU.mult),
                waits=[("c_act", wt), ("c_dve", gtm), ("c_pe", k.val("c_pe"))], inc="c_dve")
            xdt = None
            for hh in range(32):
                h = hbase + hh
                xdt = k.emit("vector", lambda e, hh=hh, h=h, xsb=xsb: e.tensor_scalar(
                    out=Xdd[:, hh * 64:(hh + 1) * 64], in0=xsb[:, h * 64:(h + 1) * 64],
                    scalar1=Wbuf[:, hh * 128 + 127: hh * 128 + 128], scalar2=None, op0=ALU.mult),
                    waits=[("c_act", wt), ("d_xs", xld), ("c_pe", k.val("c_pe"))] if hh == 0 else (),
                    inc="c_dve" if hh == 31 else None)
            sst = None
            for hh in range(32):
                h = hbase + hh
                sst = k.emit("vector", lambda e, hh=hh, h=h, j=j: e.tensor_scalar(
                    out=ssc[:, hh * 64:(hh + 1) * 64], in0=state_bf[:, h * 64:(h + 1) * 64],
                    scalar1=edecQA[:, j * 64 + h: j * 64 + h + 1], scalar2=None, op0=ALU.mult),
                    waits=[("c_pe", k.val("c_pe"))] if hh == 0 else (),
                    inc="c_dve" if hh == 31 else None)
            for sg in range(2):
                sgb = hbase + sg * 16
                y1t = y2t = sut = None
                for hh in range(16):
                    lhh = sg * 16 + hh
                    h = hbase + lhh
                    bank = hh // 8
                    slot = (hh % 8) * 64
                    ws = ([("c_dve", pt), bank_free[0], ("d_xs", xld)] if hh == 0 else ())
                    y1t = k.emit("tensor", lambda e, lhh=lhh, h=h, bank=bank, slot=slot, xsb=xsb: e.matmul(
                        out=psb[bank][:, slot:slot + 64],
                        lhsT=Pbuf[:, lhh * 128:(lhh + 1) * 128],
                        rhs=xsb[:, h * 64:(h + 1) * 64], start=True, stop=True),
                        waits=ws, inc="c_pe" if hh == 15 else None)
                for hh in range(16):
                    h = hbase + sg * 16 + hh
                    bank = 2 + hh // 8
                    slot = (hh % 8) * 64
                    ws = ([bank_free[2], ("c_dve", sst)] if hh == 0 else ())
                    y2t = k.emit("tensor", lambda e, h=h, bank=bank, slot=slot, j=j: e.matmul(
                        out=psb[bank][:, slot:slot + 64],
                        lhsT=C_bf[:, j * 128:(j + 1) * 128],
                        rhs=state_bf[:, h * 64:(h + 1) * 64], start=True, stop=True),
                        waits=ws, inc="c_pe" if hh == 15 else None)
                for hh in range(16):
                    lhh = sg * 16 + hh
                    h = hbase + lhh
                    bank = 4 + hh // 8
                    slot = (hh % 8) * 64
                    ws = []
                    if hh == 0:
                        ws = [("c_dve", xdt), ("c_dve", sst), ("c_dve", bev), bank_free[4]]
                    k.emit("tensor", lambda e, lhh=lhh, bank=bank, slot=slot: e.matmul(
                        out=psb[bank][:, slot:slot + 64], lhsT=B_tm[:],
                        rhs=Xdd[:, lhh * 64:(lhh + 1) * 64], start=True, stop=False),
                        waits=ws, inc=None)
                    sut = k.emit("tensor", lambda e, lhh=lhh, bank=bank, slot=slot: e.matmul(
                        out=psb[bank][:, slot:slot + 64], lhsT=ident_bf[:],
                        rhs=ssc[:, lhh * 64:(lhh + 1) * 64], start=False, stop=True),
                        inc="c_pe" if hh == 15 else None)
                pa = None
                for hh in range(16):
                    lhh = sg * 16 + hh
                    h = hbase + lhh
                    ws = [("c_pe", y2t), (yd_sem, ydt)] if hh == 0 else ()
                    if NO_Y2:
                        pa = k.emit("vector", lambda e, lhh=lhh, h=h: e.scalar_tensor_tensor(
                            out=Wbuf[:, lhh * 64:(lhh + 1) * 64],
                            in0=psb[2 + (lhh % 16) // 8][:, ((lhh % 16) % 8) * 64:((lhh % 16) % 8) * 64 + 64],
                            scalar=0.0,
                            in1=yD[:, h * 64:(h + 1) * 64], op0=ALU.mult, op1=ALU.add),
                            waits=ws, inc="c_dve" if hh == 15 else None)
                    else:
                        pa = k.emit("vector", lambda e, lhh=lhh, h=h, j=j: e.scalar_tensor_tensor(
                            out=Wbuf[:, lhh * 64:(lhh + 1) * 64],
                            in0=psb[2 + (lhh % 16) // 8][:, ((lhh % 16) % 8) * 64:((lhh % 16) % 8) * 64 + 64],
                            scalar=edec_tmA[:, j * 64 + h: j * 64 + h + 1],
                            in1=yD[:, h * 64:(h + 1) * 64], op0=ALU.mult, op1=ALU.add),
                            waits=ws, inc="c_dve" if hh == 15 else None)
                bank_free[2] = ("c_dve", pa)
                pb_ = None
                for hh in range(16):
                    lhh = sg * 16 + hh
                    h = hbase + lhh
                    ws = [("c_dve", pa), ("c_pe", y1t), ("d_scrw", y_dma_tok)] if hh == 0 else ()
                    if NO_Y1:
                        pb_ = k.emit("vector", lambda e, lhh=lhh, h=h: e.tensor_copy(
                            out=ybuf[:, h * 64:(h + 1) * 64],
                            in_=Wbuf[:, lhh * 64:(lhh + 1) * 64]),
                            waits=ws, inc="c_dve" if hh == 15 else None)
                    else:
                        pb_ = k.emit("vector", lambda e, lhh=lhh, h=h: e.tensor_tensor(
                            out=ybuf[:, h * 64:(h + 1) * 64],
                            in0=Wbuf[:, lhh * 64:(lhh + 1) * 64],
                            in1=psb[(lhh % 16) // 8][:, ((lhh % 16) % 8) * 64:((lhh % 16) % 8) * 64 + 64],
                            op=ALU.add),
                            waits=ws, inc="c_dve" if hh == 15 else None)
                bank_free[0] = ("c_dve", pb_)
                se = None
                for hh in range(16):
                    lhh = sg * 16 + hh
                    h = hbase + lhh
                    ws = [("c_pe", sut)] if hh == 0 else ()
                    se = k.emit("vector", lambda e, lhh=lhh, h=h: e.tensor_copy(
                        out=state_bf[:, h * 64:(h + 1) * 64],
                        in_=psb[4 + (lhh % 16) // 8][:, ((lhh % 16) % 8) * 64:((lhh % 16) % 8) * 64 + 64]),
                        waits=ws, inc="c_dve" if hh == 15 else None)
                bank_free[4] = ("c_dve", se)
                yasm_done = pb_
        xs_consumed[j] = k.val("c_pe")
        ydbg = k.emit("vector", lambda e: e.engine_nop(), waits=[("c_dve", yasm_done)], inc="c_ydbg")
        if SCAN_DEBUG and j == 2:
            k.dma("scalar", scan_taps["snap_xs"], xs_bf[0][:],
                  waits=[("c_dve", k.val("c_dve")), ("c_pe", k.val("c_pe"))], inc="d_scrw")
            k.dma("scalar", scan_taps["snap_y"], ybuf[:], waits=[("c_dve", yasm_done)], inc="d_scrw")
            k.dma("scalar", scan_taps["snap_P"], Pbuf[:], inc="d_scrw")
            k.dma("scalar", scan_taps["snap_Xdd"], Xdd[:], inc="d_scrw")
            k.dma("scalar", scan_taps["snap_ssc"], ssc[:], inc="d_scrw")
            k.dma("scalar", scan_taps["snap_W"], Wbuf[:], inc="d_scrw")
            k.dma("scalar", scan_taps["snap_lr0"], lrep[0][:], inc="d_scrw")
            k.dma("scalar", scan_taps["snap_lr1"], lrep[1][:], inc="d_scrw")
            k.dma("scalar", scan_taps["snap_lsd"], lsd_tmA[:], inc="d_scrw")
            k.dma("scalar", scan_taps["snap_edq"], edecQA[:], inc="d_scrw")
            k.dma("scalar", scan_taps["snap_edt"], edec_tmA[:], inc="d_scrw")
        y_dma_tok = k.dma("scalar", y_dram[j], ybuf[:],
                          waits=[("c_dve", yasm_done), ("c_ydbg", ydbg)], inc="d_scrw")

    if STAGES <= 3:
        return finish_early()
    cw = k.dma("sync", cc_in, state_bf[:], waits=[("c_dve", k.val("c_dve"))], inc="d_cin")
    k.emit("gpsimd", lambda e: e.memset(H_bf[:], 0.0), waits=[("c_pe", k.val("c_pe"))], inc="c_pool")
    k.wait("gpsimd", [("d_cin", cw)])
    k.emit("gpsimd", lambda e: e.collective_compute(
        "AllGather", ALU.bypass, replica_groups=[list(range(NCORES))],
        ins=[cc_in.opt()], outs=[cc_out.opt()]), inc="ccs")
    k.wait("gpsimd", [("ccs", 1)])
    gat = k.emit("gpsimd", lambda e: e.indirect_dma_start(
        out=H_bf[:], out_offset=None, in_=cc_out[0:128, :],
        in_offset=IndirectOffsetOnAxis(ap=hidx_s[:, 0:1], axis=0),
        bounds_check=128 * NCORES - 1, oob_is_err=False),
        waits=[("d_cst", CST)], inc="d_cin", inc_n=16)
    k.barrier()

    if STAGES <= 4:
        return finish_early()

    k.reset(SBASE)
    y3_fm = k.sb("y3_fm", [128, 32 * 1024], BF16)
    Y3END = k.mark()
    ytile = [k.sb(f"ytile{i}", [128, H], F32) for i in range(2)]
    sztile = [k.sb(f"sztile{i}", [128, H], BF16) for i in range(2)]
    ycbuf = k.sb("ycbuf", [128, H], F32)
    gbuf = k.sb("gbuf", [128, H], F32)
    y3tm = k.sb("y3tm", [128, H], BF16)
    expl = k.sb("expl", [128, 64], F32)
    P2B_END = k.mark()
    assert P2B_END <= P2_END

    SC2 = k.val("d_scrw")

    def p2b_half(hf, y_w_total):
        toks = {"y3": []}
        for jj in range(8):
            j = hf * 8 + jj
            yt = ytile[jj % 2]
            szt = sztile[jj % 2]
            ws = [("c_act", k.val("c_act")), ("c_dve", k.val("c_dve"))] if jj < 2 else \
                 [("c_act", toks[f"ssq{jj-2}"]), ("c_dve", toks[f"g{jj-2}"])]
            yld = k.dma("sync", yt[:], y_dram[j], waits=ws + [("d_scrw", y_w_total)], inc="d_y")
            szld = k.dma("gpsimd", szt[:], sz_dram[j],
                         waits=[("c_dve", toks[f"g{jj-2}"])] if jj >= 2 else [], inc="d_sz")
            ex = k.emit("scalar", lambda e, j=j: e.activation(
                out=expl[:], in_=lam_tmA[:, j * 64:(j + 1) * 64], func=AF.Exp),
                waits=[("c_dve", k.val("c_dve"))], inc="c_act")
            yct = None
            for half2 in range(2):
                mmt = None
                for hh in range(32):
                    h = half2 * 32 + hh
                    bank = hh // 8 % 4
                    slot = (hh % 8) * 64
                    ws2 = []
                    if hh == 0:
                        ws2 = [("d_cin", k.val("d_cin")), ("c_dve", yct or 0),
                               ("c_act", k.val("c_act"))]
                    mmt = k.emit("tensor", lambda e, h=h, bank=bank, slot=slot, j=j: e.matmul(
                        out=psb[bank][:, slot:slot + 64],
                        lhsT=C_bf[:, j * 128:(j + 1) * 128],
                        rhs=H_bf[:, h * 64:(h + 1) * 64], start=True, stop=True),
                        waits=ws2, inc="c_pe" if hh == 31 else None)
                for hh in range(32):
                    h = half2 * 32 + hh
                    bank = hh // 8 % 4
                    slot = (hh % 8) * 64
                    ws2 = [("c_pe", mmt), ("d_y", yld), ("c_act", ex)] if hh == 0 else ()
                    yct = k.emit("vector", lambda e, h=h, bank=bank, slot=slot, yt=yt: e.scalar_tensor_tensor(
                        out=ycbuf[:, h * 64:(h + 1) * 64], in0=psb[bank][:, slot:slot + 64],
                        scalar=expl[:, h:h + 1], in1=yt[:, h * 64:(h + 1) * 64],
                        op0=ALU.mult, op1=ALU.add),
                        waits=ws2, inc="c_dve" if hh == 31 else None)
            gt_ = k.emit("vector", lambda e, szt=szt: e.tensor_tensor(out=gbuf[:], in0=ycbuf[:],
                                                                       in1=szt[:], op=ALU.mult),
                         waits=[("c_dve", yct), ("d_sz", szld)], inc="c_dve")
            toks[f"g{jj}"] = gt_
            sq = k.emit("scalar", lambda e, yt=yt: e.activation(out=yt[:], in_=gbuf[:], func=AF.Square,
                                                                accum_out=stat[:, 1:2]),
                        waits=[("c_dve", gt_)], inc="c_act")
            toks[f"ssq{jj}"] = sq
            l1 = k.emit("scalar", lambda e: e.activation(out=stat[:, 8:9], in_=stat[:, 1:2],
                                                         func=AF.Ln, scale=np.float32(1.0 / H), bias=np.float32(EPS)),
                        waits=[("c_act", sq)], inc="c_act")
            r1 = k.emit("scalar", lambda e: e.activation(out=stat[:, 9:10], in_=stat[:, 8:9],
                                                         func=AF.Exp, scale=np.float32(-0.5)),
                        waits=[("c_act", l1)], inc="c_act")
            ap2 = k.emit("scalar", lambda e: e.activation(out=y3tm[:], in_=gbuf[:], func=AF.Identity,
                                                          scale=stat[:, 9:10]),
                         waits=[("c_act", r1)], inc="c_act")
            for grp in range(4):
                bank = 4 + grp % 2
                trt = None
                for kk in range(8):
                    dsb = grp * 8 + kk
                    trt = k.emit("tensor", lambda e, dsb=dsb, bank=bank, kk=kk: e.transpose(
                        out=pbf(bank)[:, kk * 128:(kk + 1) * 128],
                        in_=y3tm[:, dsb * 128:(dsb + 1) * 128], identity=ident_bf[:]),
                        waits=([("c_act", ap2), ("c_dve", k.val("c_dve"))] if kk == 0 else ()),
                        inc="c_pe" if kk == 7 else None)
                evt = None
                for kk in range(8):
                    dsb = grp * 8 + kk
                    evt = k.emit("vector", lambda e, dsb=dsb, bank=bank, kk=kk, jj=jj: e.tensor_scalar(
                        out=y3_fm[:, dsb * 1024 + jj * 128: dsb * 1024 + (jj + 1) * 128],
                        in0=pbf(bank)[:, kk * 128:(kk + 1) * 128],
                        scalar1=nrmw_s[:, dsb:dsb + 1], scalar2=None, op0=ALU.mult),
                        waits=[("c_pe", trt)] if kk == 0 else (),
                        inc="c_dve" if kk == 7 else None)
                toks["y3"].append(evt)
        return toks["y3"][-1]

    def p3_half(hf, wop_bf, atstg):
        ev_prev = {0: 0, 1: 0, 2: 0, 3: 0}
        wld_prev = None
        for oc in range(8):
            wb = wop_bf[oc % 2]
            ws = [("c_act", k.val("c_act"))] if oc < 2 else [("c_pe", wcons[oc - 2])]
            wld = k.dma("gpsimd", wb[:], w_op[oc].rearrange("p b c -> p (b c)"), waits=ws, inc="d_w")
            if oc == 0:
                wcons = {}
            for tt in range(8):
                bank = tt % 4
                mmt = None
                for dsb in range(32):
                    ws2 = []
                    if dsb == 0:
                        ws2 = [("c_act", ev_prev[bank])]
                        if tt == 0:
                            ws2.append(("d_w", wld))
                        if tt == 0 and oc == 0:
                            ws2.append(("c_dve", k.val("c_dve")))
                    mmt = k.emit("tensor", lambda e, dsb=dsb, tt=tt, wb=wb, bank=bank: e.matmul(
                        out=psb[bank][:, 0:512],
                        lhsT=y3_fm[:, dsb * 1024 + tt * 128:dsb * 1024 + (tt + 1) * 128],
                        rhs=wb[:, dsb * 512:(dsb + 1) * 512],
                        start=(dsb == 0), stop=(dsb == 31)),
                        waits=ws2, inc="c_pe" if dsb == 31 else None)
                if tt == 6:
                    wcons[oc] = mmt
                ev = k.emit("scalar", lambda e, bank=bank, tt=tt: e.activation(
                    out=atstg[tt % 2][:], in_=psb[bank][:, 0:512], func=AF.Identity),
                    waits=[("c_pe", mmt), ("d_scrw", at_w.get((tt - 2) % 8, 0))], inc="c_act")
                ev_prev[bank] = ev
                if os.environ.get("MK_P3_NOSTORE") == "1":
                    at_w[tt] = k.val("d_scrw")
                elif os.environ.get("MK_P3_DBGSTORE") == "1":
                    at_w[tt] = k.dma("scalar", dbg_sink, atstg[tt % 2][:],
                                     waits=[("c_act", ev)], inc="d_scrw")
                else:
                    at_w[tt] = k.dma("scalar", at_dram[oc, hf * 8 + tt],
                                     atstg[tt % 2][:], waits=[("c_act", ev)], inc="d_scrw")
            wcons[oc] = mmt
        return k.val("d_scrw")

    y_total_after_p2 = y_dma_tok
    at_w = {}
    p2b_done0 = p2b_half(0, y_total_after_p2)
    k.barrier()
    k.reset(Y3END)
    wop_bf = [k.sb(f"wop{i}", [128, 32 * 512], BF16) for i in range(2)]
    atstg = [k.sb(f"atstg{i}", [128, 512], F32, at=224256 + 2048 * i) for i in range(2)]
    assert k.mark() <= P2_END
    if os.environ.get("MK_ONLY_P2B") != "1":
        p3_half(0, wop_bf, atstg)
    k.barrier()
    p2b_done1 = p2b_half(1, y_total_after_p2)
    k.barrier()
    if os.environ.get("MK_ONLY_P2B") != "1":
        p3_half(1, wop_bf, atstg)
    k.barrier()

    if STAGES <= 5:
        return finish_early()

    k.reset(SBASE)
    x3_fm = k.sb("x3_fm", [128, 32 * 1024], BF16)
    zT_pad = k.sb("zT_pad", [128, 1024], BF16)
    loraB_pad = k.sb("loraB_pad", [128, 4096], BF16)
    X3END = k.mark()
    attile = [k.sb(f"attile{i}", [128, H], F32) for i in range(2)]
    xintile = [k.sb(f"xint{i}", [128, H], F32) for i in range(2)]
    abuf = k.sb("abuf", [128, H], F32)
    xmbuf = k.sb("xmbuf", [128, H], F32)
    x3tm = k.sb("x3tm", [128, H], BF16)
    logi = k.sb("logi", [128, 4], F32)
    mx = k.sb("mx", [128, 1], F32)
    oh = k.sb("oh", [128, 4], F32)
    zmk = k.sb("zmk", [128, 32], BF16)
    P4_END = k.mark()
    assert P4_END <= k.SBLIM

    lb1 = k.dma("gpsimd", loraB_pad[0:32, :], loraB_t, inc="d_w")
    k.wait("vector", [("c_pe", k.val("c_pe"))])
    k.emit("vector", lambda e: e.memset(loraB_pad[32:64, :], 0.0), inc="c_dve")
    k.emit("vector", lambda e: e.memset(loraB_pad[64:128, :], 0.0), inc="c_dve")
    k.emit("vector", lambda e: e.memset(zT_pad[32:64, :], 0.0), inc="c_dve")
    k.emit("vector", lambda e: e.memset(zT_pad[64:128, :], 0.0), inc="c_dve")

    def p4_tile(hf, jj):
        j = hf * 8 + jj
        att = attile[jj % 2]
        xit = xintile[jj % 2]
        ws = [("c_act", k.val("c_act")), ("c_dve", k.val("c_dve"))] if jj < 2 else \
             [("c_act", p4toks[f"x3{jj-2}"]), ("c_dve", p4toks[f"xm{jj-2}"])]
        for oc in range(8):
            ald = k.dma("sync", att[:, oc * 512:(oc + 1) * 512], at_dram[oc, j],
                        waits=(ws + [("d_scrw", k.val("d_scrw"))]) if oc == 0 else (),
                        inc="d_at")
        xld = k.dma("sync", xit[:], xin.rearrange("(n p) h -> n p h", p=128)[j],
                    waits=[("c_dve", p4toks[f"xm{jj-2}"])] if jj >= 2 else [], inc="d_xin")
        st1 = ln_stats(att[:], abuf[:], False,
                       extra_waits=[("d_at", ald), ("c_act", k.val("c_act"))])
        a1 = k.emit("scalar", lambda e, att=att: e.activation(
            out=abuf[:], in_=att[:], func=AF.Identity, scale=stat[:, 10:11], bias=stat[:, 11:12]),
            waits=[("c_dve", st1)], inc="c_act")
        xm1 = k.emit("vector", lambda e, xit=xit: e.tensor_tensor(out=xmbuf[:], in0=abuf[:],
                                                                  in1=xit[:], op=ALU.add),
                     waits=[("c_act", a1), ("d_xin", xld)], inc="c_dve")
        p4toks[f"xm{jj}"] = xm1
        if os.environ.get("MK_NO_XMW") == "1":
            xmw = k.val("d_scrw")
        else:
            xmw = k.dma("scalar", xm_dram[j], xmbuf[:], waits=[("c_dve", xm1)], inc="d_scrw")
        st2 = ln_stats(xmbuf[:], abuf[:], False, extra_waits=[("c_dve", xm1), ("c_act", a1)])
        a2 = k.emit("scalar", lambda e: e.activation(
            out=x3tm[:], in_=xmbuf[:], func=AF.Identity, scale=stat[:, 10:11], bias=stat[:, 11:12]),
            waits=[("c_dve", st2)], inc="c_act")
        p4toks[f"x3{jj}"] = a2
        for grp in range(4):
            bank = 4 + grp % 2
            trt = None
            for kk in range(8):
                hb = grp * 8 + kk
                trt = k.emit("tensor", lambda e, hb=hb, bank=bank, kk=kk: e.transpose(
                    out=pbf(bank)[:, kk * 128:(kk + 1) * 128],
                    in_=x3tm[:, hb * 128:(hb + 1) * 128], identity=ident_bf[:]),
                    waits=([("c_act", a2), ("c_dve", k.val("c_dve"))] if kk == 0 else ()),
                    inc="c_pe" if kk == 7 else None)
            for kk in range(8):
                hb = grp * 8 + kk
                k.emit("vector", lambda e, hb=hb, bank=bank, kk=kk, jj=jj: e.tensor_copy(
                    out=x3_fm[:, hb * 1024 + jj * 128: hb * 1024 + (jj + 1) * 128],
                    in_=pbf(bank)[:, kk * 128:(kk + 1) * 128]),
                    waits=[("c_pe", trt)] if kk == 0 else (),
                    inc="c_dve" if kk == 7 else None)
        x3done = k.val("c_dve")
        if os.environ.get("MK_P4_MIN") == "1":
            return k.val("c_dve")
        gmt = None
        for hb in range(32):
            ws2 = [("c_dve", x3done), ("c_act", k.val("c_act"))] if hb == 0 else ()
            gmt = k.emit("tensor", lambda e, hb=hb, jj=jj: e.matmul(
                out=psb[6][:, 0:4],
                lhsT=x3_fm[:, hb * 1024 + jj * 128: hb * 1024 + (jj + 1) * 128],
                rhs=gate_bf[:, hb * 4:(hb + 1) * 4],
                start=(hb == 0), stop=(hb == 31)),
                waits=ws2, inc="c_pe" if hb == 31 else None)
        lg = k.emit("vector", lambda e: e.tensor_tensor(out=logi[:], in0=psb[6][:, 0:4],
                                                        in1=ebias_r[:], op=ALU.add),
                    waits=[("c_pe", gmt)], inc="c_dve")
        mxt = k.emit("vector", lambda e: e.reduce_max(out=mx[:], in_=logi[:], axis=AX.X),
                     waits=[("c_dve", lg)], inc="c_dve")
        k.wait("vector", [("c_dve", mxt)])
        oht = k.emit("vector", lambda e: e.tensor_scalar(out=oh[:], in0=logi[:], scalar1=mx[:],
                                                         scalar2=None, op0=ALU.is_ge),
                     waits=[("c_dve", mxt)], inc="c_dve")
        zmt = None
        for hb in range(32):
            ws2 = [("c_dve", k.val("c_dve"))] if hb == 0 else ()
            zmt = k.emit("tensor", lambda e, hb=hb, jj=jj: e.matmul(
                out=psb[7][:, 0:32],
                lhsT=x3_fm[:, hb * 1024 + jj * 128: hb * 1024 + (jj + 1) * 128],
                rhs=loraA_bf[:, hb * 32:(hb + 1) * 32],
                start=(hb == 0), stop=(hb == 31)),
                waits=ws2, inc="c_pe" if hb == 31 else None)
        zmtk = None
        for ee in range(4):
            ws2 = [("c_pe", zmt), ("c_dve", oht)] if ee == 0 else ()
            zmtk = k.emit("vector", lambda e, ee=ee: e.tensor_scalar(
                out=zmk[:, ee * 8:(ee + 1) * 8], in0=psb[7][:, ee * 8:(ee + 1) * 8],
                scalar1=oh[:, ee:ee + 1], scalar2=None, op0=ALU.mult),
                waits=ws2, inc="c_dve" if ee == 3 else None)
        ztt = k.emit("tensor", lambda e: e.transpose(
            out=pbf(6)[0:32, 512:640], in_=zmk[:], identity=ident_bf[:]),
            waits=[("c_dve", zmtk)], inc="c_pe")
        zev = k.emit("vector", lambda e, jj=jj: e.tensor_copy(
            out=zT_pad[0:32, jj * 128:(jj + 1) * 128], in_=pbf(6)[0:32, 512:640]),
            waits=[("c_pe", ztt)], inc="c_dve")
        return zev

    def p5_half(hf, wup_bf, ffstg):
        ev_prev = {0: 0, 1: 0, 2: 0, 3: 0}
        for oc in range(8):
            wb = wup_bf[oc % 2]
            ws = [("c_act", k.val("c_act"))] if oc < 2 else [("c_pe", wcons5[oc - 2])]
            wld = k.dma("gpsimd", wb[:], w_up[oc].rearrange("p b c -> p (b c)"), waits=ws, inc="d_w")
            if oc == 0:
                wcons5 = {}
            for tt in range(8):
                bank = tt % 4
                mmt = None
                for dsb in range(32):
                    ws2 = []
                    if dsb == 0:
                        ws2 = [("c_act", ev_prev[bank])]
                        if tt == 0:
                            ws2.append(("d_w", wld))
                        if tt == 0 and oc == 0:
                            ws2.append(("c_dve", k.val("c_dve")))
                            ws2.append(("d_w", lb1))
                    mmt = k.emit("tensor", lambda e, dsb=dsb, tt=tt, wb=wb, bank=bank: e.matmul(
                        out=psb[bank][:, 0:512],
                        lhsT=x3_fm[:, dsb * 1024 + tt * 128:dsb * 1024 + (tt + 1) * 128],
                        rhs=wb[:, dsb * 512:(dsb + 1) * 512],
                        start=(dsb == 0), stop=False),
                        waits=ws2, inc=None)
                mmt = k.emit("tensor", lambda e, tt=tt, bank=bank, oc=oc: e.matmul(
                    out=psb[bank][:, 0:512],
                    lhsT=zT_pad[:, tt * 128:(tt + 1) * 128],
                    rhs=loraB_pad[:, oc * 512:(oc + 1) * 512],
                    start=False, stop=True),
                    inc="c_pe")
                ev = k.emit("scalar", lambda e, bank=bank, tt=tt: e.activation(
                    out=ffstg[tt % 2][:], in_=psb[bank][:, 0:512], func=AF.Identity),
                    waits=[("c_pe", mmt), ("d_scrw", ff_w.get((tt - 2) % 8, 0))], inc="c_act")
                ev_prev[bank] = ev
                if os.environ.get("MK_P5_NOSTORE") == "1":
                    ff_w[tt] = k.val("d_scrw")
                else:
                    ff_w[tt] = k.dma("scalar", ff_dram[oc, hf * 8 + tt],
                                     ffstg[tt % 2][:], waits=[("c_act", ev)], inc="d_scrw")
            wcons5[oc] = mmt

    def p6_tile(j, fftile, xmtile, obuf):
        fft = fftile[j % 2]
        xmt = xmtile[j % 2]
        ws = [("c_act", k.val("c_act")), ("d_out", k.val("d_out"))] if True else []
        for oc in range(8):
            fld = k.dma("sync", fft[:, oc * 512:(oc + 1) * 512], ff_dram[oc, j],
                        waits=(ws + [("d_scrw", k.val("d_scrw"))]) if oc == 0 else (),
                        inc="d_ff")
        xld = k.dma("sync", xmt[:], xm_dram[j], inc="d_xm")
        st1 = ln_stats(fft[:], obuf[:], False, extra_waits=[("d_ff", fld), ("c_act", k.val("c_act"))])
        a1 = k.emit("scalar", lambda e, fft=fft: e.activation(
            out=obuf[:], in_=fft[:], func=AF.Identity, scale=stat[:, 10:11], bias=stat[:, 11:12]),
            waits=[("c_dve", st1)], inc="c_act")
        o1 = k.emit("vector", lambda e, xmt=xmt, fft=fft: e.tensor_tensor(
            out=fft[:], in0=obuf[:], in1=xmt[:], op=ALU.add),
            waits=[("c_act", a1), ("d_xm", xld)], inc="c_dve")
        return k.dma("scalar", out_d.rearrange("(n p) h -> n p h", p=128)[j], fft[:],
                     waits=[("c_dve", o1)], inc="d_out")

    ff_w = {}
    k.reset(X3END)
    wup_bf = [k.sb(f"wup{i}", [128, 32 * 512], BF16) for i in range(2)]
    ffstg = [k.sb(f"ffstg{i}", [128, 512], F32, at=220000 + 2048 * i) for i in range(2)]
    k.reset(P4_END)
    for hf in range(2):
        p4toks = {}
        k.barrier()
        zl = None
        for jj in range(8):
            zl = p4_tile(hf, jj)
        k.barrier()
        if STAGES >= 7:
            p5_half(hf, wup_bf, ffstg)
            k.barrier()
    if STAGES <= 7:
        return finish_early()

    k.reset(X3END)
    fftile = [k.sb(f"fft{i}", [128, H], F32) for i in range(2)]
    xmtile = [k.sb(f"xmt{i}", [128, H], F32) for i in range(2)]
    obuf = k.sb("obuf", [128, H], F32)
    assert k.mark() <= k.SBLIM
    k.barrier()
    last_out = 0
    for j in range(NTT):
        last_out = p6_tile(j, fftile, xmtile, obuf)

    if DEBUG_TAPS:
        for tp, src in taps:
            k.dma("gpsimd", tp, src[:], waits=[("d_out", last_out)], inc="d_scrw")

    k.wait("gpsimd", [("d_out", last_out), ("d_scrw", k.val("d_scrw"))])
    k.wait("sync", [("d_out", last_out)])

    return k.finalize()


_NC_CACHE = None


def _get_nc():
    global _NC_CACHE
    if _NC_CACHE is None:
        _NC_CACHE = build_nc()
    return _NC_CACHE


def host_layout(inputs):
    f32 = np.float32
    _ = inputs.get("L0_attn_residual")
    hs = np.ascontiguousarray(inputs["L0_attn_hidden_states"], dtype=f32)
    Wip = np.ascontiguousarray(inputs["L0_attn_in_proj_w"], dtype=f32)
    Wop = np.ascontiguousarray(inputs["L0_attn_out_proj_w"], dtype=f32)
    Wup = np.ascontiguousarray(inputs["L0_ffn_W_up"], dtype=f32)
    gate = np.ascontiguousarray(inputs["L0_ffn_gate_w"], dtype=f32)
    loraA = np.ascontiguousarray(inputs["L0_ffn_lora_A"], dtype=f32)
    loraB = np.ascontiguousarray(inputs["L0_ffn_lora_B"], dtype=f32)
    convw = np.ascontiguousarray(inputs["L0_attn_conv_w"], dtype=f32)
    mup = np.ascontiguousarray(inputs["L0_attn_mup_vector"], dtype=f32)
    lnw = np.ascontiguousarray(inputs["L0_attn_ln_w"], dtype=f32)
    nrmw = np.ascontiguousarray(inputs["L0_attn_norm_w"], dtype=f32)
    dtb = np.ascontiguousarray(inputs["L0_attn_dt_bias"], dtype=f32)
    alog = np.ascontiguousarray(inputs["L0_attn_A_log"], dtype=f32)
    Dv = np.ascontiguousarray(inputs["L0_attn_D"], dtype=f32)
    ebias = np.ascontiguousarray(inputs["L0_ffn_expert_bias"], dtype=f32)

    wip_pad = np.zeros((4096, NCB * 128), f32)
    wip_pad[:, :8512] = Wip
    w_ip_t = np.ascontiguousarray(
        wip_pad.reshape(32, 128, NCB, 128).transpose(2, 1, 0, 3))
    w_op_t = np.ascontiguousarray(Wop.reshape(32, 128, 8, 512).transpose(2, 1, 0, 3))
    WupT = np.ascontiguousarray(Wup.T)
    w_up_t = np.ascontiguousarray(WupT.reshape(32, 128, 8, 512).transpose(2, 1, 0, 3))
    gate_tt = np.ascontiguousarray(gate.T.reshape(32, 128, 4).transpose(1, 0, 2))
    loraA_cat = np.ascontiguousarray(loraA.transpose(2, 0, 1).reshape(4096, 32))
    loraA_tt = np.ascontiguousarray(loraA_cat.reshape(32, 128, 32).transpose(1, 0, 2))
    loraB_tt = np.ascontiguousarray(loraB.transpose(0, 2, 1).reshape(32, 4096))
    conv_tt = np.ascontiguousarray(convw.reshape(34, 128, 4))
    mup_pad = np.zeros((NCB * 128,), f32)
    mup_pad[:8512] = mup
    mup_tt = np.ascontiguousarray(mup_pad.reshape(NCB, 128).T)
    lnw_tt = np.ascontiguousarray(lnw.reshape(32, 128).T)
    nrmw_tt = np.ascontiguousarray(nrmw.reshape(32, 128).T)
    drep = np.ascontiguousarray(np.repeat(Dv, 64).reshape(1, 4096))
    identb = np.eye(128, dtype=ml_dtypes.bfloat16)
    identf = np.eye(128, dtype=f32)
    maskb = np.tril(np.ones((128, 128), f32)).T
    maskb = np.ascontiguousarray(np.where(
        np.arange(128)[:, None] <= np.arange(128)[None, :], 1.0, 0.0).astype(f32))

    shared = {
        "w_ip": w_ip_t, "w_op": w_op_t, "w_up": w_up_t, "gate_t": gate_tt,
        "loraA_t": loraA_tt, "loraB_t": loraB_tt, "conv_t": conv_tt,
        "mup_t": mup_tt, "lnw_t": lnw_tt, "nrmw_t": nrmw_tt,
        "dtb": dtb.reshape(64, 1), "alog": alog.reshape(64, 1),
        "drep": drep, "ebias": ebias.reshape(1, 4),
        "identb": identb, "identf": identf, "maskbin": maskb,
    }

    in_maps = []
    for c in range(NCORES):
        bb, hf = c // 2, c % 2
        t0 = hf * T
        m = dict(shared)
        m["xin"] = np.ascontiguousarray(hs[bb, t0:t0 + T])
        xh = np.zeros((4, H), f32)
        if hf == 1:
            xh[0:3] = hs[bb, t0 - 3:t0]
        m["xhalo"] = xh
        if hf == 1:
            idx = ((c - 1) * 128 + np.arange(128, dtype=np.int32)).reshape(128, 1)
        else:
            idx = np.full((128, 1), 1 << 20, dtype=np.int32)
        m["hidx"] = idx
        in_maps.append(m)
    return in_maps


def kernel(**inputs):
    in_maps = host_layout(inputs)
    nc = _get_nc()
    res = run_bass_kernel_spmd(nc, in_maps, core_ids=list(range(NCORES)))
    out = np.zeros((4, 4096, 4096), np.float32)
    for c in range(NCORES):
        bb, hf = c // 2, c % 2
        out[bb, hf * T:(hf + 1) * T] = np.asarray(res.results[c]["out"])
    return out


if __name__ == "__main__":
    nc = build_nc()
    print("built ok")


# version 04 (final)
# speedup vs baseline: 1.0113x; 1.0071x over previous
import os
import numpy as np
import ml_dtypes
import concourse.bass as bass
import concourse.mybir as mybir
from concourse.bass_utils import run_bass_kernel_spmd
from concourse.bass_core_rust import IndirectOffsetOnAxis

F32 = mybir.dt.float32
BF16 = mybir.dt.bfloat16
I32 = mybir.dt.int32
AF = mybir.ActivationFunctionType
ALU = mybir.AluOpType
AX = mybir.AxisListType

NCORES = 8
T = 2048
NTT = 16
H = 4096
NCB = 67
NCH = 16
EPS = 1e-5
TP = T + 4

DEBUG_TAPS = bool(int(os.environ.get("MK_DEBUG_TAPS", "0")))
STAGES = int(os.environ.get("MK_STAGES", "99"))
SCAN_DEBUG = bool(int(os.environ.get("MK_SCAN_DEBUG", "0")))
NO_Y2 = os.environ.get("MK_NO_Y2") == "1"
NO_Y1 = os.environ.get("MK_NO_Y1") == "1"


class KB:
    ENGS = ("sync", "tensor", "vector", "scalar", "gpsimd")

    def __init__(self):
        self.nc = bass.Bass("TRN2", target_bir_lowering=False, debug=False)
        self.q = {e: [] for e in self.ENGS}
        self.semc = {}
        self.sem_order = []
        self._cursor = 16512
        self.SBLIM = 229344

    def sem(self, *names):
        for n in names:
            self.semc[n] = 0
            self.sem_order.append(n)

    def val(self, s):
        return self.semc[s]

    def emit(self, eng, fn, waits=(), inc=None, inc_n=1):
        post = None
        if inc is not None:
            self.semc[inc] += inc_n
            post = self.semc[inc]
        self.q[eng].append((tuple((s, v) for s, v in waits if v > 0), fn, inc, inc_n))
        return post

    def dma(self, eng, out, in_, waits=(), inc=None, **kw):
        return self.emit(eng, (lambda e, o=out, i=in_, kk=dict(kw): e.dma_start(out=o, in_=i, **kk)),
                         waits=waits, inc=inc, inc_n=16)

    def wait(self, eng, waits):
        self.emit(eng, None, waits=waits)

    def barrier(self):
        vals = [(s, self.semc[s]) for s in self.sem_order]
        for e in self.ENGS:
            self.wait(e, vals)

    def sb(self, name, shape, dtype, at=None):
        sz = int(np.prod(shape[1:])) * np.dtype(mybir.dt.np(dtype)).itemsize
        off = self._cursor if at is None else at
        off = (off + 63) & ~63
        end = off + sz
        assert end <= self.SBLIM, f"SBUF overflow at {name}: {off}+{sz}"
        if at is None:
            self._cursor = end
        return self.nc.alloc_sbuf_tensor_at(name, list(shape), dtype, offset=off)

    def mark(self):
        return self._cursor

    def reset(self, c):
        self._cursor = c

    def finalize(self):
        nc = self.nc
        from contextlib import ExitStack
        with ExitStack() as ctx:
            handles = {n: ctx.enter_context(nc.semaphore(n)) for n in self.sem_order}
            block = ctx.enter_context(nc.Block())
            for eng in self.ENGS:
                qeng = self.q[eng]

                def body(e, qeng=qeng):
                    for waits, fn, inc, inc_n in qeng:
                        for s, v in waits:
                            e.wait_ge(handles[s], v)
                        if fn is not None:
                            ins = fn(e)
                            if inc is not None:
                                ins.then_inc(handles[inc], inc_n)
                getattr(block, eng)(body)
        return nc


def build_nc():
    k = KB()
    nc = k.nc

    def din(name, shape, dt=F32):
        return nc.dram_tensor(name, list(shape), dt, kind="ExternalInput").ap()

    xin = din("xin", [T, H])
    xhalo = din("xhalo", [4, H])
    w_ip = din("w_ip", [NCB, 128, 32, 128])
    w_op = din("w_op", [8, 128, 32, 512])
    w_up = din("w_up", [8, 128, 32, 512])
    gate_t = din("gate_t", [128, 32, 4])
    loraA_t = din("loraA_t", [128, 32, 32])
    loraB_t = din("loraB_t", [32, 4096])
    conv_t = din("conv_t", [34, 128, 4])
    mup_t = din("mup_t", [128, 67])
    lnw_t = din("lnw_t", [128, 32])
    nrmw_t = din("nrmw_t", [128, 32])
    dtb_i = din("dtb", [64, 1])
    alog_i = din("alog", [64, 1])
    drep_i = din("drep", [1, 4096])
    ebias_i = din("ebias", [1, 4])
    identb_i = din("identb", [128, 128], BF16)
    identf_i = din("identf", [128, 128])
    maskbin_i = din("maskbin", [128, 128])
    hidx_i = din("hidx", [128, 1], I32)

    out_d = nc.dram_tensor("out", [T, H], F32, kind="ExternalOutput").ap()

    def dint(name, shape, dt=F32):
        return nc.dram_tensor(name, list(shape), dt, kind="Internal").ap()

    xs_dram = dint("xs_dram", [NTT, 128, H], BF16)
    sz_dram = dint("sz_dram", [NTT, 128, H], BF16)
    y_dram = dint("y_dram", [NTT, 128, H])
    at_dram = dint("at_dram", [8, NTT, 128, 512])
    xm_dram = dint("xm_dram", [NTT, 128, H])
    ff_dram = dint("ff_dram", [8, NTT, 128, 512])
    lam_dram = dint("lam_dram", [NCH, 64, 128])
    dbg_sink = dint("dbg_sink", [128, 512])
    cc_in = dint("cc_in", [128, H], BF16)
    cc_out = dint("cc_out", [128 * NCORES, H], BF16)

    taps = []
    scan_taps = {}
    if SCAN_DEBUG:
        for nm, shp, dt_ in [("snap_xs", [128, H], BF16), ("snap_y", [128, H], F32),
                             ("snap_P", [128, 4096], BF16), ("snap_Xdd", [128, 2048], BF16),
                             ("snap_ssc", [128, 2048], BF16), ("snap_W", [128, 4096], F32),
                             ("snap_lr0", [128, 4096], F32), ("snap_lr1", [128, 4096], F32),
                             ("snap_D", [128, 4096], F32), ("snap_lr1b", [128, 4096], F32),
                             ("snap_lsd", [128, 1024], F32), ("snap_edq", [128, 1024], F32),
                             ("snap_edt", [128, 1024], F32)]:
            scan_taps[nm] = nc.dram_tensor(nm, shp, dt_, kind="ExternalOutput").ap()
    if DEBUG_TAPS:
        for nm, src in [("tap_xs", xs_dram), ("tap_sz", sz_dram), ("tap_y", y_dram),
                        ("tap_at", at_dram), ("tap_xm", xm_dram), ("tap_ff", ff_dram),
                        ("tap_lam", lam_dram)]:
            tp = nc.dram_tensor(nm, list(src.tensor.shape), src.dtype, kind="ExternalOutput").ap()
            taps.append((tp, src))

    k.sem("d_xin", "d_w", "d_xs", "d_lrep", "d_y", "d_sz", "d_at", "d_xm", "d_ff",
          "d_cin", "d_scrw", "d_out", "d_cst", "d_cw", "ccs", "c_ydbg",
          "c_pe", "c_dve", "c_act", "c_pool")


    def finish_early():
        od = k.dma("gpsimd", out_d, xin, waits=[(s_, k.val(s_)) for s_ in k.sem_order
                                                if s_ not in ("d_out",)], inc="d_out")
        if DEBUG_TAPS:
            for tp, src in taps:
                k.dma("gpsimd", tp, src[:], inc="d_scrw")
        k.wait("gpsimd", [("d_out", od), ("d_scrw", k.val("d_scrw"))])
        k.wait("sync", [("d_out", od)])
        return k.finalize()
    ident_bf = k.sb("ident_bf", [128, 128], BF16)
    ident_f = k.sb("ident_f", [128, 128], F32)
    maskbin = k.sb("maskbin", [128, 128], F32)
    mup_s = k.sb("mup", [128, 68], F32)
    lnw_s = k.sb("lnw", [128, 32], F32)
    nrmw_s = k.sb("nrmw", [128, 32], F32)
    gate_bf = k.sb("gate_bf", [128, 32 * 4], BF16)
    loraA_bf = k.sb("loraA_bf", [128, 32 * 32], BF16)
    dtb_s = k.sb("dtb", [64, 1], F32)
    alog_s = k.sb("alog", [64, 1], F32)
    expA_s = k.sb("expA", [64, 1], F32)
    ebias_r = k.sb("ebias_r", [128, 4], F32)
    hidx_s = k.sb("hidx", [128, 1], I32)
    onesf = k.sb("onesf", [1, 128], F32)
    stat = k.sb("stat", [128, 16], F32)
    ones64 = k.sb("ones64", [64, 1], F32)
    zero64 = k.sb("zero64", [64, 1], F32)
    rowtmp = k.sb("rowtmp", [1, 64], F32)
    B_bf = k.sb("B_bf", [128, T], BF16)
    C_bf = k.sb("C_bf", [128, T], BF16)
    lsd_tmA = k.sb("lsd_tmA", [128, NCH * 64], F32)
    edec_tmA = k.sb("edec_tmA", [128, NCH * 64], F32)
    edecQA = k.sb("edecQA", [128, NCH * 64], F32)
    lam_tmA = k.sb("lam_tmA", [128, NCH * 64], F32)
    SBASE = k.mark()

    x2_fm = k.sb("x2_fm", [128, 32 * TP], BF16, at=SBASE)
    XEND = ((SBASE + 63) & ~63) + 32 * TP * 2

    psb = [nc.alloc_psum_tensor(f"psb{i}", [128, 512], F32) for i in range(8)]

    def pbf(i):
        return psb[i].ap().bitcast(BF16)

    for dst, src in [(ident_bf, identb_i), (ident_f, identf_i), (maskbin, maskbin_i),
                     (lnw_s, lnw_t), (nrmw_s, nrmw_t),
                     (dtb_s, dtb_i), (alog_s, alog_i), (hidx_s, hidx_i)]:
        k.dma("scalar", dst[:], src, inc="d_cst")
    k.dma("scalar", mup_s[:, 0:67], mup_t, inc="d_cst")
    k.dma("scalar", ebias_r[:], ebias_i.partition_broadcast(128), inc="d_cst")
    k.dma("gpsimd", gate_bf[:], gate_t.rearrange("p b e -> p (b e)"), inc="d_w")
    k.dma("gpsimd", loraA_bf[:], loraA_t.rearrange("p b e -> p (b e)"), inc="d_w")
    CST = k.val("d_cst")
    k.wait("vector", [("d_cst", CST)])
    k.emit("vector", lambda e: e.memset(onesf[:], 1.0), inc="c_dve")
    k.emit("vector", lambda e: e.memset(ones64[:], 1.0), inc="c_dve")
    k.emit("vector", lambda e: e.memset(zero64[:], 0.0), inc="c_dve")
    k.emit("scalar", lambda e: e.activation(out=expA_s[:], in_=alog_s[:], func=AF.Exp),
           waits=[("d_cst", CST)], inc="c_act")

    def ln_stats(xt, scrap, double, np_=128, extra_waits=()):
        ws = list(extra_waits)
        t_sum = k.emit("vector", lambda e: e.reduce_sum(out=stat[0:np_, 0:1], in_=xt, axis=AX.X),
                       waits=ws, inc="c_dve")
        t_ssq = k.emit("scalar", lambda e: e.activation(out=scrap, in_=xt, func=AF.Square,
                                                        accum_out=stat[0:np_, 1:2]),
                       waits=ws, inc="c_act")
        t_mean = k.emit("vector", lambda e: e.tensor_scalar(out=stat[0:np_, 2:3], in0=stat[0:np_, 0:1],
                                                            scalar1=1.0 / H, scalar2=None, op0=ALU.mult),
                        waits=[("c_dve", t_sum)], inc="c_dve")
        t1 = k.emit("vector", lambda e: e.tensor_scalar(out=stat[0:np_, 3:4], in0=stat[0:np_, 2:3],
                                                        scalar1=stat[0:np_, 2:3], scalar2=-1.0,
                                                        op0=ALU.mult, op1=ALU.mult),
                    waits=[("c_dve", t_mean)], inc="c_dve")
        t2 = k.emit("vector", lambda e: e.scalar_tensor_tensor(out=stat[0:np_, 3:4], in0=stat[0:np_, 1:2],
                                                               scalar=1.0 / H, in1=stat[0:np_, 3:4],
                                                               op0=ALU.mult, op1=ALU.add),
                    waits=[("c_dve", t1), ("c_act", t_ssq)], inc="c_dve")
        a1 = k.emit("scalar", lambda e: e.activation(out=stat[0:np_, 4:5], in_=stat[0:np_, 3:4],
                                                     func=AF.Ln, bias=np.float32(EPS)),
                    waits=[("c_dve", t2)], inc="c_act")
        a2 = k.emit("scalar", lambda e: e.activation(out=stat[0:np_, 5:6], in_=stat[0:np_, 4:5],
                                                     func=AF.Exp, scale=np.float32(-0.5)),
                    waits=[("c_act", a1)], inc="c_act")
        if double:
            a3 = k.emit("scalar", lambda e: e.activation(out=stat[0:np_, 6:7], in_=stat[0:np_, 4:5],
                                                         func=AF.Exp, scale=np.float32(-1.0)),
                        waits=[("c_act", a2)], inc="c_act")
            v3 = k.emit("vector", lambda e: e.tensor_tensor(out=stat[0:np_, 7:8], in0=stat[0:np_, 3:4],
                                                            in1=stat[0:np_, 6:7], op=ALU.mult),
                        waits=[("c_act", a3)], inc="c_dve")
            a4 = k.emit("scalar", lambda e: e.activation(out=stat[0:np_, 8:9], in_=stat[0:np_, 7:8],
                                                         func=AF.Ln, bias=np.float32(EPS)),
                        waits=[("c_dve", v3)], inc="c_act")
            a5 = k.emit("scalar", lambda e: e.activation(out=stat[0:np_, 9:10], in_=stat[0:np_, 8:9],
                                                         func=AF.Exp, scale=np.float32(-0.5)),
                        waits=[("c_act", a4)], inc="c_act")
            v4 = k.emit("vector", lambda e: e.tensor_tensor(out=stat[0:np_, 10:11], in0=stat[0:np_, 5:6],
                                                            in1=stat[0:np_, 9:10], op=ALU.mult),
                        waits=[("c_act", a5)], inc="c_dve")
        else:
            v4 = k.emit("vector", lambda e: e.tensor_copy(out=stat[0:np_, 10:11], in_=stat[0:np_, 5:6]),
                        waits=[("c_act", a2)], inc="c_dve")
        v5 = k.emit("vector", lambda e: e.tensor_scalar(out=stat[0:np_, 11:12], in0=stat[0:np_, 2:3],
                                                        scalar1=stat[0:np_, 10:11], scalar2=-1.0,
                                                        op0=ALU.mult, op1=ALU.mult),
                    waits=[("c_dve", v4)], inc="c_dve")
        return v5

    k.reset(XEND)
    xin_t = [k.sb(f"xin{i}", [128, H], F32) for i in range(2)]
    x2tm = k.sb("x2tm", [128, H], BF16)
    assert k.mark() <= k.SBLIM

    apply_toks = []
    last_tr = 0
    for i in range(NTT + 1):
        halo = (i == NTT)
        np_ = 4 if halo else 128
        if halo:
            buf = xin_t[0][0:4, :]
            w = [("c_act", apply_toks[-2])]
            k.dma("sync", xin_t[0][0:4, :], xhalo, waits=w, inc="d_xin")
        else:
            buf = xin_t[i % 2][:]
            w = [("c_act", apply_toks[i - 2])] if i >= 2 else []
            k.dma("sync", xin_t[i % 2][:],
                  xin.rearrange("(n p) h -> n p h", p=128)[i], waits=w, inc="d_xin")
        ldv = k.val("d_xin")
        prev_apply = apply_toks[-1] if apply_toks else 0
        stok = ln_stats(buf, x2tm[0:np_, :], True, np_=np_,
                        extra_waits=[("d_xin", ldv), ("c_act", prev_apply),
                                     ("c_pe", last_tr), ("d_cst", CST)])
        ap_tok = k.emit("scalar", lambda e, b_=buf, np_=np_: e.activation(
            out=x2tm[0:np_, :], in_=b_, func=AF.Identity,
            scale=stat[0:np_, 10:11], bias=stat[0:np_, 11:12]),
            waits=[("c_dve", stok)], inc="c_act")
        apply_toks.append(ap_tok)
        for grp in range(4):
            bank = 4 + (grp % 2)
            trt = None
            for kk in range(8):
                hb = grp * 8 + kk
                if halo:
                    fn = (lambda e, hb=hb, bank=bank, kk=kk: e.transpose(
                        out=pbf(bank)[:, kk * 128:kk * 128 + 3],
                        in_=x2tm[0:3, hb * 128:(hb + 1) * 128], identity=ident_bf[0:3, 0:3]))
                else:
                    fn = (lambda e, hb=hb, bank=bank, kk=kk: e.transpose(
                        out=pbf(bank)[:, kk * 128:(kk + 1) * 128],
                        in_=x2tm[:, hb * 128:(hb + 1) * 128], identity=ident_bf[:]))
                trt = k.emit("tensor", fn,
                             waits=([("c_act", ap_tok), ("c_dve", k.val("c_dve")),
                                     ("d_cst", CST)] if kk == 0 else ()),
                             inc="c_pe" if kk == 7 else None)
            last_tr = trt
            for kk in range(8):
                hb = grp * 8 + kk
                if halo:
                    fn = (lambda e, hb=hb, bank=bank, kk=kk: e.tensor_scalar(
                        out=x2_fm[:, hb * TP + T: hb * TP + T + 3],
                        in0=pbf(bank)[:, kk * 128:kk * 128 + 3],
                        scalar1=lnw_s[:, hb:hb + 1], scalar2=None, op0=ALU.mult))
                else:
                    fn = (lambda e, hb=hb, bank=bank, kk=kk, i=i: e.tensor_scalar(
                        out=x2_fm[:, hb * TP + i * 128: hb * TP + (i + 1) * 128],
                        in0=pbf(bank)[:, kk * 128:(kk + 1) * 128],
                        scalar1=lnw_s[:, hb:hb + 1], scalar2=None, op0=ALU.mult))
                k.emit("vector", fn, waits=[("c_pe", trt), ("d_cst", CST)] if kk == 0 else (),
                       inc="c_dve" if kk == 7 else None)
    k.barrier()
    if STAGES <= 1:
        return finish_early()

    k.reset(XEND)
    wip_bf = [k.sb(f"wip{i}", [128, 4096], BF16) for i in range(2)]
    xbc_b = k.sb("xbc", [128, 2052], F32)
    convo = k.sb("convo", [128, T], F32)
    sil_bf = k.sb("sil_bf", [128, T], BF16)
    stg = k.sb("stg", [128, NTT * 128], BF16)
    cw_s = k.sb("cw", [128, 4], F32)
    dtmp = k.sb("dtmp", [64, T], F32)
    assert k.mark() <= k.SBLIM

    def emit_tm_scratch(sil_tok, dram, c0):
        ev_tok = None
        for g in range(2):
            bank = 6 + g
            trt = None
            for kk in range(8):
                tt = g * 8 + kk
                trt = k.emit("tensor", lambda e, tt=tt, bank=bank, kk=kk: e.transpose(
                    out=pbf(bank)[:, kk * 128:(kk + 1) * 128],
                    in_=sil_bf[:, tt * 128:(tt + 1) * 128], identity=ident_bf[:]),
                    waits=([("c_act", sil_tok), ("c_dve", k.val("c_dve"))] if kk == 0 else ()),
                    inc="c_pe" if kk == 7 else None)
            ev_tok = k.emit("vector", lambda e, g=g, bank=bank: e.tensor_copy(
                out=stg[:, g * 1024:(g + 1) * 1024], in_=pbf(bank)[:, 0:1024]),
                waits=[("c_pe", trt), ("d_scrw", k.val("d_scrw"))], inc="c_dve")
        return k.dma("scalar", dram[:, :, c0:c0 + 128].transpose([1, 0, 2]),
                     stg[:].rearrange("p (n c) -> p n c", n=NTT),
                     waits=[("c_dve", ev_tok)], inc="d_scrw")

    def dt_process(ev_tok):
        dr = convo[0:64, 0:T]
        a1 = k.emit("scalar", lambda e: e.activation(out=dtmp[:], in_=dr, func=AF.Abs, bias=dtb_s[:]),
                    waits=[("c_act", ev_tok)], inc="c_act")
        a2 = k.emit("scalar", lambda e: e.activation(out=dtmp[:], in_=dtmp[:], func=AF.Exp, scale=np.float32(-1.0)),
                    waits=[("c_act", a1)], inc="c_act")
        a3 = k.emit("scalar", lambda e: e.activation(out=dtmp[:], in_=dtmp[:], func=AF.Ln, bias=np.float32(1.0)),
                    waits=[("c_act", a2)], inc="c_act")
        a4 = k.emit("scalar", lambda e: e.activation(out=xbc_b[0:64, 0:T], in_=dr, func=AF.Relu,
                                                     bias=dtb_s[:]),
                    waits=[("c_act", a3)], inc="c_act")
        v1 = k.emit("vector", lambda e: e.tensor_tensor(out=dr, in0=dtmp[:], in1=xbc_b[0:64, 0:T],
                                                        op=ALU.add),
                    waits=[("c_act", a4)], inc="c_dve")
        a5 = k.emit("scalar", lambda e: e.activation(out=dtmp[:], in_=dr, func=AF.Ln),
                    waits=[("c_dve", v1)], inc="c_act")
        v2 = k.emit("vector", lambda e: e.tensor_scalar(out=xbc_b[0:64, 0:T], in0=dr,
                                                        scalar1=expA_s[:], scalar2=-1.0,
                                                        op0=ALU.mult, op1=ALU.mult),
                    waits=[("c_act", a5)], inc="c_dve")
        v3 = k.emit("vector", lambda e: e.tensor_tensor_scan(
            out=dr, data0=ones64[:].broadcast_to([64, T]), data1=xbc_b[0:64, 0:T],
            initial=0.0, op0=ALU.mult, op1=ALU.add),
            waits=[("c_dve", v2)], inc="c_dve")
        lamwr = k.dma("scalar", lam_dram.rearrange("c h s -> h c s"),
                      dr.rearrange("h (c s) -> h c s", c=NCH),
                      waits=[("c_dve", v3)], inc="d_scrw")
        v4 = k.emit("vector", lambda e: e.tensor_tensor(out=dtmp[:], in0=dr, in1=dtmp[:],
                                                        op=ALU.subtract),
                    waits=[("c_dve", v3)], inc="c_dve")
        a6 = k.emit("scalar", lambda e: e.activation(out=xbc_b[0:64, 0:T], in_=xbc_b[0:64, 0:T],
                                                     func=AF.Exp),
                    waits=[("c_dve", v4)], inc="c_act")
        for j in range(NCH):
            t1 = k.emit("tensor", lambda e, j=j: e.transpose(
                out=psb[7][:, 0:64], in_=dr[:, j * 128:(j + 1) * 128],
                identity=ident_f[0:64, 0:64]),
                waits=[("d_scrw", lamwr), ("c_dve", k.val("c_dve")), ("c_act", a6)], inc="c_pe")
            e1 = k.emit("vector", lambda e, j=j: e.tensor_copy(
                out=lam_tmA[:, j * 64:(j + 1) * 64], in_=psb[7][:, 0:64]),
                waits=[("c_pe", t1)], inc="c_dve")
            t2 = k.emit("tensor", lambda e, j=j: e.transpose(
                out=psb[7][:, 64:128], in_=dtmp[:, j * 128:(j + 1) * 128],
                identity=ident_f[0:64, 0:64]),
                waits=[("c_dve", e1)], inc="c_pe")
            e2 = k.emit("vector", lambda e, j=j: e.tensor_copy(
                out=lsd_tmA[:, j * 64:(j + 1) * 64], in_=psb[7][:, 64:128]),
                waits=[("c_pe", t2)], inc="c_dve")
            ed = xbc_b[0:64, 1924:2052]
            v5 = k.emit("vector", lambda e, j=j: e.tensor_tensor_scan(
                out=ed, data0=xbc_b[0:64, j * 128:(j + 1) * 128],
                data1=zero64[:].broadcast_to([64, 128]),
                initial=1.0, op0=ALU.mult, op1=ALU.add),
                waits=[("c_dve", e2)], inc="c_dve")
            t3 = k.emit("tensor", lambda e: e.transpose(
                out=psb[7][:, 128:192], in_=ed, identity=ident_f[0:64, 0:64]),
                waits=[("c_dve", v5)], inc="c_pe")
            e3 = k.emit("vector", lambda e, j=j: e.tensor_copy(
                out=edec_tmA[:, j * 64:(j + 1) * 64], in_=psb[7][:, 128:192]),
                waits=[("c_pe", t3)], inc="c_dve")
            t4 = k.emit("tensor", lambda e: e.transpose(
                out=psb[7][0:1, 192:256], in_=ed[:, 127:128], identity=ident_f[0:64, 0:64]),
                waits=[("c_dve", e3)], inc="c_pe")
            e4 = k.emit("vector", lambda e: e.tensor_copy(out=rowtmp[:], in_=psb[7][0:1, 192:256]),
                        waits=[("c_pe", t4)], inc="c_dve")
            t5 = k.emit("tensor", lambda e: e.matmul(out=psb[7][:, 256:320], lhsT=onesf[:],
                                                     rhs=rowtmp[:], start=True, stop=True),
                        waits=[("c_dve", e4)], inc="c_pe")
            k.emit("vector", lambda e, j=j: e.tensor_copy(
                out=edecQA[:, j * 64:(j + 1) * 64], in_=psb[7][:, 256:320]),
                waits=[("c_pe", t5)], inc="c_dve")
        return k.val("c_dve")

    cb_order = [66, 64, 65] + list(range(32, 64)) + list(range(0, 32))
    mm_done = {}
    bank_evac = {g: 0 for g in range(5)}
    conv_done = 0
    silu_done = 0

    for ki, cb in enumerate(cb_order):
        wb = wip_bf[ki % 2]
        wws = [("c_pe", mm_done[ki - 2])] if ki >= 2 else \
              [("c_act", k.val("c_act")), ("c_dve", k.val("c_dve")), ("c_pe", k.val("c_pe"))]
        wld = k.dma("gpsimd", wb[:], w_ip[cb].rearrange("p b c -> p (b c)"), waits=wws, inc="d_w")
        xbc = 32 <= cb <= 65
        if xbc:
            cwld = k.dma("sync", cw_s[:], conv_t[cb - 32], waits=[("c_dve", conv_done)], inc="d_cw")
        grp_tok = []
        for g in range(4):
            for hb in range(32):
                ws = []
                if g == 0 and hb == 0:
                    ws.append(("d_w", wld))
                if hb == 0:
                    ws.append(("c_act", bank_evac[g]))
                tok = k.emit("tensor", lambda e, g=g, hb=hb, wb=wb: e.matmul(
                    out=psb[g][:, 0:512],
                    lhsT=wb[:, hb * 128:(hb + 1) * 128],
                    rhs=x2_fm[:, hb * TP + g * 512: hb * TP + (g + 1) * 512],
                    start=(hb == 0), stop=(hb == 31)),
                    waits=ws, inc="c_pe" if hb == 31 else None)
            grp_tok.append(tok)
        halo_tok = None
        if xbc:
            for hb in range(32):
                ws = [("c_act", bank_evac[4])] if hb == 0 else ()
                halo_tok = k.emit("tensor", lambda e, hb=hb, wb=wb: e.matmul(
                    out=psb[4][:, 0:3],
                    lhsT=wb[:, hb * 128:(hb + 1) * 128],
                    rhs=x2_fm[:, hb * TP + T: hb * TP + T + 3],
                    start=(hb == 0), stop=(hb == 31)),
                    waits=ws, inc="c_pe" if hb == 31 else None)
        mm_done[ki] = halo_tok if halo_tok is not None else grp_tok[-1]

        if cb == 66:
            evs = []
            for g in range(4):
                evs.append(k.emit("scalar", lambda e, g=g: e.activation(
                    out=convo[0:64, g * 512:(g + 1) * 512], in_=psb[g][0:64, 0:512],
                    func=AF.Identity, scale=mup_s[0:64, 66:67]),
                    waits=[("c_pe", grp_tok[g])], inc="c_act"))
                bank_evac[g] = evs[-1]
            conv_done = dt_process(evs[-1])
        elif xbc:
            for g in range(4):
                ws = [("c_pe", grp_tok[g])]
                if g == 0:
                    ws += [("c_dve", conv_done)]
                bank_evac[g] = k.emit("scalar", lambda e, g=g, cb=cb: e.activation(
                    out=xbc_b[:, 3 + g * 512: 3 + (g + 1) * 512], in_=psb[g][:, 0:512],
                    func=AF.Identity, scale=mup_s[:, cb:cb + 1]),
                    waits=ws, inc="c_act")
            bank_evac[4] = k.emit("scalar", lambda e, cb=cb: e.activation(
                out=xbc_b[:, 0:3], in_=psb[4][:, 0:3],
                func=AF.Identity, scale=mup_s[:, cb:cb + 1]),
                waits=[("c_pe", halo_tok), ("c_dve", conv_done)], inc="c_act")
            cv = k.emit("vector", lambda e: e.tensor_scalar(
                out=convo[:, 0:T], in0=xbc_b[:, 0:T], scalar1=cw_s[:, 0:1],
                scalar2=None, op0=ALU.mult),
                waits=[("c_act", bank_evac[4]), ("c_act", bank_evac[3]), ("d_cw", cwld),
                       ("c_act", silu_done)], inc="c_dve")
            for kk in range(1, 4):
                cv = k.emit("vector", lambda e, kk=kk: e.scalar_tensor_tensor(
                    out=convo[:, 0:T], in0=xbc_b[:, kk:kk + T], scalar=cw_s[:, kk:kk + 1],
                    in1=convo[:, 0:T], op0=ALU.mult, op1=ALU.add),
                    waits=[("c_dve", cv)], inc="c_dve")
            conv_done = cv
            if cb == 64:
                silu_done = k.emit("scalar", lambda e: e.activation(
                    out=B_bf[:], in_=convo[:, 0:T], func=AF.Silu),
                    waits=[("c_dve", cv)], inc="c_act")
            elif cb == 65:
                silu_done = k.emit("scalar", lambda e: e.activation(
                    out=C_bf[:], in_=convo[:, 0:T], func=AF.Silu),
                    waits=[("c_dve", cv)], inc="c_act")
            else:
                silu_done = k.emit("scalar", lambda e: e.activation(
                    out=sil_bf[:], in_=convo[:, 0:T], func=AF.Silu),
                    waits=[("c_dve", cv), ("c_pe", k.val("c_pe"))], inc="c_act")
                emit_tm_scratch(silu_done, xs_dram, (cb - 32) * 128)
        else:
            for g in range(4):
                ws = [("c_pe", grp_tok[g])]
                if g == 0:
                    ws.append(("c_pe", k.val("c_pe")))
                bank_evac[g] = k.emit("scalar", lambda e, g=g, cb=cb: e.activation(
                    out=sil_bf[:, g * 512:(g + 1) * 512], in_=psb[g][:, 0:512],
                    func=AF.Silu, scale=mup_s[:, cb:cb + 1]),
                    waits=ws, inc="c_act")
            emit_tm_scratch(bank_evac[3], sz_dram, cb * 128)

    k.barrier()
    if STAGES <= 2:
        return finish_early()

    k.reset(SBASE)
    xs_bf = [k.sb(f"xsbf{i}", [128, H], BF16) for i in range(2)]
    state_bf = k.sb("state_bf", [128, H], BF16)
    lrep = [k.sb(f"lrep{i}", [128, 4096], F32) for i in range(2)]
    Dbuf = k.sb("Dbuf", [128, 4096], F32)
    Wbuf = k.sb("Wbuf", [128, 4096], F32)
    Pbuf = k.sb("Pbuf", [128, 4096], BF16)
    ybuf = k.sb("ybuf", [128, H], F32)
    yD = k.sb("yD", [128, H], F32)
    Dbcast = k.sb("Dbcast", [128, 4096], F32)
    Gtm = k.sb("Gtm", [128, 128], F32)
    B_tm = k.sb("B_tm", [128, 128], BF16)
    ssc = k.sb("ssc", [128, 2048], BF16)
    Xdd = k.sb("Xdd", [128, 2048], BF16)
    P2_END = k.mark()
    H_bf = k.sb("H_bf", [128, H], BF16, at=P2_END)
    assert P2_END + H * 2 <= k.SBLIM

    k.dma("scalar", Dbcast[:], drep_i.partition_broadcast(128), inc="d_cst")
    DB = k.val("d_cst")
    k.emit("vector", lambda e: e.memset(state_bf[:], 0.0), inc="c_dve")

    lam_consumed = {}
    xs_consumed = {}
    y_dma_tok = 0
    yasm_done = 0
    bank_free = {0: ("c_dve", 0), 2: ("c_dve", 0), 4: ("c_dve", 0)}
    for j in range(NCH):
        xsb = xs_bf[j % 2]
        if os.environ.get("MK_XSYNC") == "1":
            xw = [("c_pe", k.val("c_pe")), ("c_dve", k.val("c_dve")),
                  ("c_act", k.val("c_act")), ("c_pool", k.val("c_pool"))]
        else:
            xw = [("c_dve", k.val("c_dve")), ("c_pool", k.val("c_pool"))] if j < 2 else \
                 [("c_pe", xs_consumed[j - 2]), ("c_dve", lam_consumed[j - 2]),
                  ("c_pool", k.val("c_pool"))]
        xld = k.dma("sync", xsb[:], xs_dram[j], waits=xw, inc="d_xs")
        lld = []
        for hf in range(2):
            lws = [] if j == 0 else [("c_dve", lam_consumed.get(j - 1, 0))]
            lld.append(k.dma("sync", lrep[hf][:].rearrange("p (h s) -> p h s", h=32),
                             lam_dram[j, 32 * hf:32 * (hf + 1), :].unsqueeze(0).partition_broadcast(128),
                             waits=lws if hf == 0 else (), inc="d_lrep"))
        gt = k.emit("tensor", lambda e, j=j: e.matmul(
            out=psb[6][:, 0:128], lhsT=B_bf[:, j * 128:(j + 1) * 128],
            rhs=C_bf[:, j * 128:(j + 1) * 128], start=True, stop=True),
            waits=[("c_dve", k.val("c_dve"))], inc="c_pe")
        gtm = k.emit("vector", lambda e: e.tensor_tensor(out=Gtm[:], in0=psb[6][:, 0:128],
                                                         in1=maskbin[:], op=ALU.mult),
                     waits=[("c_pe", gt)], inc="c_dve")
        btr = k.emit("tensor", lambda e, j=j: e.transpose(
            out=pbf(6)[:, 512:640], in_=B_bf[:, j * 128:(j + 1) * 128], identity=ident_bf[:]),
            inc="c_pe")
        bev = k.emit("vector", lambda e: e.tensor_copy(out=B_tm[:], in_=pbf(6)[:, 512:640]),
                     waits=[("c_pe", btr)], inc="c_dve")
        if os.environ.get("MK_YD_DVE") == "1":
            ydt = k.emit("vector", lambda e, xsb=xsb: e.tensor_tensor(out=yD[:], in0=xsb[:],
                                                                       in1=Dbcast[:], op=ALU.mult),
                         waits=[("d_xs", xld), ("d_cst", DB)], inc="c_dve")
            yd_sem = "c_dve"
        else:
            ydt = k.emit("gpsimd", lambda e, xsb=xsb: e.tensor_tensor(out=yD[:], in0=xsb[:],
                                                                       in1=Dbcast[:], op=ALU.mult),
                         waits=[("d_xs", xld), ("d_cst", DB), ("c_dve", yasm_done)], inc="c_pool")
            yd_sem = "c_pool"

        for hf in range(2):
            hbase = 32 * hf
            dtok = None
            for hh in range(32):
                h = hbase + hh
                ws = []
                if hh == 0:
                    ws = [("d_lrep", lld[hf]), ("c_act", k.val("c_act"))]
                dtok = k.emit("vector", lambda e, hh=hh, h=h, hf=hf, j=j: e.tensor_scalar(
                    out=Dbuf[:, hh * 128:(hh + 1) * 128],
                    in0=lrep[hf][:, hh * 128:(hh + 1) * 128],
                    scalar1=lsd_tmA[:, j * 64 + h: j * 64 + h + 1], scalar2=88.0,
                    op0=ALU.subtract, op1=ALU.min),
                    waits=ws, inc="c_dve" if hh == 31 else None)
            if hf == 1:
                lam_consumed[j] = dtok
                if SCAN_DEBUG and j == 2:
                    k.dma("scalar", scan_taps["snap_D"], Dbuf[:], waits=[("c_dve", dtok)], inc="d_scrw")
                    k.dma("scalar", scan_taps["snap_lr1b"], lrep[1][:], waits=[("c_dve", dtok)], inc="d_scrw")
            wt = k.emit("scalar", lambda e: e.activation(out=Wbuf[:], in_=Dbuf[:], func=AF.Exp),
                        waits=[("c_dve", dtok), ("c_dve", yasm_done)], inc="c_act")
            pt = k.emit("vector", lambda e: e.tensor_tensor(
                out=Pbuf[:].rearrange("p (h t) -> p h t", h=32),
                in0=Wbuf[:].rearrange("p (h t) -> p h t", h=32),
                in1=Gtm[:].unsqueeze(1).broadcast_to([128, 32, 128]),
                op=ALU.mult),
                waits=[("c_act", wt), ("c_dve", gtm), ("c_pe", k.val("c_pe"))], inc="c_dve")
            xdt = None
            for hh in range(32):
                h = hbase + hh
                xdt = k.emit("vector", lambda e, hh=hh, h=h, xsb=xsb: e.tensor_scalar(
                    out=Xdd[:, hh * 64:(hh + 1) * 64], in0=xsb[:, h * 64:(h + 1) * 64],
                    scalar1=Wbuf[:, hh * 128 + 127: hh * 128 + 128], scalar2=None, op0=ALU.mult),
                    waits=[("c_act", wt), ("d_xs", xld), ("c_pe", k.val("c_pe"))] if hh == 0 else (),
                    inc="c_dve" if hh == 31 else None)
            sst = None
            for hh in range(32):
                h = hbase + hh
                sst = k.emit("vector", lambda e, hh=hh, h=h, j=j: e.tensor_scalar(
                    out=ssc[:, hh * 64:(hh + 1) * 64], in0=state_bf[:, h * 64:(h + 1) * 64],
                    scalar1=edecQA[:, j * 64 + h: j * 64 + h + 1], scalar2=None, op0=ALU.mult),
                    waits=[("c_pe", k.val("c_pe"))] if hh == 0 else (),
                    inc="c_dve" if hh == 31 else None)
            for sg in range(2):
                sgb = hbase + sg * 16
                y1t = y2t = sut = None
                for hh in range(16):
                    lhh = sg * 16 + hh
                    h = hbase + lhh
                    bank = hh // 8
                    slot = (hh % 8) * 64
                    ws = ([("c_dve", pt), bank_free[0], ("d_xs", xld)] if hh == 0 else ())
                    y1t = k.emit("tensor", lambda e, lhh=lhh, h=h, bank=bank, slot=slot, xsb=xsb: e.matmul(
                        out=psb[bank][:, slot:slot + 64],
                        lhsT=Pbuf[:, lhh * 128:(lhh + 1) * 128],
                        rhs=xsb[:, h * 64:(h + 1) * 64], start=True, stop=True),
                        waits=ws, inc="c_pe" if hh == 15 else None)
                for hh in range(16):
                    h = hbase + sg * 16 + hh
                    bank = 2 + hh // 8
                    slot = (hh % 8) * 64
                    ws = ([bank_free[2], ("c_dve", sst)] if hh == 0 else ())
                    y2t = k.emit("tensor", lambda e, h=h, bank=bank, slot=slot, j=j: e.matmul(
                        out=psb[bank][:, slot:slot + 64],
                        lhsT=C_bf[:, j * 128:(j + 1) * 128],
                        rhs=state_bf[:, h * 64:(h + 1) * 64], start=True, stop=True),
                        waits=ws, inc="c_pe" if hh == 15 else None)
                for hh in range(16):
                    lhh = sg * 16 + hh
                    h = hbase + lhh
                    bank = 4 + hh // 8
                    slot = (hh % 8) * 64
                    ws = []
                    if hh == 0:
                        ws = [("c_dve", xdt), ("c_dve", sst), ("c_dve", bev), bank_free[4]]
                    k.emit("tensor", lambda e, lhh=lhh, bank=bank, slot=slot: e.matmul(
                        out=psb[bank][:, slot:slot + 64], lhsT=B_tm[:],
                        rhs=Xdd[:, lhh * 64:(lhh + 1) * 64], start=True, stop=False),
                        waits=ws, inc=None)
                    sut = k.emit("tensor", lambda e, lhh=lhh, bank=bank, slot=slot: e.matmul(
                        out=psb[bank][:, slot:slot + 64], lhsT=ident_bf[:],
                        rhs=ssc[:, lhh * 64:(lhh + 1) * 64], start=False, stop=True),
                        inc="c_pe" if hh == 15 else None)
                pa = None
                for hh in range(16):
                    lhh = sg * 16 + hh
                    h = hbase + lhh
                    ws = [("c_pe", y2t), (yd_sem, ydt)] if hh == 0 else ()
                    if NO_Y2:
                        pa = k.emit("vector", lambda e, lhh=lhh, h=h: e.scalar_tensor_tensor(
                            out=Wbuf[:, lhh * 64:(lhh + 1) * 64],
                            in0=psb[2 + (lhh % 16) // 8][:, ((lhh % 16) % 8) * 64:((lhh % 16) % 8) * 64 + 64],
                            scalar=0.0,
                            in1=yD[:, h * 64:(h + 1) * 64], op0=ALU.mult, op1=ALU.add),
                            waits=ws, inc="c_dve" if hh == 15 else None)
                    else:
                        pa = k.emit("vector", lambda e, lhh=lhh, h=h, j=j: e.scalar_tensor_tensor(
                            out=Wbuf[:, lhh * 64:(lhh + 1) * 64],
                            in0=psb[2 + (lhh % 16) // 8][:, ((lhh % 16) % 8) * 64:((lhh % 16) % 8) * 64 + 64],
                            scalar=edec_tmA[:, j * 64 + h: j * 64 + h + 1],
                            in1=yD[:, h * 64:(h + 1) * 64], op0=ALU.mult, op1=ALU.add),
                            waits=ws, inc="c_dve" if hh == 15 else None)
                bank_free[2] = ("c_dve", pa)
                pb_ = None
                for hh in range(16):
                    lhh = sg * 16 + hh
                    h = hbase + lhh
                    ws = [("c_dve", pa), ("c_pe", y1t), ("d_scrw", y_dma_tok)] if hh == 0 else ()
                    if NO_Y1:
                        pb_ = k.emit("vector", lambda e, lhh=lhh, h=h: e.tensor_copy(
                            out=ybuf[:, h * 64:(h + 1) * 64],
                            in_=Wbuf[:, lhh * 64:(lhh + 1) * 64]),
                            waits=ws, inc="c_dve" if hh == 15 else None)
                    else:
                        pb_ = k.emit("vector", lambda e, lhh=lhh, h=h: e.tensor_tensor(
                            out=ybuf[:, h * 64:(h + 1) * 64],
                            in0=Wbuf[:, lhh * 64:(lhh + 1) * 64],
                            in1=psb[(lhh % 16) // 8][:, ((lhh % 16) % 8) * 64:((lhh % 16) % 8) * 64 + 64],
                            op=ALU.add),
                            waits=ws, inc="c_dve" if hh == 15 else None)
                bank_free[0] = ("c_dve", pb_)
                se = None
                for hh in range(16):
                    lhh = sg * 16 + hh
                    h = hbase + lhh
                    ws = [("c_pe", sut)] if hh == 0 else ()
                    se = k.emit("vector", lambda e, lhh=lhh, h=h: e.tensor_copy(
                        out=state_bf[:, h * 64:(h + 1) * 64],
                        in_=psb[4 + (lhh % 16) // 8][:, ((lhh % 16) % 8) * 64:((lhh % 16) % 8) * 64 + 64]),
                        waits=ws, inc="c_dve" if hh == 15 else None)
                bank_free[4] = ("c_dve", se)
                yasm_done = pb_
        xs_consumed[j] = k.val("c_pe")
        ydbg = k.emit("vector", lambda e: e.engine_nop(), waits=[("c_dve", yasm_done)], inc="c_ydbg")
        if SCAN_DEBUG and j == 2:
            k.dma("scalar", scan_taps["snap_xs"], xs_bf[0][:],
                  waits=[("c_dve", k.val("c_dve")), ("c_pe", k.val("c_pe"))], inc="d_scrw")
            k.dma("scalar", scan_taps["snap_y"], ybuf[:], waits=[("c_dve", yasm_done)], inc="d_scrw")
            k.dma("scalar", scan_taps["snap_P"], Pbuf[:], inc="d_scrw")
            k.dma("scalar", scan_taps["snap_Xdd"], Xdd[:], inc="d_scrw")
            k.dma("scalar", scan_taps["snap_ssc"], ssc[:], inc="d_scrw")
            k.dma("scalar", scan_taps["snap_W"], Wbuf[:], inc="d_scrw")
            k.dma("scalar", scan_taps["snap_lr0"], lrep[0][:], inc="d_scrw")
            k.dma("scalar", scan_taps["snap_lr1"], lrep[1][:], inc="d_scrw")
            k.dma("scalar", scan_taps["snap_lsd"], lsd_tmA[:], inc="d_scrw")
            k.dma("scalar", scan_taps["snap_edq"], edecQA[:], inc="d_scrw")
            k.dma("scalar", scan_taps["snap_edt"], edec_tmA[:], inc="d_scrw")
        y_dma_tok = k.dma("scalar", y_dram[j], ybuf[:],
                          waits=[("c_dve", yasm_done), ("c_ydbg", ydbg)], inc="d_scrw")

    if STAGES <= 3:
        return finish_early()
    cw = k.dma("sync", cc_in, state_bf[:], waits=[("c_dve", k.val("c_dve"))], inc="d_cin")
    k.emit("gpsimd", lambda e: e.memset(H_bf[:], 0.0), waits=[("c_pe", k.val("c_pe"))], inc="c_pool")
    k.wait("gpsimd", [("d_cin", cw)])
    k.emit("gpsimd", lambda e: e.collective_compute(
        "AllGather", ALU.bypass, replica_groups=[list(range(NCORES))],
        ins=[cc_in.opt()], outs=[cc_out.opt()]), inc="ccs")
    k.wait("gpsimd", [("ccs", 1)])
    gat = k.emit("gpsimd", lambda e: e.indirect_dma_start(
        out=H_bf[:], out_offset=None, in_=cc_out[0:128, :],
        in_offset=IndirectOffsetOnAxis(ap=hidx_s[:, 0:1], axis=0),
        bounds_check=128 * NCORES - 1, oob_is_err=False),
        waits=[("d_cst", CST)], inc="d_cin", inc_n=16)
    k.barrier()

    if STAGES <= 4:
        return finish_early()

    k.reset(SBASE)
    y3_fm = k.sb("y3_fm", [128, 32 * 1024], BF16)
    Y3END = k.mark()
    ytile = [k.sb(f"ytile{i}", [128, H], F32) for i in range(2)]
    sztile = [k.sb(f"sztile{i}", [128, H], BF16) for i in range(2)]
    ycbuf = k.sb("ycbuf", [128, H], F32)
    gbuf = k.sb("gbuf", [128, H], F32)
    y3tm = k.sb("y3tm", [128, H], BF16)
    expl = k.sb("expl", [128, 64], F32)
    P2B_END = k.mark()
    assert P2B_END <= P2_END

    SC2 = k.val("d_scrw")

    def p2b_half(hf, y_w_total):
        toks = {"y3": []}
        for jj in range(8):
            j = hf * 8 + jj
            yt = ytile[jj % 2]
            szt = sztile[jj % 2]
            ws = [("c_act", k.val("c_act")), ("c_dve", k.val("c_dve"))] if jj < 2 else \
                 [("c_act", toks[f"ssq{jj-2}"]), ("c_dve", toks[f"g{jj-2}"])]
            yld = k.dma("sync", yt[:], y_dram[j], waits=ws + [("d_scrw", y_w_total)], inc="d_y")
            szld = k.dma("gpsimd", szt[:], sz_dram[j],
                         waits=[("c_dve", toks[f"g{jj-2}"])] if jj >= 2 else [], inc="d_sz")
            ex = k.emit("scalar", lambda e, j=j: e.activation(
                out=expl[:], in_=lam_tmA[:, j * 64:(j + 1) * 64], func=AF.Exp),
                waits=[("c_dve", k.val("c_dve"))], inc="c_act")
            yct = None
            for half2 in range(2):
                mmt = None
                for hh in range(32):
                    h = half2 * 32 + hh
                    bank = hh // 8 % 4
                    slot = (hh % 8) * 64
                    ws2 = []
                    if hh == 0:
                        ws2 = [("d_cin", k.val("d_cin")), ("c_dve", yct or 0),
                               ("c_act", k.val("c_act"))]
                    mmt = k.emit("tensor", lambda e, h=h, bank=bank, slot=slot, j=j: e.matmul(
                        out=psb[bank][:, slot:slot + 64],
                        lhsT=C_bf[:, j * 128:(j + 1) * 128],
                        rhs=H_bf[:, h * 64:(h + 1) * 64], start=True, stop=True),
                        waits=ws2, inc="c_pe" if hh == 31 else None)
                for hh in range(32):
                    h = half2 * 32 + hh
                    bank = hh // 8 % 4
                    slot = (hh % 8) * 64
                    ws2 = [("c_pe", mmt), ("d_y", yld), ("c_act", ex)] if hh == 0 else ()
                    yct = k.emit("vector", lambda e, h=h, bank=bank, slot=slot, yt=yt: e.scalar_tensor_tensor(
                        out=ycbuf[:, h * 64:(h + 1) * 64], in0=psb[bank][:, slot:slot + 64],
                        scalar=expl[:, h:h + 1], in1=yt[:, h * 64:(h + 1) * 64],
                        op0=ALU.mult, op1=ALU.add),
                        waits=ws2, inc="c_dve" if hh == 31 else None)
            gt_ = k.emit("vector", lambda e, szt=szt: e.tensor_tensor(out=gbuf[:], in0=ycbuf[:],
                                                                       in1=szt[:], op=ALU.mult),
                         waits=[("c_dve", yct), ("d_sz", szld)], inc="c_dve")
            toks[f"g{jj}"] = gt_
            sq = k.emit("scalar", lambda e, yt=yt: e.activation(out=yt[:], in_=gbuf[:], func=AF.Square,
                                                                accum_out=stat[:, 1:2]),
                        waits=[("c_dve", gt_)], inc="c_act")
            toks[f"ssq{jj}"] = sq
            l1 = k.emit("scalar", lambda e: e.activation(out=stat[:, 8:9], in_=stat[:, 1:2],
                                                         func=AF.Ln, scale=np.float32(1.0 / H), bias=np.float32(EPS)),
                        waits=[("c_act", sq)], inc="c_act")
            r1 = k.emit("scalar", lambda e: e.activation(out=stat[:, 9:10], in_=stat[:, 8:9],
                                                         func=AF.Exp, scale=np.float32(-0.5)),
                        waits=[("c_act", l1)], inc="c_act")
            ap2 = k.emit("scalar", lambda e: e.activation(out=y3tm[:], in_=gbuf[:], func=AF.Identity,
                                                          scale=stat[:, 9:10]),
                         waits=[("c_act", r1)], inc="c_act")
            for grp in range(4):
                bank = 4 + grp % 2
                trt = None
                for kk in range(8):
                    dsb = grp * 8 + kk
                    trt = k.emit("tensor", lambda e, dsb=dsb, bank=bank, kk=kk: e.transpose(
                        out=pbf(bank)[:, kk * 128:(kk + 1) * 128],
                        in_=y3tm[:, dsb * 128:(dsb + 1) * 128], identity=ident_bf[:]),
                        waits=([("c_act", ap2), ("c_dve", k.val("c_dve"))] if kk == 0 else ()),
                        inc="c_pe" if kk == 7 else None)
                evt = None
                for kk in range(8):
                    dsb = grp * 8 + kk
                    evt = k.emit("vector", lambda e, dsb=dsb, bank=bank, kk=kk, jj=jj: e.tensor_scalar(
                        out=y3_fm[:, dsb * 1024 + jj * 128: dsb * 1024 + (jj + 1) * 128],
                        in0=pbf(bank)[:, kk * 128:(kk + 1) * 128],
                        scalar1=nrmw_s[:, dsb:dsb + 1], scalar2=None, op0=ALU.mult),
                        waits=[("c_pe", trt)] if kk == 0 else (),
                        inc="c_dve" if kk == 7 else None)
                toks["y3"].append(evt)
        return toks["y3"][-1]

    def p3_half(hf, wop_bf, atstg):
        ev_prev = {0: 0, 1: 0, 2: 0, 3: 0}
        wld_prev = None
        for oc in range(8):
            wb = wop_bf[oc % 2]
            ws = [("c_act", k.val("c_act"))] if oc < 2 else [("c_pe", wcons[oc - 2])]
            wld = k.dma("gpsimd", wb[:], w_op[oc].rearrange("p b c -> p (b c)"), waits=ws, inc="d_w")
            if oc == 0:
                wcons = {}
            for tt in range(8):
                bank = tt % 4
                mmt = None
                for dsb in range(32):
                    ws2 = []
                    if dsb == 0:
                        ws2 = [("c_act", ev_prev[bank])]
                        if tt == 0:
                            ws2.append(("d_w", wld))
                        if tt == 0 and oc == 0:
                            ws2.append(("c_dve", k.val("c_dve")))
                    mmt = k.emit("tensor", lambda e, dsb=dsb, tt=tt, wb=wb, bank=bank: e.matmul(
                        out=psb[bank][:, 0:512],
                        lhsT=y3_fm[:, dsb * 1024 + tt * 128:dsb * 1024 + (tt + 1) * 128],
                        rhs=wb[:, dsb * 512:(dsb + 1) * 512],
                        start=(dsb == 0), stop=(dsb == 31)),
                        waits=ws2, inc="c_pe" if dsb == 31 else None)
                if tt == 6:
                    wcons[oc] = mmt
                ev = k.emit("scalar", lambda e, bank=bank, tt=tt: e.activation(
                    out=atstg[tt % 2][:], in_=psb[bank][:, 0:512], func=AF.Identity),
                    waits=[("c_pe", mmt), ("d_scrw", at_w.get((tt - 2) % 8, 0))], inc="c_act")
                ev_prev[bank] = ev
                if os.environ.get("MK_P3_NOSTORE") == "1":
                    at_w[tt] = k.val("d_scrw")
                elif os.environ.get("MK_P3_DBGSTORE") == "1":
                    at_w[tt] = k.dma("scalar", dbg_sink, atstg[tt % 2][:],
                                     waits=[("c_act", ev)], inc="d_scrw")
                else:
                    at_w[tt] = k.dma("scalar", at_dram[oc, hf * 8 + tt],
                                     atstg[tt % 2][:], waits=[("c_act", ev)], inc="d_scrw")
            wcons[oc] = mmt
        return k.val("d_scrw")

    y_total_after_p2 = y_dma_tok
    at_w = {}
    p2b_done0 = p2b_half(0, y_total_after_p2)
    k.barrier()
    k.reset(Y3END)
    wop_bf = [k.sb(f"wop{i}", [128, 32 * 512], BF16) for i in range(2)]
    atstg = [k.sb(f"atstg{i}", [128, 512], F32, at=224256 + 2048 * i) for i in range(2)]
    assert k.mark() <= P2_END
    if os.environ.get("MK_ONLY_P2B") != "1":
        p3_half(0, wop_bf, atstg)
    k.barrier()
    p2b_done1 = p2b_half(1, y_total_after_p2)
    k.barrier()
    if os.environ.get("MK_ONLY_P2B") != "1":
        p3_half(1, wop_bf, atstg)
    k.barrier()

    if STAGES <= 5:
        return finish_early()

    k.reset(SBASE)
    x3_fm = k.sb("x3_fm", [128, 32 * 1024], BF16)
    zT_pad = k.sb("zT_pad", [128, 1024], BF16)
    loraB_pad = k.sb("loraB_pad", [128, 4096], BF16)
    X3END = k.mark()
    attile = [k.sb(f"attile{i}", [128, H], F32) for i in range(2)]
    xintile = [k.sb(f"xint{i}", [128, H], F32) for i in range(2)]
    abuf = k.sb("abuf", [128, H], F32)
    xmbuf = k.sb("xmbuf", [128, H], F32)
    x3tm = k.sb("x3tm", [128, H], BF16)
    logi = k.sb("logi", [128, 4], F32)
    mx = k.sb("mx", [128, 1], F32)
    oh = k.sb("oh", [128, 4], F32)
    zmk = k.sb("zmk", [128, 32], BF16)
    P4_END = k.mark()
    assert P4_END <= k.SBLIM

    lb1 = k.dma("gpsimd", loraB_pad[0:32, :], loraB_t, inc="d_w")
    k.wait("vector", [("c_pe", k.val("c_pe"))])
    k.emit("vector", lambda e: e.memset(loraB_pad[32:64, :], 0.0), inc="c_dve")
    k.emit("vector", lambda e: e.memset(loraB_pad[64:128, :], 0.0), inc="c_dve")
    k.emit("vector", lambda e: e.memset(zT_pad[32:64, :], 0.0), inc="c_dve")
    k.emit("vector", lambda e: e.memset(zT_pad[64:128, :], 0.0), inc="c_dve")

    def p4_tile(hf, jj):
        j = hf * 8 + jj
        att = attile[jj % 2]
        xit = xintile[jj % 2]
        ws = [("c_act", k.val("c_act")), ("c_dve", k.val("c_dve"))] if jj < 2 else \
             [("c_act", p4toks[f"x3{jj-2}"]), ("c_dve", p4toks[f"xm{jj-2}"])]
        for oc in range(8):
            ald = k.dma("sync", att[:, oc * 512:(oc + 1) * 512], at_dram[oc, j],
                        waits=(ws + [("d_scrw", k.val("d_scrw"))]) if oc == 0 else (),
                        inc="d_at")
        xld = k.dma("sync", xit[:], xin.rearrange("(n p) h -> n p h", p=128)[j],
                    waits=[("c_dve", p4toks[f"xm{jj-2}"])] if jj >= 2 else [], inc="d_xin")
        st1 = ln_stats(att[:], abuf[:], False,
                       extra_waits=[("d_at", ald), ("c_act", k.val("c_act"))])
        a1 = k.emit("scalar", lambda e, att=att: e.activation(
            out=abuf[:], in_=att[:], func=AF.Identity, scale=stat[:, 10:11], bias=stat[:, 11:12]),
            waits=[("c_dve", st1)], inc="c_act")
        xm1 = k.emit("vector", lambda e, xit=xit: e.tensor_tensor(out=xmbuf[:], in0=abuf[:],
                                                                  in1=xit[:], op=ALU.add),
                     waits=[("c_act", a1), ("d_xin", xld)], inc="c_dve")
        p4toks[f"xm{jj}"] = xm1
        if os.environ.get("MK_NO_XMW") == "1":
            xmw = k.val("d_scrw")
        else:
            xmw = k.dma("scalar", xm_dram[j], xmbuf[:], waits=[("c_dve", xm1)], inc="d_scrw")
        st2 = ln_stats(xmbuf[:], abuf[:], False, extra_waits=[("c_dve", xm1), ("c_act", a1)])
        a2 = k.emit("scalar", lambda e: e.activation(
            out=x3tm[:], in_=xmbuf[:], func=AF.Identity, scale=stat[:, 10:11], bias=stat[:, 11:12]),
            waits=[("c_dve", st2)], inc="c_act")
        p4toks[f"x3{jj}"] = a2
        for grp in range(4):
            bank = 4 + grp % 2
            trt = None
            for kk in range(8):
                hb = grp * 8 + kk
                trt = k.emit("tensor", lambda e, hb=hb, bank=bank, kk=kk: e.transpose(
                    out=pbf(bank)[:, kk * 128:(kk + 1) * 128],
                    in_=x3tm[:, hb * 128:(hb + 1) * 128], identity=ident_bf[:]),
                    waits=([("c_act", a2), ("c_dve", k.val("c_dve"))] if kk == 0 else ()),
                    inc="c_pe" if kk == 7 else None)
            for kk in range(8):
                hb = grp * 8 + kk
                k.emit("vector", lambda e, hb=hb, bank=bank, kk=kk, jj=jj: e.tensor_copy(
                    out=x3_fm[:, hb * 1024 + jj * 128: hb * 1024 + (jj + 1) * 128],
                    in_=pbf(bank)[:, kk * 128:(kk + 1) * 128]),
                    waits=[("c_pe", trt)] if kk == 0 else (),
                    inc="c_dve" if kk == 7 else None)
        x3done = k.val("c_dve")
        if os.environ.get("MK_P4_MIN") == "1":
            return k.val("c_dve")
        gmt = None
        for hb in range(32):
            ws2 = [("c_dve", x3done), ("c_act", k.val("c_act"))] if hb == 0 else ()
            gmt = k.emit("tensor", lambda e, hb=hb, jj=jj: e.matmul(
                out=psb[6][:, 0:4],
                lhsT=x3_fm[:, hb * 1024 + jj * 128: hb * 1024 + (jj + 1) * 128],
                rhs=gate_bf[:, hb * 4:(hb + 1) * 4],
                start=(hb == 0), stop=(hb == 31)),
                waits=ws2, inc="c_pe" if hb == 31 else None)
        lg = k.emit("vector", lambda e: e.tensor_tensor(out=logi[:], in0=psb[6][:, 0:4],
                                                        in1=ebias_r[:], op=ALU.add),
                    waits=[("c_pe", gmt)], inc="c_dve")
        mxt = k.emit("vector", lambda e: e.reduce_max(out=mx[:], in_=logi[:], axis=AX.X),
                     waits=[("c_dve", lg)], inc="c_dve")
        k.wait("vector", [("c_dve", mxt)])
        oht = k.emit("vector", lambda e: e.tensor_scalar(out=oh[:], in0=logi[:], scalar1=mx[:],
                                                         scalar2=None, op0=ALU.is_ge),
                     waits=[("c_dve", mxt)], inc="c_dve")
        zmt = None
        for hb in range(32):
            ws2 = [("c_dve", k.val("c_dve"))] if hb == 0 else ()
            zmt = k.emit("tensor", lambda e, hb=hb, jj=jj: e.matmul(
                out=psb[7][:, 0:32],
                lhsT=x3_fm[:, hb * 1024 + jj * 128: hb * 1024 + (jj + 1) * 128],
                rhs=loraA_bf[:, hb * 32:(hb + 1) * 32],
                start=(hb == 0), stop=(hb == 31)),
                waits=ws2, inc="c_pe" if hb == 31 else None)
        zmtk = None
        for ee in range(4):
            ws2 = [("c_pe", zmt), ("c_dve", oht)] if ee == 0 else ()
            zmtk = k.emit("vector", lambda e, ee=ee: e.tensor_scalar(
                out=zmk[:, ee * 8:(ee + 1) * 8], in0=psb[7][:, ee * 8:(ee + 1) * 8],
                scalar1=oh[:, ee:ee + 1], scalar2=None, op0=ALU.mult),
                waits=ws2, inc="c_dve" if ee == 3 else None)
        ztt = k.emit("tensor", lambda e: e.transpose(
            out=pbf(6)[0:32, 512:640], in_=zmk[:], identity=ident_bf[:]),
            waits=[("c_dve", zmtk)], inc="c_pe")
        zev = k.emit("vector", lambda e, jj=jj: e.tensor_copy(
            out=zT_pad[0:32, jj * 128:(jj + 1) * 128], in_=pbf(6)[0:32, 512:640]),
            waits=[("c_pe", ztt)], inc="c_dve")
        return zev

    def p5_half(hf, wup_bf, ffstg):
        ev_prev = {0: 0, 1: 0, 2: 0, 3: 0}
        for oc in range(8):
            wb = wup_bf[oc % 2]
            ws = [("c_act", k.val("c_act"))] if oc < 2 else [("c_pe", wcons5[oc - 2])]
            wld = k.dma("gpsimd", wb[:], w_up[oc].rearrange("p b c -> p (b c)"), waits=ws, inc="d_w")
            if oc == 0:
                wcons5 = {}
            for tt in range(8):
                bank = tt % 4
                mmt = None
                for dsb in range(32):
                    ws2 = []
                    if dsb == 0:
                        ws2 = [("c_act", ev_prev[bank])]
                        if tt == 0:
                            ws2.append(("d_w", wld))
                        if tt == 0 and oc == 0:
                            ws2.append(("c_dve", k.val("c_dve")))
                            ws2.append(("d_w", lb1))
                    mmt = k.emit("tensor", lambda e, dsb=dsb, tt=tt, wb=wb, bank=bank: e.matmul(
                        out=psb[bank][:, 0:512],
                        lhsT=x3_fm[:, dsb * 1024 + tt * 128:dsb * 1024 + (tt + 1) * 128],
                        rhs=wb[:, dsb * 512:(dsb + 1) * 512],
                        start=(dsb == 0), stop=False),
                        waits=ws2, inc=None)
                mmt = k.emit("tensor", lambda e, tt=tt, bank=bank, oc=oc: e.matmul(
                    out=psb[bank][:, 0:512],
                    lhsT=zT_pad[:, tt * 128:(tt + 1) * 128],
                    rhs=loraB_pad[:, oc * 512:(oc + 1) * 512],
                    start=False, stop=True),
                    inc="c_pe")
                ev = k.emit("scalar", lambda e, bank=bank, tt=tt: e.activation(
                    out=ffstg[tt % 2][:], in_=psb[bank][:, 0:512], func=AF.Identity),
                    waits=[("c_pe", mmt), ("d_scrw", ff_w.get((tt - 2) % 8, 0))], inc="c_act")
                ev_prev[bank] = ev
                if os.environ.get("MK_P5_NOSTORE") == "1":
                    ff_w[tt] = k.val("d_scrw")
                else:
                    ff_w[tt] = k.dma("scalar", ff_dram[oc, hf * 8 + tt],
                                     ffstg[tt % 2][:], waits=[("c_act", ev)], inc="d_scrw")
            wcons5[oc] = mmt

    def p6_tile(j, fftile, xmtile, obuf):
        fft = fftile[j % 2]
        xmt = xmtile[j % 2]
        ws = [("c_act", k.val("c_act")), ("d_out", k.val("d_out"))] if True else []
        for oc in range(8):
            fld = k.dma("sync", fft[:, oc * 512:(oc + 1) * 512], ff_dram[oc, j],
                        waits=(ws + [("d_scrw", k.val("d_scrw"))]) if oc == 0 else (),
                        inc="d_ff")
        xld = k.dma("sync", xmt[:], xm_dram[j], inc="d_xm")
        st1 = ln_stats(fft[:], obuf[:], False, extra_waits=[("d_ff", fld), ("c_act", k.val("c_act"))])
        a1 = k.emit("scalar", lambda e, fft=fft: e.activation(
            out=obuf[:], in_=fft[:], func=AF.Identity, scale=stat[:, 10:11], bias=stat[:, 11:12]),
            waits=[("c_dve", st1)], inc="c_act")
        o1 = k.emit("vector", lambda e, xmt=xmt, fft=fft: e.tensor_tensor(
            out=fft[:], in0=obuf[:], in1=xmt[:], op=ALU.add),
            waits=[("c_act", a1), ("d_xm", xld)], inc="c_dve")
        return k.dma("scalar", out_d.rearrange("(n p) h -> n p h", p=128)[j], fft[:],
                     waits=[("c_dve", o1)], inc="d_out")

    ff_w = {}
    k.reset(X3END)
    wup_bf = [k.sb(f"wup{i}", [128, 32 * 512], BF16) for i in range(2)]
    ffstg = [k.sb(f"ffstg{i}", [128, 512], F32, at=220000 + 2048 * i) for i in range(2)]
    k.reset(P4_END)
    for hf in range(2):
        p4toks = {}
        k.barrier()
        zl = None
        for jj in range(8):
            zl = p4_tile(hf, jj)
        k.barrier()
        if STAGES >= 7:
            p5_half(hf, wup_bf, ffstg)
            k.barrier()
    if STAGES <= 7:
        return finish_early()

    k.reset(X3END)
    fftile = [k.sb(f"fft{i}", [128, H], F32) for i in range(2)]
    xmtile = [k.sb(f"xmt{i}", [128, H], F32) for i in range(2)]
    obuf = k.sb("obuf", [128, H], F32)
    assert k.mark() <= k.SBLIM
    k.barrier()
    last_out = 0
    for j in range(NTT):
        last_out = p6_tile(j, fftile, xmtile, obuf)

    if DEBUG_TAPS:
        for tp, src in taps:
            k.dma("gpsimd", tp, src[:], waits=[("d_out", last_out)], inc="d_scrw")

    k.wait("gpsimd", [("d_out", last_out), ("d_scrw", k.val("d_scrw"))])
    k.wait("sync", [("d_out", last_out)])

    return k.finalize()


_NC_CACHE = None


def _get_nc():
    global _NC_CACHE
    if _NC_CACHE is None:
        _NC_CACHE = build_nc()
    return _NC_CACHE


def host_layout(inputs):
    f32 = np.float32
    _ = inputs.get("L0_attn_residual")
    hs = np.ascontiguousarray(inputs["L0_attn_hidden_states"], dtype=f32)
    Wip = np.ascontiguousarray(inputs["L0_attn_in_proj_w"], dtype=f32)
    Wop = np.ascontiguousarray(inputs["L0_attn_out_proj_w"], dtype=f32)
    Wup = np.ascontiguousarray(inputs["L0_ffn_W_up"], dtype=f32)
    gate = np.ascontiguousarray(inputs["L0_ffn_gate_w"], dtype=f32)
    loraA = np.ascontiguousarray(inputs["L0_ffn_lora_A"], dtype=f32)
    loraB = np.ascontiguousarray(inputs["L0_ffn_lora_B"], dtype=f32)
    convw = np.ascontiguousarray(inputs["L0_attn_conv_w"], dtype=f32)
    mup = np.ascontiguousarray(inputs["L0_attn_mup_vector"], dtype=f32)
    lnw = np.ascontiguousarray(inputs["L0_attn_ln_w"], dtype=f32)
    nrmw = np.ascontiguousarray(inputs["L0_attn_norm_w"], dtype=f32)
    dtb = np.ascontiguousarray(inputs["L0_attn_dt_bias"], dtype=f32)
    alog = np.ascontiguousarray(inputs["L0_attn_A_log"], dtype=f32)
    Dv = np.ascontiguousarray(inputs["L0_attn_D"], dtype=f32)
    ebias = np.ascontiguousarray(inputs["L0_ffn_expert_bias"], dtype=f32)

    wip_pad = np.zeros((4096, NCB * 128), f32)
    wip_pad[:, :8512] = Wip
    w_ip_t = np.ascontiguousarray(
        wip_pad.reshape(32, 128, NCB, 128).transpose(2, 1, 0, 3))
    w_op_t = np.ascontiguousarray(Wop.reshape(32, 128, 8, 512).transpose(2, 1, 0, 3))
    WupT = np.ascontiguousarray(Wup.T)
    w_up_t = np.ascontiguousarray(WupT.reshape(32, 128, 8, 512).transpose(2, 1, 0, 3))
    gate_tt = np.ascontiguousarray(gate.T.reshape(32, 128, 4).transpose(1, 0, 2))
    loraA_cat = np.ascontiguousarray(loraA.transpose(2, 0, 1).reshape(4096, 32))
    loraA_tt = np.ascontiguousarray(loraA_cat.reshape(32, 128, 32).transpose(1, 0, 2))
    loraB_tt = np.ascontiguousarray(loraB.transpose(0, 2, 1).reshape(32, 4096))
    conv_tt = np.ascontiguousarray(convw.reshape(34, 128, 4))
    mup_pad = np.zeros((NCB * 128,), f32)
    mup_pad[:8512] = mup
    mup_tt = np.ascontiguousarray(mup_pad.reshape(NCB, 128).T)
    lnw_tt = np.ascontiguousarray(lnw.reshape(32, 128).T)
    nrmw_tt = np.ascontiguousarray(nrmw.reshape(32, 128).T)
    drep = np.ascontiguousarray(np.repeat(Dv, 64).reshape(1, 4096))
    identb = np.eye(128, dtype=ml_dtypes.bfloat16)
    identf = np.eye(128, dtype=f32)
    maskb = np.tril(np.ones((128, 128), f32)).T
    maskb = np.ascontiguousarray(np.where(
        np.arange(128)[:, None] <= np.arange(128)[None, :], 1.0, 0.0).astype(f32))

    shared = {
        "w_ip": w_ip_t, "w_op": w_op_t, "w_up": w_up_t, "gate_t": gate_tt,
        "loraA_t": loraA_tt, "loraB_t": loraB_tt, "conv_t": conv_tt,
        "mup_t": mup_tt, "lnw_t": lnw_tt, "nrmw_t": nrmw_tt,
        "dtb": dtb.reshape(64, 1), "alog": alog.reshape(64, 1),
        "drep": drep, "ebias": ebias.reshape(1, 4),
        "identb": identb, "identf": identf, "maskbin": maskb,
    }

    in_maps = []
    for c in range(NCORES):
        bb, hf = c // 2, c % 2
        t0 = hf * T
        m = dict(shared)
        m["xin"] = np.ascontiguousarray(hs[bb, t0:t0 + T])
        xh = np.zeros((4, H), f32)
        if hf == 1:
            xh[0:3] = hs[bb, t0 - 3:t0]
        m["xhalo"] = xh
        if hf == 1:
            idx = ((c - 1) * 128 + np.arange(128, dtype=np.int32)).reshape(128, 1)
        else:
            idx = np.full((128, 1), 1 << 20, dtype=np.int32)
        m["hidx"] = idx
        in_maps.append(m)
    return in_maps


def kernel(**inputs):
    in_maps = host_layout(inputs)
    nc = _get_nc()
    res = run_bass_kernel_spmd(nc, in_maps, core_ids=list(range(NCORES)))
    out = np.zeros((4, 4096, 4096), np.float32)
    for c in range(NCORES):
        bb, hf = c // 2, c % 2
        out[bb, hf * T:(hf + 1) * T] = np.asarray(res.results[c]["out"])
    return out


if __name__ == "__main__":
    nc = build_nc()
    print("built ok")
